# version 15
# baseline (speedup 1.0000x reference)
import contextlib
import numpy as np
import ml_dtypes
import concourse.bass as bass
import concourse.mybir as mybir
from concourse.bass_utils import run_bass_kernel_spmd

F32 = mybir.dt.float32
BF16 = mybir.dt.bfloat16
AF = mybir.ActivationFunctionType
ALU = mybir.AluOpType
AX = mybir.AxisListType

M1 = 12582912.0
M2 = 25165824.0

D = 1024
KT = 8
H = 16
TOK = 2048
NT = 16
NST = 4
I_FF = 2816
IKB = 22
QW = 520
IC_CH = [(0, 512), (512, 1024), (1024, 1536), (1536, 2048), (2048, 2560), (2560, 2816)]

ENGS = ("sync", "vector", "scalar", "tensor", "gpsimd")


class Sem:
    def __init__(self, nc, name):
        self.h = nc.alloc_semaphore(name)
        self.n = 0


class Sched:

    def __init__(self, nc):
        self.nc = nc
        self.q = {e: [] for e in ENGS}

    def wait(self, eng, sem, val=None):
        v = sem.n if val is None else val
        if v > 0:
            self.q[eng].append(("w", sem.h, v))

    def op(self, eng, fn, inc=None, k=1):
        incs = []
        if inc is not None:
            for s in (inc if isinstance(inc, (list, tuple)) else [inc]):
                s.n += k
                incs.append((s.h, k))
        self.q[eng].append(("o", fn, incs))

    def dma(self, eng, out_ap, in_ap, inc=None):
        self.op(eng, lambda e, o=out_ap, i=in_ap: e.dma_start(o, i), inc=inc, k=16)

    def run(self):
        with self.nc.Block() as blk:
            def emit(e, items):
                for it in items:
                    if it[0] == "w":
                        e.wait_ge(it[1], it[2])
                    else:
                        instr = it[1](e)
                        for (h, k) in it[2]:
                            instr = instr.then_inc(h, k)
            blk.sync(lambda e: emit(e, self.q["sync"]))
            blk.vector(lambda e: emit(e, self.q["vector"]))
            blk.scalar(lambda e: emit(e, self.q["scalar"]))
            blk.tensor(lambda e: emit(e, self.q["tensor"]))
            blk.gpsimd(lambda e: emit(e, self.q["gpsimd"]))


def build_program(phases="WAB", alevel=9, dbg=0):
    assert phases in ("WA", "WAB")
    nc = bass.Bass("TRN2", target_bir_lowering=False)
    sc = Sched(nc)

    def din(name, shape, dt):
        return nc.dram_tensor(name, shape, dt, kind="ExternalInput").ap()

    xs = din("xs", [TOK + 2, D], F32)
    w_qkv = din("w_qkv", [D, 3 * D], F32)
    w_proj = din("w_proj", [D, D], F32)
    b_proj = din("b_proj", [1, D], F32)
    wg_d = din("wg", [I_FF, D], F32)
    wu_d = din("wu", [I_FF, D], F32)
    wd_d = din("wd", [D, I_FF], F32)
    ident_in = din("ident", [128, 128], BF16)
    maskmain_in = din("maskmain", [128, 130], BF16)
    masklast_in = din("masklast", [128, 130], BF16)
    onesrow_in = din("onesrow", [1, 128], BF16)
    cvec_in = din("cvec", [128, 4], F32)

    out_d = nc.dram_tensor("out", [TOK, D], F32, kind="ExternalOutput").ap()

    _stage_kind = "ExternalOutput" if (dbg & 8) else "Internal"
    d_s = nc.dram_tensor("d_s", [NT, 128, D], BF16, kind=_stage_kind).ap()
    d_xqT = nc.dram_tensor("d_xqT", [NT, 128, D], BF16, kind=_stage_kind).ap()
    if dbg & 16:
        d_e = nc.dram_tensor("d_e", [NT, 128, H * 130], BF16, kind="ExternalOutput").ap()
        d_p = nc.dram_tensor("d_p", [NT, 128, H * 130], BF16, kind="ExternalOutput").ap()
        d_rs = nc.dram_tensor("d_rs", [NT, 128, H], F32, kind="ExternalOutput").ap()
        d_eb = nc.dram_tensor("d_eb", [NT, 128, H * 130], BF16, kind="ExternalOutput").ap()
        d_ri = nc.dram_tensor("d_ri", [NT, 128, H], BF16, kind="ExternalOutput").ap()
        d_xn = nc.dram_tensor("d_xn", [NT, 128, D], F32, kind="ExternalOutput").ap()
        d_xt = nc.dram_tensor("d_xt", [NT, 128, D], F32, kind="ExternalOutput").ap()
    d_wgT = nc.dram_tensor("d_wgT", [KT, 128, IKB * 128], BF16, kind="Internal").ap()
    d_wuT = nc.dram_tensor("d_wuT", [KT, 128, IKB * 128], BF16, kind="Internal").ap()
    d_wdT = nc.dram_tensor("d_wdT", [IKB, 128, KT * 128], BF16, kind="Internal").ap()

    def sb(name, shape, dt):
        return nc.alloc_sbuf_tensor(name, shape, dt).ap()

    ident = sb("ident_s", [128, 128], BF16)
    maskm = sb("maskm_s", [128, 130], BF16)
    maskl = sb("maskl_s", [128, 130], BF16)
    onesr = sb("onesr_s", [1, 128], BF16)
    cvec = sb("cvec_s", [128, 4], F32)
    bproj16 = sb("bproj16", [1, D], BF16)

    QWID = 1024
    wbuf = sb("q_w", [128, QWID], F32)
    t1 = sb("q_t1", [128, QWID], BF16)
    t2 = sb("q_t2", [128, QWID], BF16)
    t3 = sb("q_t3", [128, QWID], BF16)
    t4 = sb("q_t4", [128, QWID], BF16)
    NBMAX = QWID // 16
    bmb = sb("q_bm", [128, NBMAX], F32)
    scb = sb("q_sc", [128, NBMAX], F32)
    lnb = sb("q_ln", [128, NBMAX], F32)
    rinvb = sb("q_rinv", [128, NBMAX], F32)
    rinv2b = sb("q_rinv2", [128, NBMAX], F32)
    qscb = sb("q_qsc", [128, NBMAX], BF16)

    P0 = nc.alloc_psum_tensor("P0", [128, 512], F32).ap()
    P1 = nc.alloc_psum_tensor("P1", [128, 512], F32).ap()
    P2 = nc.alloc_psum_tensor("P2", [128, 1024], F32).ap()
    P3 = nc.alloc_psum_tensor("P3", [128, 1024], F32).ap()
    P4 = nc.alloc_psum_tensor("P4", [128, 2048], BF16).ap()
    P3b = P3.bitcast(BF16)

    s_ld = Sem(nc, "s_ld")
    s_gld = Sem(nc, "s_gld")
    s_st = Sem(nc, "s_st")
    s_xqst = Sem(nc, "s_xqst")
    s_sst = Sem(nc, "s_sst")
    s_dbg = Sem(nc, "s_dbg")
    OUT = [Sem(nc, "s_out0"), Sem(nc, "s_out1")]
    s_pe = Sem(nc, "s_pe")
    s_v = Sem(nc, "s_v")
    s_s = Sem(nc, "s_s")

    sc.dma("sync", ident[:], ident_in[:], inc=s_ld)
    sc.dma("sync", maskm[:], maskmain_in[:], inc=s_ld)
    sc.dma("sync", maskl[:], masklast_in[:], inc=s_ld)
    sc.dma("sync", onesr[:], onesrow_in[:], inc=s_ld)
    sc.dma("sync", cvec[:], cvec_in[:], inc=s_ld)
    N_PROLOG = 5
    for _e in ("vector", "scalar", "tensor", "sync"):
        sc.wait(_e, s_ld, N_PROLOG * 16)
    sc.dma("gpsimd", bproj16[:], b_proj[:], inc=s_gld)

    def quant(src_ap, dst_ap, W, pe_guard=0):
        nb = W // 16
        src3 = src_ap.rearrange("p (nb b) -> p nb b", b=16)
        v, s = "vector", "scalar"
        sc.op(v, lambda e, src3=src3, nb=nb: e.tensor_reduce(
            bmb[:, :nb], src3, AX.X, ALU.max, apply_absolute_value=True), inc=s_v)
        sc.wait(v, s_v)
        sc.op(v, lambda e, nb=nb: e.tensor_scalar(
            scb[:, :nb], bmb[:, :nb], 1.0 / 6.0, 1e-12, ALU.mult, ALU.max), inc=s_v)
        sc.wait(v, s_v)
        sc.op(v, lambda e, nb=nb: e.reciprocal(rinvb[:, :nb], scb[:, :nb]), inc=s_v)
        sc.wait(v, s_v)
        sc.op(v, lambda e, nb=nb: e.tensor_scalar(
            rinv2b[:, :nb], rinvb[:, :nb], 2.0, None, ALU.mult))
        sc.op(v, lambda e, nb=nb: e.tensor_scalar(
            qscb[:, :nb], scb[:, :nb], 0.5, None, ALU.mult))
        r2bc = rinv2b[:, :nb].unsqueeze(2).broadcast_to([128, nb, 16])
        qsbc = qscb[:, :nb].unsqueeze(2).broadcast_to([128, nb, 16])
        wv = wbuf[:, :W]
        wv3 = wv.rearrange("p (nb b) -> p nb b", b=16)
        A_, B_, C_, D_ = (t1[:, :W], t2[:, :W], t3[:, :W], t4[:, :W])
        sc.op(v, lambda e, wv3=wv3, src3=src3, r2bc=r2bc: e.tensor_tensor(
            wv3, src3, r2bc, ALU.mult))
        sc.op(v, lambda e, A_=A_, wv=wv: e.tensor_scalar(A_, wv, M1, -M1, ALU.add, ALU.add))
        sc.op(v, lambda e, A_=A_, B_=B_: e.tensor_scalar(B_, A_, -4.0, 4.0, ALU.max, ALU.min))
        sc.op(v, lambda e, C_=C_, wv=wv: e.tensor_scalar(C_, wv, M2, -M2, ALU.add, ALU.add))
        sc.op(v, lambda e, C_=C_, D_=D_: e.tensor_scalar(D_, C_, -4.0, 4.0, ALU.max, ALU.min))
        sc.op(v, lambda e, A_=A_, C_=C_, D_=D_: e.tensor_tensor(A_, C_, D_, ALU.subtract))
        sc.op(v, lambda e, A_=A_, C_=C_: e.tensor_scalar(C_, A_, -4.0, 4.0, ALU.max, ALU.min))
        sc.op(v, lambda e, D_=D_, wv=wv: e.tensor_scalar(D_, wv, 10.0, 4.0, ALU.is_ge, ALU.mult))
        sc.op(v, lambda e, A_=A_, wv=wv: e.tensor_scalar(A_, wv, -10.0, 4.0, ALU.is_le, ALU.mult))
        sc.op(v, lambda e, A_=A_, D_=D_: e.tensor_tensor(D_, D_, A_, ALU.subtract))
        sc.op(v, lambda e, B_=B_, C_=C_: e.tensor_tensor(B_, B_, C_, ALU.add))
        sc.op(v, lambda e, B_=B_, D_=D_: e.tensor_tensor(B_, B_, D_, ALU.add))
        dst3 = dst_ap.rearrange("p (nb b) -> p nb b", b=16)
        B3 = B_.rearrange("p (nb b) -> p nb b", b=16)
        if pe_guard:
            sc.wait(v, s_pe, pe_guard)
        sc.op(v, lambda e, dst3=dst3, B3=B3, qsbc=qsbc: e.tensor_tensor(
            dst3, B3, qsbc, ALU.mult), inc=s_v)


    XL = [Sem(nc, "s_xl0"), Sem(nc, "s_xl1")]
    XH = Sem(nc, "s_xh")
    SH = Sem(nc, "s_sh")

    with contextlib.ExitStack() as _esA:
        def sba(name, shape, dt):
            return _esA.enter_context(nc.sbuf_tensor(name, shape, dt)).ap()
        xtile = sba("xtile", [128, 2 * D], F32)
        xhalo = sba("xhalo", [2, D], F32)
        stats = sba("ln_stats", [128, 12], F32)
        aggr = sba("ln_aggr", [128, 2], F32)
        lntmp = sba("ln_tmp", [128, 4], F32)
        xnf = sba("xnf", [128, D], F32)
        xn16 = sba("xn16", [128, D], BF16)
        xq16 = sba("xq16", [128, D], BF16)
        xqTst = sba("xqTst", [128, D], BF16)
        hstat = sba("hstat", [2, 12], F32)
        haggr = sba("haggr", [2, 2], F32)
        htmp = sba("htmp", [2, 4], F32)
        xnh = sba("xnh", [2, D], F32)
        xnh16 = sba("xnh16", [2, D], BF16)
        xnT = sba("xnT", [128, KT * QW], BF16)
        qkT = [sba("qkT0", [128, 16 * QW], BF16), sba("qkT1", [128, 16 * QW], BF16)]
        vsb = [sba("vsb0", [128, 4 * D], BF16), sba("vsb1", [128, 4 * D], BF16)]
        vhalo = sba("vhalo", [2, D], BF16)
        vh16 = sba("vh16", [128, 16], BF16)
        e16 = sba("e16", [128, H * 130], BF16)
        rs32 = sba("rs32", [128, H], F32)
        rsi32 = sba("rsi32", [128, H], F32)
        rinv16 = sba("rinv16b", [128, H], BF16)
        qodd = sba("qodd", [64, 8 * 128], BF16)
        kodd = sba("kodd", [64, 8 * 132], BF16)
        PTm = sba("PTm", [128, H * 128], BF16)
        PTt = sba("PTt", [2, H * 128], BF16)
        outT16 = sba("outT16", [128, D], BF16)
        sOut = sba("sOut", [128, D], BF16)
        wqkv16 = sba("wqkv16", [128, KT * 3 * D], BF16)
        wproj16 = sba("wproj16", [128, KT * D], BF16)

        gld_kt = []
        for kt in range(KT):
            sc.dma("gpsimd", wqkv16[:, kt * 3 * D:(kt + 1) * 3 * D],
                   w_qkv[kt * 128:(kt + 1) * 128, :], inc=s_gld)
            gld_kt.append(s_gld.n)
        ngld = [0]

        wpce = [sba("wpce0", [128, 1024], F32), sba("wpce1", [128, 1024], F32)]
        wq16s = [sba("w_q16%d" % i, [128, 1024], BF16) for i in range(4)]
        wstage = [sba("w_stage0", [128, 1024], BF16), sba("w_stage1", [128, 1024], BF16)]
        wtr = P4[:, 1024:2048]
        WL = [Sem(nc, "s_wl0"), Sem(nc, "s_wl1")]
        WLg = [Sem(nc, "s_wlg0"), Sem(nc, "s_wlg1")]
        wpieces = []
        for (srcw, dstw, nch, k) in ((wg_d, d_wgT, IKB, D), (wu_d, d_wuT, IKB, D),
                                     (wd_d, d_wdT, KT, I_FF)):
            for c in range(nch):
                for p0 in range(0, k, 1024):
                    wpieces.append((srcw, dstw, c, p0, min(p0 + 1024, k)))
        NWP = len(wpieces)
        wp_ld = {}
        wq_end = {}
        wt_end = {}
        wst_tot = {}
        wcur = [0, 0, 0]

        def w_load(pi):
            srcw, dstw, c, p0, p1 = wpieces[pi]
            if pi < 6:
                q, sem = "sync", WL[pi % 2]
            else:
                q, sem = "gpsimd", WLg[pi % 2]
            if pi - 2 in wq_end:
                sc.wait(q, s_v, wq_end[pi - 2])
            sc.dma(q, wpce[pi % 2][:, :p1 - p0],
                   srcw[c * 128:(c + 1) * 128, p0:p1], inc=sem)
            wp_ld[pi] = (sem, sem.n)

        def w_quant(pi):
            while wcur[0] < min(pi + 2, NWP):
                w_load(wcur[0])
                wcur[0] += 1
            srcw, dstw, c, p0, p1 = wpieces[pi]
            lsem, lcnt = wp_ld[pi]
            sc.wait("vector", lsem, lcnt)
            assert pi < 4 or (pi - 4) in wt_end
            quant(wpce[pi % 2][:, :p1 - p0], wq16s[pi % 4][:, :p1 - p0], p1 - p0,
                  pe_guard=wt_end.get(pi - 4, 0))
            wq_end[pi] = s_v.n

        def w_store(pi):
            srcw, dstw, c, p0, p1 = wpieces[pi]
            nb = (p1 - p0) // 128
            sc.wait("tensor", s_v, wq_end[pi])
            sc.wait("tensor", s_s)
            for b in range(nb):
                sc.op("tensor", lambda e, b=b, pi=pi: e.transpose(
                    wtr[:, b * 128:(b + 1) * 128],
                    wq16s[pi % 4][:, b * 128:(b + 1) * 128], ident[:]),
                    inc=(s_pe if b == nb - 1 else None))
            wt_end[pi] = s_pe.n
            sc.wait("scalar", s_pe)
            if pi - 2 in wst_tot:
                sc.wait("scalar", s_st, wst_tot[pi - 2])
            sc.op("scalar", lambda e, nb=nb, pi=pi: e.copy(
                wstage[pi % 2][:, :nb * 128], wtr[:, :nb * 128]), inc=s_s)
            sc.wait("scalar", s_s)
            out_ap = dstw[p0 // 128:p0 // 128 + nb, :, c * 128:(c + 1) * 128].transpose([1, 0, 2])
            in_ap = wstage[pi % 2][:, :nb * 128].rearrange("p (kb q) -> p kb q", q=128)
            sc.dma("scalar", out_ap, in_ap, inc=s_st)
            wst_tot[pi] = s_st.n

        wtick = [0]

        def emit_w_stores():
            while wcur[2] < wcur[1] - 2:
                w_store(wcur[2])
                wcur[2] += 1

        def emit_w_quants():
            wtick[0] += 1
            qtarget = min(wtick[0] * NWP // 44 + 1, NWP)
            while wcur[1] < qtarget:
                pi = wcur[1]
                while wcur[2] <= pi - 4:
                    w_store(wcur[2])
                    wcur[2] += 1
                w_quant(pi)
                wcur[1] += 1

        def flush_w():
            wtick[0] = 32
            emit_w_quants()
            while wcur[2] < NWP:
                w_store(wcur[2])
                wcur[2] += 1

        xl_idx = {}
        xl_cnt = [0]
        xn_guard = {}
        trx_guard = [0]
        sc_ev = [0, 0]
        qk_ev = [0, 0]
        xqst_guard = [0]
        sst_guard = [0]

        def xload(t, guard):
            if guard:
                sc.wait("scalar", s_v, guard)
            sc.dma("scalar", xtile[:, (t % 2) * D:(t % 2) * D + D],
                   xs[t * 128:(t + 1) * 128, :], inc=XL[t % 2])
            xl_idx[t] = XL[t % 2].n

        def ln_tile(xap, n_p, statsb, aggrb, tmpb, xout, x16out, pe_guard=0):
            v, s = "vector", "scalar"
            sc.op(v, lambda e, xap=xap, statsb=statsb, n_p=n_p: e.bn_stats(
                statsb[:n_p, 0:6], xap[:, 0:512]))
            sc.op(v, lambda e, xap=xap, statsb=statsb, n_p=n_p: e.bn_stats(
                statsb[:n_p, 6:12], xap[:, 512:1024]), inc=s_v)
            sc.wait(v, s_v)
            sc.op(v, lambda e, statsb=statsb, aggrb=aggrb, n_p=n_p: e.bn_aggr(
                aggrb[:n_p, :], statsb[:n_p, :]), inc=s_v)
            sc.wait(v, s_v)
            sc.op(v, lambda e, aggrb=aggrb, tmpb=tmpb, n_p=n_p: e.tensor_scalar(
                tmpb[:n_p, 0:1], aggrb[:n_p, 0:1], -1.0, None, ALU.mult), inc=s_v)
            sc.wait(s, s_v)
            sc.op(s, lambda e, aggrb=aggrb, tmpb=tmpb, n_p=n_p: e.activation(
                tmpb[:n_p, 1:2], aggrb[:n_p, 1:2], AF.Ln, bias=cvec[:n_p, 0:1]), inc=s_s)
            sc.wait(s, s_s)
            sc.op(s, lambda e, tmpb=tmpb, n_p=n_p: e.activation(
                tmpb[:n_p, 2:3], tmpb[:n_p, 1:2], AF.Exp, scale=cvec[:n_p, 1:2]),
                inc=s_s)
            sc.wait(v, s_s)
            sc.op(v, lambda e, xap=xap, xout=xout, tmpb=tmpb, n_p=n_p: e.tensor_scalar(
                xout, xap, tmpb[:n_p, 0:1], tmpb[:n_p, 2:3], ALU.add, ALU.mult),
                inc=s_v)
            sc.wait(s, s_v)
            if pe_guard:
                sc.wait(s, s_pe, pe_guard)
            sc.op(s, lambda e, xout=xout, x16out=x16out: e.copy(x16out, xout), inc=s_s)

        def stage1(st):
            par = st % 2
            if st == 1:
                for kt in range(KT):
                    sc.dma("gpsimd", wproj16[:, kt * D:(kt + 1) * D],
                           w_proj[kt * 128:(kt + 1) * 128, :], inc=s_gld)
                ngld[0] = s_gld.n
            if True:
                trA = P4[:, 0:1024]
                qkps = [P0, P1]
                vps = P2
                hps = P3[:, 0:16]
                sc.wait("tensor", s_s)
                sc.wait("tensor", s_v)

                for sub in range(2):
                    t = 4 * st + sub
                    xload(t, xn_guard.get(t - 2, 0))
                if st == NST - 1:
                    sc.dma("scalar", xhalo[:], xs[TOK:TOK + 2, :], inc=XH)

                for sub in range(4):
                    t = 4 * st + sub
                    emit_w_stores()
                    xap = xtile[:, (t % 2) * D:(t % 2) * D + D]
                    sc.wait("vector", XL[t % 2], xl_idx[t])
                    ln_tile(xap, 128, stats, aggr, lntmp, xnf[:], xn16[:],
                            pe_guard=trx_guard[0])
                    if dbg & 16:
                        sc.wait("sync", s_v)
                        sc.dma("sync", d_xn[t, :, :], xnf[:], inc=s_dbg)
                        sc.dma("sync", d_xt[t, :, :], xap, inc=s_dbg)
                        sc.wait("vector", s_dbg)
                    xn_guard[t] = s_v.n
                    if sub + 2 < 4 or st + 1 < NST:
                        xload(t + 2, xn_guard[t])
                    quant(xnf[:], xq16[:], D, pe_guard=trx_guard[0])
                    sc.wait("tensor", s_s)
                    sc.wait("tensor", s_v)
                    for kb in range(KT):
                        sc.op("tensor", lambda e, kb=kb, trA=trA: e.transpose(
                            trA[:, kb * 128:(kb + 1) * 128],
                            xn16[:, kb * 128:(kb + 1) * 128], ident[:]),
                            inc=(s_pe if kb == KT - 1 else None))
                    sc.wait("scalar", s_pe)
                    sc.op("scalar", lambda e, sub=sub, trA=trA: e.copy(
                        xnT[:].rearrange("p (kt q) -> p kt q", q=QW)[:, :, sub * 128:sub * 128 + 128],
                        trA[:].rearrange("p (kt q) -> p kt q", q=128)), inc=s_s)
                    sc.wait("tensor", s_s)
                    for kb in range(KT):
                        sc.op("tensor", lambda e, kb=kb, trA=trA: e.transpose(
                            trA[:, kb * 128:(kb + 1) * 128],
                            xq16[:, kb * 128:(kb + 1) * 128], ident[:]),
                            inc=(s_pe if kb == KT - 1 else None))
                    trx_guard[0] = s_pe.n
                    sc.wait("scalar", s_pe)
                    sc.wait("scalar", s_xqst, xqst_guard[0])
                    sc.op("scalar", lambda e, trA=trA: e.copy(xqTst[:], trA[:]), inc=s_s)
                    sc.wait("sync", s_s)
                    sc.dma("sync", d_xqT[t, :, :], xqTst[:], inc=s_xqst)
                    xqst_guard[0] = s_xqst.n
                    sc.wait("tensor", s_s)
                    emit_w_quants()

                if st == NST - 1:
                    sc.wait("vector", XH, 16)
                    ln_tile(xhalo[:], 2, hstat, haggr, htmp, xnh[:], xnh16[:],
                            pe_guard=trx_guard[0])
                    sc.wait("tensor", s_s)
                    for kb in range(KT):
                        sc.op("tensor", lambda e, kb=kb, trA=trA: e.transpose(
                            trA[:, kb * 2:(kb + 1) * 2],
                            xnh16[:, kb * 128:(kb + 1) * 128], ident[0:2, 0:2]),
                            inc=(s_pe if kb == KT - 1 else None))
                    sc.wait("scalar", s_pe)
                    sc.op("scalar", lambda e, trA=trA: e.copy(
                        xnT[:].rearrange("p (kt q) -> p kt q", q=QW)[:, :, 512:514],
                        trA[:, :2 * KT].rearrange("p (kt q) -> p kt q", q=2)), inc=s_s)
                    sc.wait("tensor", s_s)

                if alevel < 2:
                    return
                xnT3 = xnT[:].rearrange("p (kt q) -> p kt q", q=QW)
                for m in range(16):
                    q = m % 2
                    if qk_ev[q] > 0:
                        sc.wait("tensor", s_s, qk_ev[q])
                    for kt in range(KT):
                        if st == 0 and m == 0:
                            sc.wait("tensor", s_gld, gld_kt[kt])
                        sc.op("tensor", lambda e, kt=kt, m=m, q=q, qkps=qkps, xnT3=xnT3: e.matmul(
                            qkps[q][:, :],
                            wqkv16[:, kt * 3 * D + m * 128:kt * 3 * D + m * 128 + 128],
                            xnT3[:, kt, 0:512], start=(kt == 0), stop=(kt == KT - 1)),
                            inc=(s_pe if kt == KT - 1 else None))
                    sc.wait("scalar", s_pe)
                    sc.op("scalar", lambda e, m=m, q=q, par=par, qkps=qkps: e.copy(
                        qkT[par][:, m * QW:m * QW + 512], qkps[q][:, :]), inc=s_s)
                    qk_ev[q] = s_s.n
                    if m % 8 == 7:
                        emit_w_quants()
                if st == NST - 1:
                    for m in range(8, 16):
                        for kt in range(KT):
                            sc.op("tensor", lambda e, kt=kt, m=m, hps=hps, xnT3=xnT3: e.matmul(
                                hps[:, (m - 8) * 2:(m - 8) * 2 + 2],
                                wqkv16[:, kt * 3 * D + m * 128:kt * 3 * D + m * 128 + 128],
                                xnT3[:, kt, 512:514], start=(kt == 0), stop=(kt == KT - 1)),
                                inc=(s_pe if kt == KT - 1 and m == 15 else None))
                    sc.wait("scalar", s_pe)
                    sc.op("scalar", lambda e, par=par, hps=hps: e.copy(
                        qkT[par][:].rearrange("p (m q) -> p m q", q=QW)[:, 8:16, 512:514],
                        hps[:].rearrange("p (m q) -> p m q", q=2)), inc=s_s)

                v_ev = 0
                for sub in range(4):
                    if v_ev > 0:
                        sc.wait("tensor", s_s, v_ev)
                    for kt in range(KT):
                        for nch in range(2):
                            sc.op("tensor", lambda e, kt=kt, sub=sub, nch=nch, vps=vps, xnT3=xnT3: e.matmul(
                                vps[:, nch * 512:nch * 512 + 512],
                                xnT3[:, kt, sub * 128:sub * 128 + 128],
                                wqkv16[:, kt * 3 * D + 2 * D + nch * 512:kt * 3 * D + 2 * D + nch * 512 + 512],
                                start=(kt == 0), stop=(kt == KT - 1)),
                                inc=(s_pe if kt == KT - 1 and nch == 1 else None))
                    sc.wait("scalar", s_pe)
                    sc.op("scalar", lambda e, sub=sub, par=par, vps=vps: e.copy(
                        vsb[par][:, sub * D:sub * D + D], vps[:, :]), inc=s_s)
                    v_ev = s_s.n
                    if sub % 2 == 1:
                        emit_w_quants()
                if st == NST - 1:
                    sc.wait("tensor", s_s, v_ev)
                    for m in range(8):
                        for kt in range(KT):
                            sc.op("tensor", lambda e, kt=kt, m=m, hps=hps, xnT3=xnT3: e.matmul(
                                hps[:, m * 2:m * 2 + 2],
                                wqkv16[:, kt * 3 * D + 2 * D + m * 128:kt * 3 * D + 2 * D + m * 128 + 128],
                                xnT3[:, kt, 512:514], start=(kt == 0), stop=(kt == KT - 1)),
                                inc=(s_pe if kt == KT - 1 and m == 7 else None))
                    sc.wait("scalar", s_pe)
                    sc.op("scalar", lambda e, hps=hps: e.copy(vh16[:], hps[:]), inc=s_s)
                    sc.wait("tensor", s_s)
                    for m in range(8):
                        sc.op("tensor", lambda e, m=m, trA=trA: e.transpose(
                            trA[0:2, m * 128:(m + 1) * 128],
                            vh16[:, m * 2:m * 2 + 2], ident[:]),
                            inc=(s_pe if m == 7 else None))
                    sc.wait("scalar", s_pe)
                    sc.op("scalar", lambda e, trA=trA: e.copy(
                        vhalo[:], trA[0:2, 0:D]), inc=s_s)

        def attn(st):
            if alevel < 3:
                return
            par = st % 2
            qk3 = qkT[par][:].rearrange("p (m q) -> p m q", q=QW)
            qk3n = qkT[1 - par][:].rearrange("p (m q) -> p m q", q=QW)
            scps = [P0[:, 0:390], P1[:, 0:390]]
            sc.wait("tensor", s_s)
            sc.wait("tensor", s_v)
            qk3o = qkT[par][:].rearrange("p (m q) -> p m q", q=QW)
            qk3no = qkT[1 - par][:].rearrange("p (m q) -> p m q", q=QW)
            ep_guard = [0]
            pt_guard = [0]
            ptm_ev = {}
            sout_free = [0]
            sc.wait("sync", s_s)
            odd_guard = [0]

            def emit_tail(sub):
                t = 4 * st + sub
                otp = P2
                sc.wait("tensor", s_s, ptm_ev[sub])
                sc.wait("tensor", s_s, sout_free[0])
                for hh in range(H):
                    r0 = 64 * (hh % 2)
                    mco = (hh // 2) * 128
                    vmain = vsb[par][:, sub * D + hh * 64:sub * D + hh * 64 + 64]
                    if sub < 3:
                        vnext = vsb[par][0:2, (sub + 1) * D + hh * 64:(sub + 1) * D + hh * 64 + 64]
                    elif st < NST - 1:
                        vnext = vsb[1 - par][0:2, hh * 64:hh * 64 + 64]
                    else:
                        vnext = vhalo[0:2, hh * 64:hh * 64 + 64]
                    sc.op("tensor", lambda e, vmain=vmain, hh=hh, r0=r0, mco=mco, otp=otp: e.matmul(
                        otp[r0:r0 + 64, mco:mco + 128], vmain,
                        PTm[:, hh * 128:hh * 128 + 128], start=True, stop=False))
                    sc.op("tensor", lambda e, vnext=vnext, hh=hh, r0=r0, mco=mco, otp=otp: e.matmul(
                        otp[r0:r0 + 64, mco:mco + 128], vnext,
                        PTt[0:2, hh * 128:hh * 128 + 128], start=False, stop=True),
                        inc=(s_pe if hh == H - 1 else None))
                sc.wait("scalar", s_pe)
                sc.op("scalar", lambda e, otp=otp: e.copy(outT16[:], otp[:]), inc=s_s)
                sps = P2
                sc.wait("tensor", s_s)
                sc.wait("tensor", s_gld, ngld[0])
                for kt in range(KT):
                    for nch in range(2):
                        sc.op("tensor", lambda e, kt=kt, nch=nch, sps=sps: e.matmul(
                            sps[:, nch * 512:nch * 512 + 512],
                            outT16[:, kt * 128:kt * 128 + 128],
                            wproj16[:, kt * D + nch * 512:kt * D + nch * 512 + 512],
                            start=(kt == 0), stop=False))
                for nch in range(2):
                    sc.op("tensor", lambda e, nch=nch, sps=sps: e.matmul(
                        sps[:, nch * 512:nch * 512 + 512], onesr[:],
                        bproj16[:, nch * 512:nch * 512 + 512],
                        start=False, stop=True),
                        inc=(s_pe if nch == 1 else None))
                sc.wait("scalar", s_pe)
                sc.wait("scalar", s_sst, sst_guard[0])
                sc.op("scalar", lambda e, sps=sps: e.copy(sOut[:], sps[:]), inc=s_s)
                sout_free[0] = s_s.n
                sc.wait("sync", s_s)
                sc.dma("sync", d_s[t, :, :], sOut[:], inc=s_sst)
                sst_guard[0] = s_sst.n

            for sub in range(4):
                t = 4 * st + sub
                if odd_guard[0] > 0:
                    sc.wait("sync", s_pe, odd_guard[0])
                sc.dma("sync", qodd[:].rearrange("p (m q) -> p m q", q=128),
                       qk3o[64:128, 0:8, sub * 128:sub * 128 + 128], inc=SH)
                kodd3 = kodd[:].rearrange("p (m q) -> p m q", q=132)
                if sub < 3 or st == NST - 1:
                    sc.dma("sync", kodd3[:, :, 0:130],
                           qk3o[64:128, 8:16, sub * 128:sub * 128 + 130], inc=SH)
                else:
                    sc.dma("sync", kodd3[:, :, 0:128],
                           qk3o[64:128, 8:16, sub * 128:sub * 128 + 128], inc=SH)
                    sc.dma("sync", kodd3[:, :, 128:130],
                           qk3no[64:128, 8:16, 0:2], inc=SH)
                sh_n = SH.n
                groups = [(0, 3), (3, 6), (6, 9), (9, 12), (12, 14), (14, 16)]
                sc.wait("tensor", SH, sh_n)
                for gi, (h0, h1) in enumerate(groups):
                    g = gi % 2
                    if sc_ev[g] > 0:
                        sc.wait("tensor", s_s, sc_ev[g])
                    for hh in range(h0, h1):
                        odd = hh % 2
                        mq = hh // 2
                        co = (hh - h0) * 130
                        if odd:
                            lhs = qodd[:].rearrange("p (m q) -> p m q", q=128)[:, mq, :]
                            rhs = kodd[:].rearrange("p (m q) -> p m q", q=132)[:, mq, 0:130]
                            sc.op("tensor", lambda e, lhs=lhs, rhs=rhs, g=g, co=co, scps=scps: e.matmul(
                                scps[g][:, co:co + 130], lhs, rhs, start=True, stop=True),
                                inc=(s_pe if hh == h1 - 1 else None))
                        else:
                            mk = 8 + hh // 2
                            lhs = qk3[0:64, mq, sub * 128:sub * 128 + 128]
                            if sub < 3 or st == NST - 1:
                                rhs = qk3[0:64, mk, sub * 128:sub * 128 + 130]
                                sc.op("tensor", lambda e, lhs=lhs, rhs=rhs, g=g, co=co, scps=scps: e.matmul(
                                    scps[g][:, co:co + 130], lhs, rhs, start=True, stop=True),
                                    inc=(s_pe if hh == h1 - 1 else None))
                            else:
                                rhs = qk3[0:64, mk, sub * 128:sub * 128 + 128]
                                rhs2 = qk3n[0:64, mk, 0:2]
                                sc.op("tensor", lambda e, lhs=lhs, rhs=rhs, g=g, co=co, scps=scps: e.matmul(
                                    scps[g][:, co:co + 128], lhs, rhs, start=True, stop=True))
                                sc.op("tensor", lambda e, lhs=lhs, rhs2=rhs2, g=g, co=co, scps=scps: e.matmul(
                                    scps[g][:, co + 128:co + 130], lhs, rhs2, start=True, stop=True),
                                    inc=(s_pe if hh == h1 - 1 else None))
                    sc.wait("scalar", s_pe)
                    if gi == 0:
                        sc.wait("scalar", s_v, ep_guard[0])
                        sc.wait("scalar", s_pe, pt_guard[0])
                    sc.op("scalar", lambda e, h0=h0, h1=h1, g=g, scps=scps: e.activation(
                        e16[:, h0 * 130:h1 * 130], scps[g][:, :(h1 - h0) * 130],
                        AF.Exp, scale=cvec[:, 2:3]), inc=s_s)
                    sc_ev[g] = s_s.n
                exp_ev = s_s.n
                odd_guard[0] = s_pe.n
                emit_w_stores()
                if alevel < 4:
                    continue
                if sub > 0 and alevel >= 6:
                    emit_tail(sub - 1)
                mask = maskl if t == NT - 1 else maskm
                e3 = e16[:].rearrange("p (h w) -> p h w", w=130)
                eb3 = e3
                p3 = e3
                mb = mask[:].unsqueeze(1).broadcast_to([128, H, 130])
                sc.wait("vector", s_s, exp_ev)
                sc.wait("vector", s_pe, pt_guard[0])
                sc.op("vector", lambda e, e3=e3, eb3=eb3, mb=mb: e.tensor_tensor(
                    eb3, e3, mb, ALU.mult))
                sc.op("vector", lambda e, eb3=eb3: e.tensor_reduce(
                    rs32[:], eb3, AX.X, ALU.add), inc=s_v)
                sc.wait("vector", s_v)
                sc.op("vector", lambda e: e.reciprocal(rsi32[:], rs32[:]), inc=s_v)
                sc.wait("vector", s_v)
                sc.op("vector", lambda e: e.tensor_copy(rinv16[:], rsi32[:]), inc=s_v)
                sc.wait("vector", s_v)
                rb = rinv16[:].unsqueeze(2).broadcast_to([128, H, 130])
                sc.op("vector", lambda e, p3=p3, eb3=eb3, rb=rb: e.tensor_tensor(
                    p3, eb3, rb, ALU.mult), inc=s_v)
                ep_guard[0] = s_v.n
                if dbg & 16:
                    sc.wait("sync", s_v)
                    sc.dma("sync", d_e[t, :, :], e16[:], inc=s_dbg)
                    sc.dma("sync", d_p[t, :, :], e16[:], inc=s_dbg)
                    sc.dma("sync", d_rs[t, :, :], rs32[:], inc=s_dbg)
                    sc.dma("sync", d_eb[t, :, :], e16[:], inc=s_dbg)
                    sc.dma("sync", d_ri[t, :, :], rinv16[:], inc=s_dbg)
                    sc.wait("scalar", s_dbg)
                if alevel < 5:
                    continue
                if True:
                    ptm = P4; ptt = P3b
                    sc.wait("tensor", s_v)
                    sc.wait("tensor", s_s)
                    for hh in range(H):
                        sc.op("tensor", lambda e, hh=hh, ptm=ptm: e.transpose(
                            ptm[:, hh * 128:hh * 128 + 128],
                            e16[:, hh * 130:hh * 130 + 128], ident[:]))
                    for hh in range(H):
                        sc.op("tensor", lambda e, hh=hh, ptt=ptt: e.transpose(
                            ptt[0:2, hh * 128:hh * 128 + 128],
                            e16[:, hh * 130 + 128:hh * 130 + 130], ident[:]),
                            inc=(s_pe if hh == H - 1 else None))
                    pt_guard[0] = s_pe.n
                    sc.wait("scalar", s_pe)
                    sc.op("scalar", lambda e, ptm=ptm: e.copy(PTm[:], ptm[:]))
                    sc.op("scalar", lambda e, ptt=ptt: e.copy(PTt[:], ptt[0:2, :]), inc=s_s)
                    ptm_ev[sub] = s_s.n
                emit_w_quants()
            if alevel >= 6:
                emit_tail(3)

        for st in range(NST):
            stage1(st)
            if st > 0:
                attn(st - 1)
        attn(NST - 1)
        flush_w()
        n_w_store = s_st.n

    nA_st = s_st.n
    nA_xq = s_xqst.n
    nA_ss = s_sst.n
    nA_v = s_v.n
    nA_s = s_s.n
    nA_pe = s_pe.n
    if phases == "WA":
        sc.wait("sync", s_st, nA_st)
        sc.wait("sync", s_xqst, nA_xq)
        sc.wait("sync", s_sst, nA_ss)
        sc.wait("sync", s_v, nA_v)
        sc.wait("sync", s_s, nA_s)
        sc.wait("sync", s_pe, nA_pe)
        sc.run()
        return nc

    with contextlib.ExitStack() as _esB:
        def sbb(name, shape, dt):
            return _esB.enter_context(nc.sbuf_tensor(name, shape, dt)).ap()
        wgT = sbb("wgT", [128, KT * I_FF], BF16)
        wuT = sbb("wuT", [128, KT * I_FF], BF16)
        wdT = sbb("wdT", [128, IKB * D], BF16)
        xqTt = sbb("xqTt", [128, 2 * D], BF16)
        sT = sbb("sT", [128, 2 * D], BF16)
        xT = sbb("xT", [128, 2 * D], F32)
        sgb = sbb("sgb", [128, 1024], F32)
        hb = sbb("hb", [128, 1024], F32)
        hq16 = sbb("hq16", [128, I_FF], BF16)
        hqT = sbb("hqT", [128, IKB * 128], BF16)
        gb = sbb("gb", [128, D], F32)
        tb = sbb("tbb", [128, D], F32)
        ob = sbb("ob", [128, 2 * D], F32)

        for qq in ("sync", "scalar", "gpsimd"):
            sc.wait(qq, s_st, nA_st)
            sc.wait(qq, s_xqst, nA_xq)
            sc.wait(qq, s_sst, nA_ss)
            sc.wait(qq, s_v, nA_v)
            sc.wait(qq, s_s, nA_s)
            sc.wait(qq, s_pe, nA_pe)

        BL = [Sem(nc, "s_bl0"), Sem(nc, "s_bl1")]
        bl_idx = {}
        pe_tile_end = {}
        v_tile_end = {}
        hq_guard = {}

        def bload(t):
            par = t % 2
            if t - 2 in pe_tile_end:
                sc.wait("gpsimd", s_pe, pe_tile_end[t - 2])
                sc.wait("gpsimd", s_v, v_tile_end[t - 2])
            sc.dma("gpsimd", xqTt[:, par * D:par * D + D], d_xqT[t, :, :], inc=BL[par])
            sc.dma("gpsimd", sT[:, par * D:par * D + D], d_s[t, :, :], inc=BL[par])
            sc.dma("gpsimd", xT[:, par * D:par * D + D],
                   xs[t * 128:(t + 1) * 128, :], inc=BL[par])
            bl_idx[t] = BL[par].n

        s_wg = [Sem(nc, "s_wg%d" % i) for i in range(6)]
        s_wu = [Sem(nc, "s_wu%d" % i) for i in range(6)]
        s_wd = [Sem(nc, "s_wd%d" % i) for i in range(3)]
        for ci, (c0, c1) in enumerate(IC_CH):
            for kt in range(KT):
                sc.dma("sync", wgT[:, kt * I_FF + c0:kt * I_FF + c1],
                       d_wgT[kt, :, c0:c1], inc=s_wg[ci])
                sc.dma("scalar", wuT[:, kt * I_FF + c0:kt * I_FF + c1],
                       d_wuT[kt, :, c0:c1], inc=s_wu[ci])
        bload(0)
        for g in range(3):
            for kt in range(8 * g, min(IKB, 8 * g + 8)):
                sc.dma("gpsimd", wdT[:, kt * D:(kt + 1) * D], d_wdT[kt, :, :], inc=s_wd[g])
            if g == 0:
                bload(1)

        if True:
            gpsA = [P0, P3[:, 0:512]]
            upsA = [P1, P3[:, 512:1024]]
            dps = P2
            trB = P4[:, 0:1024]

            h_ev = {}
            sg_ev = {}
            q_ev = {}
            hqT_ev = {}
            hqT_ev_roll = {}
            gb_ev = {}
            pending = {}

            def tr_pair(t, pr, p0, pw):
                sc.wait("tensor", s_v, q_ev[(t, pr)])
                nblk = pw // 128
                gpr = t * 3 + pr
                if gpr >= 1:
                    sc.wait("tensor", s_s, hqT_ev_roll[gpr - 1])
                for b in range(nblk):
                    kb = p0 // 128 + b
                    sc.op("tensor", lambda e, b=b, kb=kb: e.transpose(
                        trB[:, b * 128:(b + 1) * 128],
                        hq16[:, kb * 128:(kb + 1) * 128], ident[:]),
                        inc=(s_pe if b == nblk - 1 else None))
                hq_guard[pr] = s_pe.n
                sc.wait("scalar", s_pe, hq_guard[pr])
                sc.op("scalar", lambda e, p0=p0, nblk=nblk: e.copy(
                    hqT[:, p0:p0 + nblk * 128], trB[:, :nblk * 128]), inc=s_s)
                hqT_ev[(t, pr)] = s_s.n
                hqT_ev_roll[gpr] = s_s.n

            def emit_down_pair(t, pr):
                c0 = IC_CH[pr * 2][0]
                c1 = IC_CH[pr * 2 + 1][1]
                if t <= 1:
                    g = min(pr, 2)
                    sc.wait("tensor", s_wd[g], s_wd[g].n)
                if pr == 0 and (t - 1) in gb_ev:
                    sc.wait("tensor", s_s, gb_ev[t - 1])
                sc.wait("tensor", s_s, hqT_ev[(t, pr)])
                for kt in range(c0 // 128, c1 // 128):
                    for nch in range(2):
                        sc.op("tensor", lambda e, kt=kt, nch=nch: e.matmul(
                            dps[:, nch * 512:nch * 512 + 512],
                            hqT[:, kt * 128:kt * 128 + 128],
                            wdT[:, kt * D + nch * 512:kt * D + nch * 512 + 512],
                            start=(kt == 0), stop=(kt == IKB - 1)),
                            inc=(s_pe if kt == c1 // 128 - 1 and nch == 1 else None))

            def tile_epilogue(t):
                par = t % 2
                s_t = sT[:, par * D:par * D + D]
                x_t = xT[:, par * D:par * D + D]
                pe_tile_end[t] = s_pe.n
                sc.wait("scalar", s_pe)
                sc.wait("scalar", s_v)
                sc.op("scalar", lambda e: e.activation(gb[:], dps[:], AF.Silu), inc=s_s)
                gb_ev[t] = s_s.n
                sc.wait("vector", s_s, gb_ev[t])
                if t >= 2:
                    sc.wait("vector", OUT[par], OUT[par].n)
                sc.op("vector", lambda e, s_t=s_t: e.tensor_tensor(
                    tb[:], s_t, gb[:], ALU.mult))
                sc.op("vector", lambda e, x_t=x_t, par=par: e.tensor_tensor(
                    ob[:, par * D:par * D + D], x_t, tb[:], ALU.add), inc=s_v)
                v_tile_end[t] = s_v.n
                sc.wait("sync", s_v)
                sc.dma("sync", out_d[t * 128:(t + 1) * 128, :],
                       ob[:, par * D:par * D + D], inc=OUT[par])
                if t + 2 < NT:
                    bload(t + 2)

            NG = NT * 6
            for gci in range(NG):
                t, ci = divmod(gci, 6)
                c0, c1 = IC_CH[ci]
                cw = c1 - c0
                par = t % 2
                xq_t = xqTt[:, par * D:par * D + D]
                gps = gpsA[ci % 2]
                ups = upsA[ci % 2]
                if ci == 0:
                    sc.wait("tensor", BL[par], bl_idx[t])
                if gci >= 2:
                    sc.wait("tensor", s_s, sg_ev[gci - 2])
                    sc.wait("tensor", s_v, h_ev[gci - 2])
                if t <= 1:
                    sc.wait("tensor", s_wg[ci], s_wg[ci].n)
                    sc.wait("tensor", s_wu[ci], s_wu[ci].n)
                for kt in range(KT):
                    sc.op("tensor", lambda e, kt=kt, c0=c0, cw=cw, xq_t=xq_t, gps=gps: e.matmul(
                        gps[:, :cw], xq_t[:, kt * 128:kt * 128 + 128],
                        wgT[:, kt * I_FF + c0:kt * I_FF + c0 + cw],
                        start=(kt == 0), stop=(kt == KT - 1)))
                for kt in range(KT):
                    sc.op("tensor", lambda e, kt=kt, c0=c0, cw=cw, xq_t=xq_t, ups=ups: e.matmul(
                        ups[:, :cw], xq_t[:, kt * 128:kt * 128 + 128],
                        wuT[:, kt * I_FF + c0:kt * I_FF + c0 + cw],
                        start=(kt == 0), stop=(kt == KT - 1)),
                        inc=(s_pe if kt == KT - 1 else None))
                gu_end = s_pe.n
                for fn in pending.pop(gci, []):
                    fn()
                sc.wait("scalar", s_pe, gu_end)
                sc.wait("scalar", s_v, h_ev.get(gci - 2, 0))
                sc.op("scalar", lambda e, cw=cw, gps=gps, ci=ci: e.activation(
                    sgb[:, (ci % 2) * 512:(ci % 2) * 512 + cw], gps[:, :cw], AF.Silu), inc=s_s)
                sg_ev[gci] = s_s.n
                sc.wait("vector", s_s, sg_ev[gci])
                sc.op("vector", lambda e, cw=cw, ups=ups, ci=ci: e.tensor_tensor(
                    hb[:, (ci % 2) * 512:(ci % 2) * 512 + cw],
                    sgb[:, (ci % 2) * 512:(ci % 2) * 512 + cw],
                    ups[:, :cw], ALU.mult), inc=s_v)
                h_ev[gci] = s_v.n
                if ci % 2 == 1:
                    pr = ci // 2
                    p0 = IC_CH[ci - 1][0]
                    pw = c1 - p0
                    quant(hb[:, :pw], hq16[:, p0:c1], pw, pe_guard=hq_guard.get(pr, 0))
                    q_ev[(t, pr)] = s_v.n

                    def deferred(t=t, pr=pr, p0=p0, pw=pw):
                        tr_pair(t, pr, p0, pw)
                        emit_down_pair(t, pr)
                        if pr == 2:
                            tile_epilogue(t)
                    pending.setdefault(gci + 2, []).append(deferred)
            for gci in range(NG, NG + 2):
                for fn in pending.pop(gci, []):
                    fn()

    sc.wait("sync", OUT[0], OUT[0].n)
    sc.wait("sync", OUT[1], OUT[1].n)
    sc.run()
    return nc


_prog_cache = {}


def kernel(L0_attn_x, L0_attn_w_qkv, L0_attn_w_proj, L0_attn_b_proj,
           L0_ffn_w_gate, L0_ffn_w_up, L0_ffn_w_down):
    x = np.ascontiguousarray(np.asarray(L0_attn_x, dtype=np.float32))
    B, T, _ = x.shape
    n_cores = 8
    per_b = n_cores // B

    if "nc" not in _prog_cache:
        _prog_cache["nc"] = build_program()
    nc = _prog_cache["nc"]

    i_idx = np.arange(128)[:, None]
    j_idx = np.arange(130)[None, :]
    band = (j_idx >= i_idx) & (j_idx <= i_idx + 2)
    maskmain = band.astype(ml_dtypes.bfloat16)
    masklast_edge = (band & (j_idx < 128)).astype(ml_dtypes.bfloat16)
    ident = np.eye(128, dtype=ml_dtypes.bfloat16)
    onesrow = np.ones((1, 128), dtype=ml_dtypes.bfloat16)
    cvec = np.tile(np.array([[1e-5, -0.5, 0.125, -1.0]], dtype=np.float32), (128, 1))

    wqkv = np.ascontiguousarray(np.asarray(L0_attn_w_qkv, dtype=np.float32))
    wproj = np.ascontiguousarray(np.asarray(L0_attn_w_proj, dtype=np.float32))
    bproj = np.ascontiguousarray(np.asarray(L0_attn_b_proj, dtype=np.float32)).reshape(1, -1)
    wg = np.ascontiguousarray(np.asarray(L0_ffn_w_gate, dtype=np.float32))
    wu = np.ascontiguousarray(np.asarray(L0_ffn_w_up, dtype=np.float32))
    wd = np.ascontiguousarray(np.asarray(L0_ffn_w_down, dtype=np.float32))

    in_maps = []
    for c in range(n_cores):
        b = c // per_b
        r0 = (c % per_b) * TOK
        xsc = np.zeros((TOK + 2, D), dtype=np.float32)
        end = min(r0 + TOK + 2, T)
        xsc[:end - r0] = x[b, r0:end]
        last = (c % per_b) == per_b - 1
        in_maps.append(dict(
            xs=xsc, w_qkv=wqkv, w_proj=wproj, b_proj=bproj,
            wg=wg, wu=wu, wd=wd, ident=ident,
            maskmain=maskmain, masklast=(masklast_edge if last else maskmain),
            onesrow=onesrow, cvec=cvec,
        ))

    res = run_bass_kernel_spmd(nc, in_maps, core_ids=list(range(n_cores)))
    out = np.empty((B, T, D), dtype=np.float32)
    for c in range(n_cores):
        b = c // per_b
        r0 = (c % per_b) * TOK
        out[b, r0:r0 + TOK] = np.asarray(res.results[c]["out"], dtype=np.float32)
    return out


# version 16
# speedup vs baseline: 1.0094x; 1.0094x over previous
import contextlib
import numpy as np
import ml_dtypes
import concourse.bass as bass
import concourse.mybir as mybir
from concourse.bass_utils import run_bass_kernel_spmd

F32 = mybir.dt.float32
BF16 = mybir.dt.bfloat16
AF = mybir.ActivationFunctionType
ALU = mybir.AluOpType
AX = mybir.AxisListType

M1 = 12582912.0
M2 = 25165824.0

D = 1024
KT = 8
H = 16
TOK = 2048
NT = 16
NST = 4
I_FF = 2816
IKB = 22
QW = 520
IC_CH = [(0, 512), (512, 1024), (1024, 1536), (1536, 2048), (2048, 2560), (2560, 2816)]

ENGS = ("sync", "vector", "scalar", "tensor", "gpsimd")


class Sem:
    def __init__(self, nc, name):
        self.h = nc.alloc_semaphore(name)
        self.n = 0


class Sched:

    def __init__(self, nc):
        self.nc = nc
        self.q = {e: [] for e in ENGS}

    def wait(self, eng, sem, val=None):
        v = sem.n if val is None else val
        if v > 0:
            self.q[eng].append(("w", sem.h, v))

    def op(self, eng, fn, inc=None, k=1):
        incs = []
        if inc is not None:
            for s in (inc if isinstance(inc, (list, tuple)) else [inc]):
                s.n += k
                incs.append((s.h, k))
        self.q[eng].append(("o", fn, incs))

    def dma(self, eng, out_ap, in_ap, inc=None):
        self.op(eng, lambda e, o=out_ap, i=in_ap: e.dma_start(o, i), inc=inc, k=16)

    def run(self):
        with self.nc.Block() as blk:
            def emit(e, items):
                for it in items:
                    if it[0] == "w":
                        e.wait_ge(it[1], it[2])
                    else:
                        instr = it[1](e)
                        for (h, k) in it[2]:
                            instr = instr.then_inc(h, k)
            blk.sync(lambda e: emit(e, self.q["sync"]))
            blk.vector(lambda e: emit(e, self.q["vector"]))
            blk.scalar(lambda e: emit(e, self.q["scalar"]))
            blk.tensor(lambda e: emit(e, self.q["tensor"]))
            blk.gpsimd(lambda e: emit(e, self.q["gpsimd"]))


def build_program(phases="WAB", alevel=9, dbg=0):
    assert phases in ("WA", "WAB")
    nc = bass.Bass("TRN2", target_bir_lowering=False)
    sc = Sched(nc)

    def din(name, shape, dt):
        return nc.dram_tensor(name, shape, dt, kind="ExternalInput").ap()

    xs = din("xs", [TOK + 2, D], F32)
    w_qkv = din("w_qkv", [D, 3 * D], F32)
    w_proj = din("w_proj", [D, D], F32)
    b_proj = din("b_proj", [1, D], F32)
    wg_d = din("wg", [I_FF, D], F32)
    wu_d = din("wu", [I_FF, D], F32)
    wd_d = din("wd", [D, I_FF], F32)
    ident_in = din("ident", [128, 128], BF16)
    maskmain_in = din("maskmain", [128, 130], BF16)
    masklast_in = din("masklast", [128, 130], BF16)
    onesrow_in = din("onesrow", [1, 128], BF16)
    cvec_in = din("cvec", [128, 4], F32)

    out_d = nc.dram_tensor("out", [TOK, D], F32, kind="ExternalOutput").ap()

    _stage_kind = "ExternalOutput" if (dbg & 8) else "Internal"
    d_s = nc.dram_tensor("d_s", [NT, 128, D], BF16, kind=_stage_kind).ap()
    d_xqT = nc.dram_tensor("d_xqT", [NT, 128, D], BF16, kind=_stage_kind).ap()
    if dbg & 16:
        d_e = nc.dram_tensor("d_e", [NT, 128, H * 130], BF16, kind="ExternalOutput").ap()
        d_p = nc.dram_tensor("d_p", [NT, 128, H * 130], BF16, kind="ExternalOutput").ap()
        d_rs = nc.dram_tensor("d_rs", [NT, 128, H], F32, kind="ExternalOutput").ap()
        d_eb = nc.dram_tensor("d_eb", [NT, 128, H * 130], BF16, kind="ExternalOutput").ap()
        d_ri = nc.dram_tensor("d_ri", [NT, 128, H], BF16, kind="ExternalOutput").ap()
        d_xn = nc.dram_tensor("d_xn", [NT, 128, D], F32, kind="ExternalOutput").ap()
        d_xt = nc.dram_tensor("d_xt", [NT, 128, D], F32, kind="ExternalOutput").ap()
    d_wgT = nc.dram_tensor("d_wgT", [KT, 128, IKB * 128], BF16, kind="Internal").ap()
    d_wuT = nc.dram_tensor("d_wuT", [KT, 128, IKB * 128], BF16, kind="Internal").ap()
    d_wdT = nc.dram_tensor("d_wdT", [IKB, 128, KT * 128], BF16, kind="Internal").ap()

    def sb(name, shape, dt):
        return nc.alloc_sbuf_tensor(name, shape, dt).ap()

    ident = sb("ident_s", [128, 128], BF16)
    maskm = sb("maskm_s", [128, 130], BF16)
    maskl = sb("maskl_s", [128, 130], BF16)
    onesr = sb("onesr_s", [1, 128], BF16)
    cvec = sb("cvec_s", [128, 4], F32)
    bproj16 = sb("bproj16", [1, D], BF16)

    QWID = 1024
    wbuf = sb("q_w", [128, QWID], F32)
    t1 = sb("q_t1", [128, QWID], BF16)
    t2 = sb("q_t2", [128, QWID], BF16)
    t3 = sb("q_t3", [128, QWID], BF16)
    t4 = sb("q_t4", [128, QWID], BF16)
    NBMAX = QWID // 16
    bmb = sb("q_bm", [128, NBMAX], F32)
    scb = sb("q_sc", [128, NBMAX], F32)
    lnb = sb("q_ln", [128, NBMAX], F32)
    rinvb = sb("q_rinv", [128, NBMAX], F32)
    rinv2b = sb("q_rinv2", [128, NBMAX], F32)
    qscb = sb("q_qsc", [128, NBMAX], BF16)

    P0 = nc.alloc_psum_tensor("P0", [128, 512], F32).ap()
    P1 = nc.alloc_psum_tensor("P1", [128, 512], F32).ap()
    P2 = nc.alloc_psum_tensor("P2", [128, 1024], F32).ap()
    P3 = nc.alloc_psum_tensor("P3", [128, 1024], F32).ap()
    P4 = nc.alloc_psum_tensor("P4", [128, 2048], BF16).ap()
    P3b = P3.bitcast(BF16)

    s_ld = Sem(nc, "s_ld")
    s_gld = Sem(nc, "s_gld")
    s_st = Sem(nc, "s_st")
    s_xqst = Sem(nc, "s_xqst")
    s_sst = Sem(nc, "s_sst")
    s_dbg = Sem(nc, "s_dbg")
    OUT = [Sem(nc, "s_out0"), Sem(nc, "s_out1")]
    s_pe = Sem(nc, "s_pe")
    s_v = Sem(nc, "s_v")
    s_s = Sem(nc, "s_s")

    sc.dma("sync", ident[:], ident_in[:], inc=s_ld)
    sc.dma("sync", maskm[:], maskmain_in[:], inc=s_ld)
    sc.dma("sync", maskl[:], masklast_in[:], inc=s_ld)
    sc.dma("sync", onesr[:], onesrow_in[:], inc=s_ld)
    sc.dma("sync", cvec[:], cvec_in[:], inc=s_ld)
    N_PROLOG = 5
    for _e in ("vector", "scalar", "tensor", "sync"):
        sc.wait(_e, s_ld, N_PROLOG * 16)
    sc.dma("gpsimd", bproj16[:], b_proj[:], inc=s_gld)

    def quant(src_ap, dst_ap, W, pe_guard=0):
        nb = W // 16
        src3 = src_ap.rearrange("p (nb b) -> p nb b", b=16)
        v, s = "vector", "scalar"
        sc.op(v, lambda e, src3=src3, nb=nb: e.tensor_reduce(
            bmb[:, :nb], src3, AX.X, ALU.max, apply_absolute_value=True), inc=s_v)
        sc.wait(v, s_v)
        sc.op(v, lambda e, nb=nb: e.tensor_scalar(
            scb[:, :nb], bmb[:, :nb], 1.0 / 6.0, 1e-12, ALU.mult, ALU.max), inc=s_v)
        sc.wait(v, s_v)
        sc.op(v, lambda e, nb=nb: e.reciprocal(rinvb[:, :nb], scb[:, :nb]), inc=s_v)
        sc.wait(v, s_v)
        sc.op(v, lambda e, nb=nb: e.tensor_scalar(
            rinv2b[:, :nb], rinvb[:, :nb], 2.0, None, ALU.mult))
        sc.op(v, lambda e, nb=nb: e.tensor_scalar(
            qscb[:, :nb], scb[:, :nb], 0.5, None, ALU.mult))
        r2bc = rinv2b[:, :nb].unsqueeze(2).broadcast_to([128, nb, 16])
        qsbc = qscb[:, :nb].unsqueeze(2).broadcast_to([128, nb, 16])
        wv = wbuf[:, :W]
        wv3 = wv.rearrange("p (nb b) -> p nb b", b=16)
        A_, B_, C_, D_ = (t1[:, :W], t2[:, :W], t3[:, :W], t4[:, :W])
        sc.op(v, lambda e, wv3=wv3, src3=src3, r2bc=r2bc: e.tensor_tensor(
            wv3, src3, r2bc, ALU.mult))
        sc.op(v, lambda e, A_=A_, wv=wv: e.tensor_scalar(A_, wv, M1, -M1, ALU.add, ALU.add))
        sc.op(v, lambda e, A_=A_, B_=B_: e.tensor_scalar(B_, A_, -4.0, 4.0, ALU.max, ALU.min))
        sc.op(v, lambda e, C_=C_, wv=wv: e.tensor_scalar(C_, wv, M2, -M2, ALU.add, ALU.add))
        sc.op(v, lambda e, C_=C_, D_=D_: e.tensor_scalar(D_, C_, -4.0, 4.0, ALU.max, ALU.min))
        sc.op(v, lambda e, A_=A_, C_=C_, D_=D_: e.tensor_tensor(A_, C_, D_, ALU.subtract))
        sc.op(v, lambda e, A_=A_, C_=C_: e.tensor_scalar(C_, A_, -4.0, 4.0, ALU.max, ALU.min))
        sc.op(v, lambda e, D_=D_, wv=wv: e.tensor_scalar(D_, wv, 10.0, 4.0, ALU.is_ge, ALU.mult))
        sc.op(v, lambda e, A_=A_, wv=wv: e.tensor_scalar(A_, wv, -10.0, 4.0, ALU.is_le, ALU.mult))
        sc.op(v, lambda e, A_=A_, D_=D_: e.tensor_tensor(D_, D_, A_, ALU.subtract))
        sc.op(v, lambda e, B_=B_, C_=C_: e.tensor_tensor(B_, B_, C_, ALU.add))
        sc.op(v, lambda e, B_=B_, D_=D_: e.tensor_tensor(B_, B_, D_, ALU.add))
        dst3 = dst_ap.rearrange("p (nb b) -> p nb b", b=16)
        B3 = B_.rearrange("p (nb b) -> p nb b", b=16)
        if pe_guard:
            sc.wait(v, s_pe, pe_guard)
        sc.op(v, lambda e, dst3=dst3, B3=B3, qsbc=qsbc: e.tensor_tensor(
            dst3, B3, qsbc, ALU.mult), inc=s_v)


    XL = [Sem(nc, "s_xl0"), Sem(nc, "s_xl1")]
    XH = Sem(nc, "s_xh")
    SH = Sem(nc, "s_sh")

    with contextlib.ExitStack() as _esA:
        def sba(name, shape, dt):
            return _esA.enter_context(nc.sbuf_tensor(name, shape, dt)).ap()
        xtile = sba("xtile", [128, 2 * D], F32)
        xhalo = sba("xhalo", [2, D], F32)
        stats = sba("ln_stats", [128, 12], F32)
        aggr = sba("ln_aggr", [128, 2], F32)
        lntmp = sba("ln_tmp", [128, 4], F32)
        xnf = sba("xnf", [128, D], F32)
        xn16 = sba("xn16", [128, D], BF16)
        xq16 = sba("xq16", [128, D], BF16)
        xqTst = sba("xqTst", [128, D], BF16)
        hstat = sba("hstat", [2, 12], F32)
        haggr = sba("haggr", [2, 2], F32)
        htmp = sba("htmp", [2, 4], F32)
        xnh = sba("xnh", [2, D], F32)
        xnh16 = sba("xnh16", [2, D], BF16)
        xnT = sba("xnT", [128, KT * QW], BF16)
        qkT = [sba("qkT0", [128, 16 * QW], BF16), sba("qkT1", [128, 16 * QW], BF16)]
        vsb = [sba("vsb0", [128, 4 * D], BF16), sba("vsb1", [128, 4 * D], BF16)]
        vhalo = sba("vhalo", [2, D], BF16)
        vh16 = sba("vh16", [128, 16], BF16)
        e16 = sba("e16", [128, H * 130], BF16)
        rs32 = sba("rs32", [128, H], F32)
        rsi32 = sba("rsi32", [128, H], F32)
        rinv16 = sba("rinv16b", [128, H], BF16)
        qodd = sba("qodd", [64, 8 * 128], BF16)
        kodd = sba("kodd", [64, 8 * 132], BF16)
        PTm = sba("PTm", [128, H * 128], BF16)
        PTt = sba("PTt", [2, H * 128], BF16)
        outT16 = sba("outT16", [128, D], BF16)
        sOut = sba("sOut", [128, D], BF16)
        wqkv16 = sba("wqkv16", [128, KT * 3 * D], BF16)
        wproj16 = sba("wproj16", [128, KT * D], BF16)

        gld_kt = []
        for kt in range(KT):
            sc.dma("gpsimd", wqkv16[:, kt * 3 * D:(kt + 1) * 3 * D],
                   w_qkv[kt * 128:(kt + 1) * 128, :], inc=s_gld)
            gld_kt.append(s_gld.n)
        ngld = [0]

        wpce = [sba("wpce0", [128, 1024], F32), sba("wpce1", [128, 1024], F32)]
        wq16s = [sba("w_q16%d" % i, [128, 1024], BF16) for i in range(4)]
        wstage = [sba("w_stage0", [128, 1024], BF16), sba("w_stage1", [128, 1024], BF16)]
        wtr = P4[:, 1024:2048]
        WL = [Sem(nc, "s_wl0"), Sem(nc, "s_wl1")]
        WLg = [Sem(nc, "s_wlg0"), Sem(nc, "s_wlg1")]
        wpieces = []
        for (srcw, dstw, nch, k) in ((wg_d, d_wgT, IKB, D), (wu_d, d_wuT, IKB, D),
                                     (wd_d, d_wdT, KT, I_FF)):
            for c in range(nch):
                for p0 in range(0, k, 1024):
                    wpieces.append((srcw, dstw, c, p0, min(p0 + 1024, k)))
        NWP = len(wpieces)
        wp_ld = {}
        wq_end = {}
        wt_end = {}
        wst_tot = {}
        wcur = [0, 0, 0]

        def w_load(pi):
            srcw, dstw, c, p0, p1 = wpieces[pi]
            if pi < 6:
                q, sem = "sync", WL[pi % 2]
            else:
                q, sem = "gpsimd", WLg[pi % 2]
            if pi - 2 in wq_end:
                sc.wait(q, s_v, wq_end[pi - 2])
            sc.dma(q, wpce[pi % 2][:, :p1 - p0],
                   srcw[c * 128:(c + 1) * 128, p0:p1], inc=sem)
            wp_ld[pi] = (sem, sem.n)

        def w_quant(pi):
            while wcur[0] < min(pi + 2, NWP):
                w_load(wcur[0])
                wcur[0] += 1
            srcw, dstw, c, p0, p1 = wpieces[pi]
            lsem, lcnt = wp_ld[pi]
            sc.wait("vector", lsem, lcnt)
            assert pi < 4 or (pi - 4) in wt_end
            quant(wpce[pi % 2][:, :p1 - p0], wq16s[pi % 4][:, :p1 - p0], p1 - p0,
                  pe_guard=wt_end.get(pi - 4, 0))
            wq_end[pi] = s_v.n

        def w_store(pi):
            srcw, dstw, c, p0, p1 = wpieces[pi]
            nb = (p1 - p0) // 128
            sc.wait("tensor", s_v, wq_end[pi])
            sc.wait("tensor", s_s)
            for b in range(nb):
                sc.op("tensor", lambda e, b=b, pi=pi: e.transpose(
                    wtr[:, b * 128:(b + 1) * 128],
                    wq16s[pi % 4][:, b * 128:(b + 1) * 128], ident[:]),
                    inc=(s_pe if b == nb - 1 else None))
            wt_end[pi] = s_pe.n
            sc.wait("scalar", s_pe)
            if pi - 2 in wst_tot:
                sc.wait("scalar", s_st, wst_tot[pi - 2])
            sc.op("scalar", lambda e, nb=nb, pi=pi: e.copy(
                wstage[pi % 2][:, :nb * 128], wtr[:, :nb * 128]), inc=s_s)
            sc.wait("scalar", s_s)
            out_ap = dstw[p0 // 128:p0 // 128 + nb, :, c * 128:(c + 1) * 128].transpose([1, 0, 2])
            in_ap = wstage[pi % 2][:, :nb * 128].rearrange("p (kb q) -> p kb q", q=128)
            sc.dma("scalar", out_ap, in_ap, inc=s_st)
            wst_tot[pi] = s_st.n

        wtick = [0]

        def emit_w_stores():
            while wcur[2] < wcur[1] - 2:
                w_store(wcur[2])
                wcur[2] += 1

        def emit_w_quants():
            wtick[0] += 1
            qtarget = min(wtick[0] * NWP // 48 + 1, NWP)
            while wcur[1] < qtarget:
                pi = wcur[1]
                while wcur[2] <= pi - 4:
                    w_store(wcur[2])
                    wcur[2] += 1
                w_quant(pi)
                wcur[1] += 1

        def flush_w():
            wtick[0] = 32
            emit_w_quants()
            while wcur[2] < NWP:
                w_store(wcur[2])
                wcur[2] += 1

        xl_idx = {}
        xl_cnt = [0]
        xn_guard = {}
        trx_guard = [0]
        sc_ev = [0, 0]
        qk_ev = [0, 0]
        xqst_guard = [0]
        sst_guard = [0]

        def xload(t, guard):
            if guard:
                sc.wait("gpsimd", s_v, guard)
            sc.dma("gpsimd", xtile[:, (t % 2) * D:(t % 2) * D + D],
                   xs[t * 128:(t + 1) * 128, :], inc=XL[t % 2])
            xl_idx[t] = XL[t % 2].n

        def ln_tile(xap, n_p, statsb, aggrb, tmpb, xout, x16out, pe_guard=0):
            v, s = "vector", "scalar"
            sc.op(v, lambda e, xap=xap, statsb=statsb, n_p=n_p: e.bn_stats(
                statsb[:n_p, 0:6], xap[:, 0:512]))
            sc.op(v, lambda e, xap=xap, statsb=statsb, n_p=n_p: e.bn_stats(
                statsb[:n_p, 6:12], xap[:, 512:1024]), inc=s_v)
            sc.wait(v, s_v)
            sc.op(v, lambda e, statsb=statsb, aggrb=aggrb, n_p=n_p: e.bn_aggr(
                aggrb[:n_p, :], statsb[:n_p, :]), inc=s_v)
            sc.wait(v, s_v)
            sc.op(v, lambda e, aggrb=aggrb, tmpb=tmpb, n_p=n_p: e.tensor_scalar(
                tmpb[:n_p, 0:1], aggrb[:n_p, 0:1], -1.0, None, ALU.mult), inc=s_v)
            sc.wait(s, s_v)
            sc.op(s, lambda e, aggrb=aggrb, tmpb=tmpb, n_p=n_p: e.activation(
                tmpb[:n_p, 1:2], aggrb[:n_p, 1:2], AF.Ln, bias=cvec[:n_p, 0:1]), inc=s_s)
            sc.wait(s, s_s)
            sc.op(s, lambda e, tmpb=tmpb, n_p=n_p: e.activation(
                tmpb[:n_p, 2:3], tmpb[:n_p, 1:2], AF.Exp, scale=cvec[:n_p, 1:2]),
                inc=s_s)
            sc.wait(v, s_s)
            sc.op(v, lambda e, xap=xap, xout=xout, tmpb=tmpb, n_p=n_p: e.tensor_scalar(
                xout, xap, tmpb[:n_p, 0:1], tmpb[:n_p, 2:3], ALU.add, ALU.mult),
                inc=s_v)
            sc.wait(s, s_v)
            if pe_guard:
                sc.wait(s, s_pe, pe_guard)
            sc.op(s, lambda e, xout=xout, x16out=x16out: e.copy(x16out, xout), inc=s_s)

        def stage1(st):
            par = st % 2
            if st == 1:
                for kt in range(KT):
                    sc.dma("gpsimd", wproj16[:, kt * D:(kt + 1) * D],
                           w_proj[kt * 128:(kt + 1) * 128, :], inc=s_gld)
                ngld[0] = s_gld.n
            if True:
                trA = P4[:, 0:1024]
                qkps = [P0, P1]
                vps = P2
                hps = P3[:, 0:16]
                sc.wait("tensor", s_s)
                sc.wait("tensor", s_v)

                for sub in range(2):
                    t = 4 * st + sub
                    xload(t, xn_guard.get(t - 2, 0))
                if st == NST - 1:
                    sc.dma("gpsimd", xhalo[:], xs[TOK:TOK + 2, :], inc=XH)

                for sub in range(4):
                    t = 4 * st + sub
                    emit_w_stores()
                    xap = xtile[:, (t % 2) * D:(t % 2) * D + D]
                    sc.wait("vector", XL[t % 2], xl_idx[t])
                    ln_tile(xap, 128, stats, aggr, lntmp, xnf[:], xn16[:],
                            pe_guard=trx_guard[0])
                    if dbg & 16:
                        sc.wait("sync", s_v)
                        sc.dma("sync", d_xn[t, :, :], xnf[:], inc=s_dbg)
                        sc.dma("sync", d_xt[t, :, :], xap, inc=s_dbg)
                        sc.wait("vector", s_dbg)
                    xn_guard[t] = s_v.n
                    if sub + 2 < 4 or st + 1 < NST:
                        xload(t + 2, xn_guard[t])
                    quant(xnf[:], xq16[:], D, pe_guard=trx_guard[0])
                    sc.wait("tensor", s_s)
                    sc.wait("tensor", s_v)
                    for kb in range(KT):
                        sc.op("tensor", lambda e, kb=kb, trA=trA: e.transpose(
                            trA[:, kb * 128:(kb + 1) * 128],
                            xn16[:, kb * 128:(kb + 1) * 128], ident[:]),
                            inc=(s_pe if kb == KT - 1 else None))
                    sc.wait("scalar", s_pe)
                    sc.op("scalar", lambda e, sub=sub, trA=trA: e.copy(
                        xnT[:].rearrange("p (kt q) -> p kt q", q=QW)[:, :, sub * 128:sub * 128 + 128],
                        trA[:].rearrange("p (kt q) -> p kt q", q=128)), inc=s_s)
                    sc.wait("tensor", s_s)
                    for kb in range(KT):
                        sc.op("tensor", lambda e, kb=kb, trA=trA: e.transpose(
                            trA[:, kb * 128:(kb + 1) * 128],
                            xq16[:, kb * 128:(kb + 1) * 128], ident[:]),
                            inc=(s_pe if kb == KT - 1 else None))
                    trx_guard[0] = s_pe.n
                    sc.wait("scalar", s_pe)
                    sc.wait("scalar", s_xqst, xqst_guard[0])
                    sc.op("scalar", lambda e, trA=trA: e.copy(xqTst[:], trA[:]), inc=s_s)
                    sc.wait("sync", s_s)
                    sc.dma("sync", d_xqT[t, :, :], xqTst[:], inc=s_xqst)
                    xqst_guard[0] = s_xqst.n
                    sc.wait("tensor", s_s)
                    emit_w_quants()

                if st == NST - 1:
                    sc.wait("vector", XH, 16)
                    ln_tile(xhalo[:], 2, hstat, haggr, htmp, xnh[:], xnh16[:],
                            pe_guard=trx_guard[0])
                    sc.wait("tensor", s_s)
                    for kb in range(KT):
                        sc.op("tensor", lambda e, kb=kb, trA=trA: e.transpose(
                            trA[:, kb * 2:(kb + 1) * 2],
                            xnh16[:, kb * 128:(kb + 1) * 128], ident[0:2, 0:2]),
                            inc=(s_pe if kb == KT - 1 else None))
                    sc.wait("scalar", s_pe)
                    sc.op("scalar", lambda e, trA=trA: e.copy(
                        xnT[:].rearrange("p (kt q) -> p kt q", q=QW)[:, :, 512:514],
                        trA[:, :2 * KT].rearrange("p (kt q) -> p kt q", q=2)), inc=s_s)
                    sc.wait("tensor", s_s)

                if alevel < 2:
                    return
                xnT3 = xnT[:].rearrange("p (kt q) -> p kt q", q=QW)
                for m in range(16):
                    q = m % 2
                    if qk_ev[q] > 0:
                        sc.wait("tensor", s_s, qk_ev[q])
                    for kt in range(KT):
                        if st == 0 and m == 0:
                            sc.wait("tensor", s_gld, gld_kt[kt])
                        sc.op("tensor", lambda e, kt=kt, m=m, q=q, qkps=qkps, xnT3=xnT3: e.matmul(
                            qkps[q][:, :],
                            wqkv16[:, kt * 3 * D + m * 128:kt * 3 * D + m * 128 + 128],
                            xnT3[:, kt, 0:512], start=(kt == 0), stop=(kt == KT - 1)),
                            inc=(s_pe if kt == KT - 1 else None))
                    sc.wait("scalar", s_pe)
                    sc.op("scalar", lambda e, m=m, q=q, par=par, qkps=qkps: e.copy(
                        qkT[par][:, m * QW:m * QW + 512], qkps[q][:, :]), inc=s_s)
                    qk_ev[q] = s_s.n
                    if m % 8 == 7:
                        emit_w_quants()
                if st == NST - 1:
                    for m in range(8, 16):
                        for kt in range(KT):
                            sc.op("tensor", lambda e, kt=kt, m=m, hps=hps, xnT3=xnT3: e.matmul(
                                hps[:, (m - 8) * 2:(m - 8) * 2 + 2],
                                wqkv16[:, kt * 3 * D + m * 128:kt * 3 * D + m * 128 + 128],
                                xnT3[:, kt, 512:514], start=(kt == 0), stop=(kt == KT - 1)),
                                inc=(s_pe if kt == KT - 1 and m == 15 else None))
                    sc.wait("scalar", s_pe)
                    sc.op("scalar", lambda e, par=par, hps=hps: e.copy(
                        qkT[par][:].rearrange("p (m q) -> p m q", q=QW)[:, 8:16, 512:514],
                        hps[:].rearrange("p (m q) -> p m q", q=2)), inc=s_s)

                v_ev = 0
                for sub in range(4):
                    if v_ev > 0:
                        sc.wait("tensor", s_s, v_ev)
                    for kt in range(KT):
                        for nch in range(2):
                            sc.op("tensor", lambda e, kt=kt, sub=sub, nch=nch, vps=vps, xnT3=xnT3: e.matmul(
                                vps[:, nch * 512:nch * 512 + 512],
                                xnT3[:, kt, sub * 128:sub * 128 + 128],
                                wqkv16[:, kt * 3 * D + 2 * D + nch * 512:kt * 3 * D + 2 * D + nch * 512 + 512],
                                start=(kt == 0), stop=(kt == KT - 1)),
                                inc=(s_pe if kt == KT - 1 and nch == 1 else None))
                    sc.wait("scalar", s_pe)
                    sc.op("scalar", lambda e, sub=sub, par=par, vps=vps: e.copy(
                        vsb[par][:, sub * D:sub * D + D], vps[:, :]), inc=s_s)
                    v_ev = s_s.n
                    if sub % 2 == 1:
                        emit_w_quants()
                if st == NST - 1:
                    sc.wait("tensor", s_s, v_ev)
                    for m in range(8):
                        for kt in range(KT):
                            sc.op("tensor", lambda e, kt=kt, m=m, hps=hps, xnT3=xnT3: e.matmul(
                                hps[:, m * 2:m * 2 + 2],
                                wqkv16[:, kt * 3 * D + 2 * D + m * 128:kt * 3 * D + 2 * D + m * 128 + 128],
                                xnT3[:, kt, 512:514], start=(kt == 0), stop=(kt == KT - 1)),
                                inc=(s_pe if kt == KT - 1 and m == 7 else None))
                    sc.wait("scalar", s_pe)
                    sc.op("scalar", lambda e, hps=hps: e.copy(vh16[:], hps[:]), inc=s_s)
                    sc.wait("tensor", s_s)
                    for m in range(8):
                        sc.op("tensor", lambda e, m=m, trA=trA: e.transpose(
                            trA[0:2, m * 128:(m + 1) * 128],
                            vh16[:, m * 2:m * 2 + 2], ident[:]),
                            inc=(s_pe if m == 7 else None))
                    sc.wait("scalar", s_pe)
                    sc.op("scalar", lambda e, trA=trA: e.copy(
                        vhalo[:], trA[0:2, 0:D]), inc=s_s)

        def attn(st):
            if alevel < 3:
                return
            par = st % 2
            qk3 = qkT[par][:].rearrange("p (m q) -> p m q", q=QW)
            qk3n = qkT[1 - par][:].rearrange("p (m q) -> p m q", q=QW)
            scps = [P0[:, 0:390], P1[:, 0:390]]
            sc.wait("tensor", s_s)
            sc.wait("tensor", s_v)
            qk3o = qkT[par][:].rearrange("p (m q) -> p m q", q=QW)
            qk3no = qkT[1 - par][:].rearrange("p (m q) -> p m q", q=QW)
            ep_guard = [0]
            pt_guard = [0]
            ptm_ev = {}
            sout_free = [0]
            sc.wait("sync", s_s)
            odd_guard = [0]

            def emit_tail(sub):
                t = 4 * st + sub
                otp = P2
                sc.wait("tensor", s_s, ptm_ev[sub])
                sc.wait("tensor", s_s, sout_free[0])
                for hh in range(H):
                    r0 = 64 * (hh % 2)
                    mco = (hh // 2) * 128
                    vmain = vsb[par][:, sub * D + hh * 64:sub * D + hh * 64 + 64]
                    if sub < 3:
                        vnext = vsb[par][0:2, (sub + 1) * D + hh * 64:(sub + 1) * D + hh * 64 + 64]
                    elif st < NST - 1:
                        vnext = vsb[1 - par][0:2, hh * 64:hh * 64 + 64]
                    else:
                        vnext = vhalo[0:2, hh * 64:hh * 64 + 64]
                    sc.op("tensor", lambda e, vmain=vmain, hh=hh, r0=r0, mco=mco, otp=otp: e.matmul(
                        otp[r0:r0 + 64, mco:mco + 128], vmain,
                        PTm[:, hh * 128:hh * 128 + 128], start=True, stop=False))
                    sc.op("tensor", lambda e, vnext=vnext, hh=hh, r0=r0, mco=mco, otp=otp: e.matmul(
                        otp[r0:r0 + 64, mco:mco + 128], vnext,
                        PTt[0:2, hh * 128:hh * 128 + 128], start=False, stop=True),
                        inc=(s_pe if hh == H - 1 else None))
                sc.wait("scalar", s_pe)
                sc.op("scalar", lambda e, otp=otp: e.copy(outT16[:], otp[:]), inc=s_s)
                sps = P2
                sc.wait("tensor", s_s)
                sc.wait("tensor", s_gld, ngld[0])
                for kt in range(KT):
                    for nch in range(2):
                        sc.op("tensor", lambda e, kt=kt, nch=nch, sps=sps: e.matmul(
                            sps[:, nch * 512:nch * 512 + 512],
                            outT16[:, kt * 128:kt * 128 + 128],
                            wproj16[:, kt * D + nch * 512:kt * D + nch * 512 + 512],
                            start=(kt == 0), stop=False))
                for nch in range(2):
                    sc.op("tensor", lambda e, nch=nch, sps=sps: e.matmul(
                        sps[:, nch * 512:nch * 512 + 512], onesr[:],
                        bproj16[:, nch * 512:nch * 512 + 512],
                        start=False, stop=True),
                        inc=(s_pe if nch == 1 else None))
                sc.wait("scalar", s_pe)
                sc.wait("scalar", s_sst, sst_guard[0])
                sc.op("scalar", lambda e, sps=sps: e.copy(sOut[:], sps[:]), inc=s_s)
                sout_free[0] = s_s.n
                sc.wait("sync", s_s)
                sc.dma("sync", d_s[t, :, :], sOut[:], inc=s_sst)
                sst_guard[0] = s_sst.n

            for sub in range(4):
                t = 4 * st + sub
                if odd_guard[0] > 0:
                    sc.wait("sync", s_pe, odd_guard[0])
                sc.dma("sync", qodd[:].rearrange("p (m q) -> p m q", q=128),
                       qk3o[64:128, 0:8, sub * 128:sub * 128 + 128], inc=SH)
                kodd3 = kodd[:].rearrange("p (m q) -> p m q", q=132)
                if sub < 3 or st == NST - 1:
                    sc.dma("sync", kodd3[:, :, 0:130],
                           qk3o[64:128, 8:16, sub * 128:sub * 128 + 130], inc=SH)
                else:
                    sc.dma("sync", kodd3[:, :, 0:128],
                           qk3o[64:128, 8:16, sub * 128:sub * 128 + 128], inc=SH)
                    sc.dma("sync", kodd3[:, :, 128:130],
                           qk3no[64:128, 8:16, 0:2], inc=SH)
                sh_n = SH.n
                groups = [(0, 3), (3, 6), (6, 9), (9, 12), (12, 14), (14, 16)]
                sc.wait("tensor", SH, sh_n)
                for gi, (h0, h1) in enumerate(groups):
                    g = gi % 2
                    if sc_ev[g] > 0:
                        sc.wait("tensor", s_s, sc_ev[g])
                    for hh in range(h0, h1):
                        odd = hh % 2
                        mq = hh // 2
                        co = (hh - h0) * 130
                        if odd:
                            lhs = qodd[:].rearrange("p (m q) -> p m q", q=128)[:, mq, :]
                            rhs = kodd[:].rearrange("p (m q) -> p m q", q=132)[:, mq, 0:130]
                            sc.op("tensor", lambda e, lhs=lhs, rhs=rhs, g=g, co=co, scps=scps: e.matmul(
                                scps[g][:, co:co + 130], lhs, rhs, start=True, stop=True),
                                inc=(s_pe if hh == h1 - 1 else None))
                        else:
                            mk = 8 + hh // 2
                            lhs = qk3[0:64, mq, sub * 128:sub * 128 + 128]
                            if sub < 3 or st == NST - 1:
                                rhs = qk3[0:64, mk, sub * 128:sub * 128 + 130]
                                sc.op("tensor", lambda e, lhs=lhs, rhs=rhs, g=g, co=co, scps=scps: e.matmul(
                                    scps[g][:, co:co + 130], lhs, rhs, start=True, stop=True),
                                    inc=(s_pe if hh == h1 - 1 else None))
                            else:
                                rhs = qk3[0:64, mk, sub * 128:sub * 128 + 128]
                                rhs2 = qk3n[0:64, mk, 0:2]
                                sc.op("tensor", lambda e, lhs=lhs, rhs=rhs, g=g, co=co, scps=scps: e.matmul(
                                    scps[g][:, co:co + 128], lhs, rhs, start=True, stop=True))
                                sc.op("tensor", lambda e, lhs=lhs, rhs2=rhs2, g=g, co=co, scps=scps: e.matmul(
                                    scps[g][:, co + 128:co + 130], lhs, rhs2, start=True, stop=True),
                                    inc=(s_pe if hh == h1 - 1 else None))
                    sc.wait("scalar", s_pe)
                    if gi == 0:
                        sc.wait("scalar", s_v, ep_guard[0])
                        sc.wait("scalar", s_pe, pt_guard[0])
                    sc.op("scalar", lambda e, h0=h0, h1=h1, g=g, scps=scps: e.activation(
                        e16[:, h0 * 130:h1 * 130], scps[g][:, :(h1 - h0) * 130],
                        AF.Exp, scale=cvec[:, 2:3]), inc=s_s)
                    sc_ev[g] = s_s.n
                exp_ev = s_s.n
                odd_guard[0] = s_pe.n
                emit_w_stores()
                if alevel < 4:
                    continue
                if sub > 0 and alevel >= 6:
                    emit_tail(sub - 1)
                mask = maskl if t == NT - 1 else maskm
                e3 = e16[:].rearrange("p (h w) -> p h w", w=130)
                eb3 = e3
                p3 = e3
                mb = mask[:].unsqueeze(1).broadcast_to([128, H, 130])
                sc.wait("vector", s_s, exp_ev)
                sc.wait("vector", s_pe, pt_guard[0])
                sc.op("vector", lambda e, e3=e3, eb3=eb3, mb=mb: e.tensor_tensor(
                    eb3, e3, mb, ALU.mult))
                sc.op("vector", lambda e, eb3=eb3: e.tensor_reduce(
                    rs32[:], eb3, AX.X, ALU.add), inc=s_v)
                sc.wait("vector", s_v)
                sc.op("vector", lambda e: e.reciprocal(rsi32[:], rs32[:]), inc=s_v)
                sc.wait("vector", s_v)
                sc.op("vector", lambda e: e.tensor_copy(rinv16[:], rsi32[:]), inc=s_v)
                sc.wait("vector", s_v)
                rb = rinv16[:].unsqueeze(2).broadcast_to([128, H, 130])
                sc.op("vector", lambda e, p3=p3, eb3=eb3, rb=rb: e.tensor_tensor(
                    p3, eb3, rb, ALU.mult), inc=s_v)
                ep_guard[0] = s_v.n
                if dbg & 16:
                    sc.wait("sync", s_v)
                    sc.dma("sync", d_e[t, :, :], e16[:], inc=s_dbg)
                    sc.dma("sync", d_p[t, :, :], e16[:], inc=s_dbg)
                    sc.dma("sync", d_rs[t, :, :], rs32[:], inc=s_dbg)
                    sc.dma("sync", d_eb[t, :, :], e16[:], inc=s_dbg)
                    sc.dma("sync", d_ri[t, :, :], rinv16[:], inc=s_dbg)
                    sc.wait("scalar", s_dbg)
                if alevel < 5:
                    continue
                if True:
                    ptm = P4; ptt = P3b
                    sc.wait("tensor", s_v)
                    sc.wait("tensor", s_s)
                    for hh in range(H):
                        sc.op("tensor", lambda e, hh=hh, ptm=ptm: e.transpose(
                            ptm[:, hh * 128:hh * 128 + 128],
                            e16[:, hh * 130:hh * 130 + 128], ident[:]))
                    for hh in range(H):
                        sc.op("tensor", lambda e, hh=hh, ptt=ptt: e.transpose(
                            ptt[0:2, hh * 128:hh * 128 + 128],
                            e16[:, hh * 130 + 128:hh * 130 + 130], ident[:]),
                            inc=(s_pe if hh == H - 1 else None))
                    pt_guard[0] = s_pe.n
                    sc.wait("scalar", s_pe)
                    sc.op("scalar", lambda e, ptm=ptm: e.copy(PTm[:], ptm[:]))
                    sc.op("scalar", lambda e, ptt=ptt: e.copy(PTt[:], ptt[0:2, :]), inc=s_s)
                    ptm_ev[sub] = s_s.n
                emit_w_quants()
            if alevel >= 6:
                emit_tail(3)

        for st in range(NST):
            stage1(st)
            if st > 0:
                attn(st - 1)
        attn(NST - 1)
        flush_w()
        n_w_store = s_st.n

    nA_st = s_st.n
    nA_xq = s_xqst.n
    nA_ss = s_sst.n
    nA_v = s_v.n
    nA_s = s_s.n
    nA_pe = s_pe.n
    if phases == "WA":
        sc.wait("sync", s_st, nA_st)
        sc.wait("sync", s_xqst, nA_xq)
        sc.wait("sync", s_sst, nA_ss)
        sc.wait("sync", s_v, nA_v)
        sc.wait("sync", s_s, nA_s)
        sc.wait("sync", s_pe, nA_pe)
        sc.run()
        return nc

    with contextlib.ExitStack() as _esB:
        def sbb(name, shape, dt):
            return _esB.enter_context(nc.sbuf_tensor(name, shape, dt)).ap()
        wgT = sbb("wgT", [128, KT * I_FF], BF16)
        wuT = sbb("wuT", [128, KT * I_FF], BF16)
        wdT = sbb("wdT", [128, IKB * D], BF16)
        xqTt = sbb("xqTt", [128, 2 * D], BF16)
        sT = sbb("sT", [128, 2 * D], BF16)
        xT = sbb("xT", [128, 2 * D], F32)
        sgb = sbb("sgb", [128, 1024], F32)
        hb = sbb("hb", [128, 1024], F32)
        hq16 = sbb("hq16", [128, I_FF], BF16)
        hqT = sbb("hqT", [128, IKB * 128], BF16)
        gb = sbb("gb", [128, D], F32)
        tb = sbb("tbb", [128, D], F32)
        ob = sbb("ob", [128, 2 * D], F32)

        for qq in ("sync", "scalar", "gpsimd"):
            sc.wait(qq, s_st, nA_st)
            sc.wait(qq, s_xqst, nA_xq)
            sc.wait(qq, s_sst, nA_ss)
            sc.wait(qq, s_v, nA_v)
            sc.wait(qq, s_s, nA_s)
            sc.wait(qq, s_pe, nA_pe)

        BL = [Sem(nc, "s_bl0"), Sem(nc, "s_bl1")]
        bl_idx = {}
        pe_tile_end = {}
        v_tile_end = {}
        hq_guard = {}

        def bload(t):
            par = t % 2
            if t - 2 in pe_tile_end:
                sc.wait("gpsimd", s_pe, pe_tile_end[t - 2])
                sc.wait("gpsimd", s_v, v_tile_end[t - 2])
            sc.dma("gpsimd", xqTt[:, par * D:par * D + D], d_xqT[t, :, :], inc=BL[par])
            sc.dma("gpsimd", sT[:, par * D:par * D + D], d_s[t, :, :], inc=BL[par])
            sc.dma("gpsimd", xT[:, par * D:par * D + D],
                   xs[t * 128:(t + 1) * 128, :], inc=BL[par])
            bl_idx[t] = BL[par].n

        s_wg = [Sem(nc, "s_wg%d" % i) for i in range(6)]
        s_wu = [Sem(nc, "s_wu%d" % i) for i in range(6)]
        s_wd = [Sem(nc, "s_wd%d" % i) for i in range(3)]
        for ci, (c0, c1) in enumerate(IC_CH):
            for kt in range(KT):
                sc.dma("sync", wgT[:, kt * I_FF + c0:kt * I_FF + c1],
                       d_wgT[kt, :, c0:c1], inc=s_wg[ci])
                sc.dma("scalar", wuT[:, kt * I_FF + c0:kt * I_FF + c1],
                       d_wuT[kt, :, c0:c1], inc=s_wu[ci])
        bload(0)
        for g in range(3):
            for kt in range(8 * g, min(IKB, 8 * g + 8)):
                sc.dma("gpsimd", wdT[:, kt * D:(kt + 1) * D], d_wdT[kt, :, :], inc=s_wd[g])
            if g == 0:
                bload(1)

        if True:
            gpsA = [P0, P3[:, 0:512]]
            upsA = [P1, P3[:, 512:1024]]
            dps = P2
            trB = P4[:, 0:1024]

            h_ev = {}
            sg_ev = {}
            q_ev = {}
            hqT_ev = {}
            hqT_ev_roll = {}
            gb_ev = {}
            pending = {}

            def tr_pair(t, pr, p0, pw):
                sc.wait("tensor", s_v, q_ev[(t, pr)])
                nblk = pw // 128
                gpr = t * 3 + pr
                if gpr >= 1:
                    sc.wait("tensor", s_s, hqT_ev_roll[gpr - 1])
                for b in range(nblk):
                    kb = p0 // 128 + b
                    sc.op("tensor", lambda e, b=b, kb=kb: e.transpose(
                        trB[:, b * 128:(b + 1) * 128],
                        hq16[:, kb * 128:(kb + 1) * 128], ident[:]),
                        inc=(s_pe if b == nblk - 1 else None))
                hq_guard[pr] = s_pe.n
                sc.wait("scalar", s_pe, hq_guard[pr])
                sc.op("scalar", lambda e, p0=p0, nblk=nblk: e.copy(
                    hqT[:, p0:p0 + nblk * 128], trB[:, :nblk * 128]), inc=s_s)
                hqT_ev[(t, pr)] = s_s.n
                hqT_ev_roll[gpr] = s_s.n

            def emit_down_pair(t, pr):
                c0 = IC_CH[pr * 2][0]
                c1 = IC_CH[pr * 2 + 1][1]
                if t <= 1:
                    g = min(pr, 2)
                    sc.wait("tensor", s_wd[g], s_wd[g].n)
                if pr == 0 and (t - 1) in gb_ev:
                    sc.wait("tensor", s_s, gb_ev[t - 1])
                sc.wait("tensor", s_s, hqT_ev[(t, pr)])
                for kt in range(c0 // 128, c1 // 128):
                    for nch in range(2):
                        sc.op("tensor", lambda e, kt=kt, nch=nch: e.matmul(
                            dps[:, nch * 512:nch * 512 + 512],
                            hqT[:, kt * 128:kt * 128 + 128],
                            wdT[:, kt * D + nch * 512:kt * D + nch * 512 + 512],
                            start=(kt == 0), stop=(kt == IKB - 1)),
                            inc=(s_pe if kt == c1 // 128 - 1 and nch == 1 else None))

            def tile_epilogue(t):
                par = t % 2
                s_t = sT[:, par * D:par * D + D]
                x_t = xT[:, par * D:par * D + D]
                pe_tile_end[t] = s_pe.n
                sc.wait("scalar", s_pe)
                sc.wait("scalar", s_v)
                sc.op("scalar", lambda e: e.activation(gb[:], dps[:], AF.Silu), inc=s_s)
                gb_ev[t] = s_s.n
                sc.wait("vector", s_s, gb_ev[t])
                if t >= 2:
                    sc.wait("vector", OUT[par], OUT[par].n)
                sc.op("vector", lambda e, s_t=s_t: e.tensor_tensor(
                    tb[:], s_t, gb[:], ALU.mult))
                sc.op("vector", lambda e, x_t=x_t, par=par: e.tensor_tensor(
                    ob[:, par * D:par * D + D], x_t, tb[:], ALU.add), inc=s_v)
                v_tile_end[t] = s_v.n
                sc.wait("sync", s_v)
                sc.dma("sync", out_d[t * 128:(t + 1) * 128, :],
                       ob[:, par * D:par * D + D], inc=OUT[par])
                if t + 2 < NT:
                    bload(t + 2)

            NG = NT * 6
            for gci in range(NG):
                t, ci = divmod(gci, 6)
                c0, c1 = IC_CH[ci]
                cw = c1 - c0
                par = t % 2
                xq_t = xqTt[:, par * D:par * D + D]
                gps = gpsA[ci % 2]
                ups = upsA[ci % 2]
                if ci == 0:
                    sc.wait("tensor", BL[par], bl_idx[t])
                if gci >= 2:
                    sc.wait("tensor", s_s, sg_ev[gci - 2])
                    sc.wait("tensor", s_v, h_ev[gci - 2])
                if t <= 1:
                    sc.wait("tensor", s_wg[ci], s_wg[ci].n)
                    sc.wait("tensor", s_wu[ci], s_wu[ci].n)
                for kt in range(KT):
                    sc.op("tensor", lambda e, kt=kt, c0=c0, cw=cw, xq_t=xq_t, gps=gps: e.matmul(
                        gps[:, :cw], xq_t[:, kt * 128:kt * 128 + 128],
                        wgT[:, kt * I_FF + c0:kt * I_FF + c0 + cw],
                        start=(kt == 0), stop=(kt == KT - 1)))
                for kt in range(KT):
                    sc.op("tensor", lambda e, kt=kt, c0=c0, cw=cw, xq_t=xq_t, ups=ups: e.matmul(
                        ups[:, :cw], xq_t[:, kt * 128:kt * 128 + 128],
                        wuT[:, kt * I_FF + c0:kt * I_FF + c0 + cw],
                        start=(kt == 0), stop=(kt == KT - 1)),
                        inc=(s_pe if kt == KT - 1 else None))
                gu_end = s_pe.n
                for fn in pending.pop(gci, []):
                    fn()
                sc.wait("scalar", s_pe, gu_end)
                sc.wait("scalar", s_v, h_ev.get(gci - 2, 0))
                sc.op("scalar", lambda e, cw=cw, gps=gps, ci=ci: e.activation(
                    sgb[:, (ci % 2) * 512:(ci % 2) * 512 + cw], gps[:, :cw], AF.Silu), inc=s_s)
                sg_ev[gci] = s_s.n
                sc.wait("vector", s_s, sg_ev[gci])
                sc.op("vector", lambda e, cw=cw, ups=ups, ci=ci: e.tensor_tensor(
                    hb[:, (ci % 2) * 512:(ci % 2) * 512 + cw],
                    sgb[:, (ci % 2) * 512:(ci % 2) * 512 + cw],
                    ups[:, :cw], ALU.mult), inc=s_v)
                h_ev[gci] = s_v.n
                if ci % 2 == 1:
                    pr = ci // 2
                    p0 = IC_CH[ci - 1][0]
                    pw = c1 - p0
                    quant(hb[:, :pw], hq16[:, p0:c1], pw, pe_guard=hq_guard.get(pr, 0))
                    q_ev[(t, pr)] = s_v.n

                    def deferred(t=t, pr=pr, p0=p0, pw=pw):
                        tr_pair(t, pr, p0, pw)
                        emit_down_pair(t, pr)
                        if pr == 2:
                            tile_epilogue(t)
                    pending.setdefault(gci + 2, []).append(deferred)
            for gci in range(NG, NG + 2):
                for fn in pending.pop(gci, []):
                    fn()

    sc.wait("sync", OUT[0], OUT[0].n)
    sc.wait("sync", OUT[1], OUT[1].n)
    sc.run()
    return nc


_prog_cache = {}


def kernel(L0_attn_x, L0_attn_w_qkv, L0_attn_w_proj, L0_attn_b_proj,
           L0_ffn_w_gate, L0_ffn_w_up, L0_ffn_w_down):
    x = np.ascontiguousarray(np.asarray(L0_attn_x, dtype=np.float32))
    B, T, _ = x.shape
    n_cores = 8
    per_b = n_cores // B

    if "nc" not in _prog_cache:
        _prog_cache["nc"] = build_program()
    nc = _prog_cache["nc"]

    i_idx = np.arange(128)[:, None]
    j_idx = np.arange(130)[None, :]
    band = (j_idx >= i_idx) & (j_idx <= i_idx + 2)
    maskmain = band.astype(ml_dtypes.bfloat16)
    masklast_edge = (band & (j_idx < 128)).astype(ml_dtypes.bfloat16)
    ident = np.eye(128, dtype=ml_dtypes.bfloat16)
    onesrow = np.ones((1, 128), dtype=ml_dtypes.bfloat16)
    cvec = np.tile(np.array([[1e-5, -0.5, 0.125, -1.0]], dtype=np.float32), (128, 1))

    wqkv = np.ascontiguousarray(np.asarray(L0_attn_w_qkv, dtype=np.float32))
    wproj = np.ascontiguousarray(np.asarray(L0_attn_w_proj, dtype=np.float32))
    bproj = np.ascontiguousarray(np.asarray(L0_attn_b_proj, dtype=np.float32)).reshape(1, -1)
    wg = np.ascontiguousarray(np.asarray(L0_ffn_w_gate, dtype=np.float32))
    wu = np.ascontiguousarray(np.asarray(L0_ffn_w_up, dtype=np.float32))
    wd = np.ascontiguousarray(np.asarray(L0_ffn_w_down, dtype=np.float32))

    in_maps = []
    for c in range(n_cores):
        b = c // per_b
        r0 = (c % per_b) * TOK
        xsc = np.zeros((TOK + 2, D), dtype=np.float32)
        end = min(r0 + TOK + 2, T)
        xsc[:end - r0] = x[b, r0:end]
        last = (c % per_b) == per_b - 1
        in_maps.append(dict(
            xs=xsc, w_qkv=wqkv, w_proj=wproj, b_proj=bproj,
            wg=wg, wu=wu, wd=wd, ident=ident,
            maskmain=maskmain, masklast=(masklast_edge if last else maskmain),
            onesrow=onesrow, cvec=cvec,
        ))

    res = run_bass_kernel_spmd(nc, in_maps, core_ids=list(range(n_cores)))
    out = np.empty((B, T, D), dtype=np.float32)
    for c in range(n_cores):
        b = c // per_b
        r0 = (c % per_b) * TOK
        out[b, r0:r0 + TOK] = np.asarray(res.results[c]["out"], dtype=np.float32)
    return out


# version 17
# speedup vs baseline: 1.1016x; 1.0913x over previous
import contextlib
import numpy as np
import ml_dtypes
import concourse.bass as bass
import concourse.mybir as mybir
from concourse.bass_utils import run_bass_kernel_spmd

F32 = mybir.dt.float32
BF16 = mybir.dt.bfloat16
AF = mybir.ActivationFunctionType
ALU = mybir.AluOpType
AX = mybir.AxisListType

M1 = 12582912.0
M2 = 25165824.0

D = 1024
KT = 8
H = 16
TOK = 2048
NT = 16
NST = 4
I_FF = 2816
IKB = 22
QW = 520
IC_CH = [(0, 512), (512, 1024), (1024, 1536), (1536, 2048), (2048, 2560), (2560, 2816)]

ENGS = ("sync", "vector", "scalar", "tensor", "gpsimd")


class Sem:
    def __init__(self, nc, name):
        self.h = nc.alloc_semaphore(name)
        self.n = 0


class Sched:

    def __init__(self, nc):
        self.nc = nc
        self.q = {e: [] for e in ENGS}

    def wait(self, eng, sem, val=None):
        v = sem.n if val is None else val
        if v > 0:
            self.q[eng].append(("w", sem.h, v))

    def op(self, eng, fn, inc=None, k=1):
        incs = []
        if inc is not None:
            for s in (inc if isinstance(inc, (list, tuple)) else [inc]):
                s.n += k
                incs.append((s.h, k))
        self.q[eng].append(("o", fn, incs))

    def dma(self, eng, out_ap, in_ap, inc=None):
        self.op(eng, lambda e, o=out_ap, i=in_ap: e.dma_start(o, i), inc=inc, k=16)

    def run(self):
        with self.nc.Block() as blk:
            def emit(e, items):
                for it in items:
                    if it[0] == "w":
                        e.wait_ge(it[1], it[2])
                    else:
                        instr = it[1](e)
                        for (h, k) in it[2]:
                            instr = instr.then_inc(h, k)
            blk.sync(lambda e: emit(e, self.q["sync"]))
            blk.vector(lambda e: emit(e, self.q["vector"]))
            blk.scalar(lambda e: emit(e, self.q["scalar"]))
            blk.tensor(lambda e: emit(e, self.q["tensor"]))
            blk.gpsimd(lambda e: emit(e, self.q["gpsimd"]))


def build_program(phases="WAB", alevel=9, dbg=0):
    assert phases in ("WA", "WAB")
    nc = bass.Bass("TRN2", target_bir_lowering=False)
    sc = Sched(nc)

    def din(name, shape, dt):
        return nc.dram_tensor(name, shape, dt, kind="ExternalInput").ap()

    xs = din("xs", [TOK + 2, D], F32)
    w_qkv = din("w_qkv", [D, 3 * D], F32)
    w_proj = din("w_proj", [D, D], F32)
    b_proj = din("b_proj", [1, D], F32)
    wg_d = din("wg", [I_FF, D], F32)
    wu_d = din("wu", [I_FF, D], F32)
    wd_d = din("wd", [D, I_FF], F32)
    ident_in = din("ident", [128, 128], BF16)
    maskmain_in = din("maskmain", [128, 130], BF16)
    masklast_in = din("masklast", [128, 130], BF16)
    onesrow_in = din("onesrow", [1, 128], BF16)
    cvec_in = din("cvec", [128, 4], F32)

    out_d = nc.dram_tensor("out", [TOK, D], F32, kind="ExternalOutput").ap()

    _stage_kind = "ExternalOutput" if (dbg & 8) else "Internal"
    d_s = nc.dram_tensor("d_s", [NT, 128, D], BF16, kind=_stage_kind).ap()
    d_xqT = nc.dram_tensor("d_xqT", [NT, 128, D], BF16, kind=_stage_kind).ap()
    if dbg & 16:
        d_e = nc.dram_tensor("d_e", [NT, 128, H * 130], BF16, kind="ExternalOutput").ap()
        d_p = nc.dram_tensor("d_p", [NT, 128, H * 130], BF16, kind="ExternalOutput").ap()
        d_rs = nc.dram_tensor("d_rs", [NT, 128, H], F32, kind="ExternalOutput").ap()
        d_eb = nc.dram_tensor("d_eb", [NT, 128, H * 130], BF16, kind="ExternalOutput").ap()
        d_ri = nc.dram_tensor("d_ri", [NT, 128, H], BF16, kind="ExternalOutput").ap()
        d_xn = nc.dram_tensor("d_xn", [NT, 128, D], F32, kind="ExternalOutput").ap()
        d_xt = nc.dram_tensor("d_xt", [NT, 128, D], F32, kind="ExternalOutput").ap()
    d_wgT = nc.dram_tensor("d_wgT", [KT, 128, IKB * 128], BF16, kind="Internal").ap()
    d_wuT = nc.dram_tensor("d_wuT", [KT, 128, IKB * 128], BF16, kind="Internal").ap()
    d_wdT = nc.dram_tensor("d_wdT", [IKB, 128, KT * 128], BF16, kind="Internal").ap()

    def sb(name, shape, dt):
        return nc.alloc_sbuf_tensor(name, shape, dt).ap()

    ident = sb("ident_s", [128, 128], BF16)
    maskm = sb("maskm_s", [128, 130], BF16)
    maskl = sb("maskl_s", [128, 130], BF16)
    onesr = sb("onesr_s", [1, 128], BF16)
    cvec = sb("cvec_s", [128, 4], F32)
    bproj16 = sb("bproj16", [1, D], BF16)

    QWID = 1024
    wbuf = sb("q_w", [128, QWID], F32)
    t1 = sb("q_t1", [128, QWID], BF16)
    t2 = sb("q_t2", [128, QWID], BF16)
    t3 = sb("q_t3", [128, QWID], BF16)
    t4 = sb("q_t4", [128, QWID], BF16)
    NBMAX = QWID // 16
    bmb = sb("q_bm", [128, NBMAX], F32)
    scb = sb("q_sc", [128, NBMAX], F32)
    lnb = sb("q_ln", [128, NBMAX], F32)
    rinvb = sb("q_rinv", [128, NBMAX], F32)
    rinv2b = sb("q_rinv2", [128, NBMAX], F32)
    qscb = sb("q_qsc", [128, NBMAX], BF16)

    P0 = nc.alloc_psum_tensor("P0", [128, 512], F32).ap()
    P1 = nc.alloc_psum_tensor("P1", [128, 512], F32).ap()
    P2 = nc.alloc_psum_tensor("P2", [128, 1024], F32).ap()
    P3 = nc.alloc_psum_tensor("P3", [128, 1024], F32).ap()
    P4 = nc.alloc_psum_tensor("P4", [128, 2048], BF16).ap()
    P3b = P3.bitcast(BF16)

    s_ld = Sem(nc, "s_ld")
    s_gld = Sem(nc, "s_gld")
    s_st = Sem(nc, "s_st")
    s_xqst = Sem(nc, "s_xqst")
    s_sst = Sem(nc, "s_sst")
    s_dbg = Sem(nc, "s_dbg")
    OUT = [Sem(nc, "s_out0"), Sem(nc, "s_out1")]
    s_pe = Sem(nc, "s_pe")
    s_v = Sem(nc, "s_v")
    s_s = Sem(nc, "s_s")

    sc.dma("sync", ident[:], ident_in[:], inc=s_ld)
    sc.dma("sync", maskm[:], maskmain_in[:], inc=s_ld)
    sc.dma("sync", maskl[:], masklast_in[:], inc=s_ld)
    sc.dma("sync", onesr[:], onesrow_in[:], inc=s_ld)
    sc.dma("sync", cvec[:], cvec_in[:], inc=s_ld)
    N_PROLOG = 5
    for _e in ("vector", "scalar", "tensor", "sync"):
        sc.wait(_e, s_ld, N_PROLOG * 16)
    sc.dma("gpsimd", bproj16[:], b_proj[:], inc=s_gld)

    def quant(src_ap, dst_ap, W, pe_guard=0):
        nb = W // 16
        src3 = src_ap.rearrange("p (nb b) -> p nb b", b=16)
        v, s = "vector", "scalar"
        sc.op(v, lambda e, src3=src3, nb=nb: e.tensor_reduce(
            bmb[:, :nb], src3, AX.X, ALU.max, apply_absolute_value=True), inc=s_v)
        sc.wait(v, s_v)
        sc.op(v, lambda e, nb=nb: e.tensor_scalar(
            scb[:, :nb], bmb[:, :nb], 1.0 / 6.0, 1e-12, ALU.mult, ALU.max), inc=s_v)
        sc.wait(v, s_v)
        sc.op(v, lambda e, nb=nb: e.reciprocal(rinvb[:, :nb], scb[:, :nb]), inc=s_v)
        sc.wait(v, s_v)
        sc.op(v, lambda e, nb=nb: e.tensor_scalar(
            rinv2b[:, :nb], rinvb[:, :nb], 2.0, None, ALU.mult))
        sc.op(v, lambda e, nb=nb: e.tensor_scalar(
            qscb[:, :nb], scb[:, :nb], 0.5, None, ALU.mult))
        r2bc = rinv2b[:, :nb].unsqueeze(2).broadcast_to([128, nb, 16])
        qsbc = qscb[:, :nb].unsqueeze(2).broadcast_to([128, nb, 16])
        wv = wbuf[:, :W]
        wv3 = wv.rearrange("p (nb b) -> p nb b", b=16)
        A_, B_, C_, D_ = (t1[:, :W], t2[:, :W], t3[:, :W], t4[:, :W])
        sc.op(v, lambda e, wv3=wv3, src3=src3, r2bc=r2bc: e.tensor_tensor(
            wv3, src3, r2bc, ALU.mult))
        sc.op(v, lambda e, A_=A_, wv=wv: e.tensor_scalar(A_, wv, M1, -M1, ALU.add, ALU.add))
        sc.op(v, lambda e, A_=A_, B_=B_: e.tensor_scalar(B_, A_, -4.0, 4.0, ALU.max, ALU.min))
        sc.op(v, lambda e, C_=C_, wv=wv: e.tensor_scalar(C_, wv, M2, -M2, ALU.add, ALU.add))
        sc.op(v, lambda e, C_=C_, D_=D_: e.tensor_scalar(D_, C_, -4.0, 4.0, ALU.max, ALU.min))
        sc.op(v, lambda e, A_=A_, C_=C_, D_=D_: e.tensor_tensor(A_, C_, D_, ALU.subtract))
        sc.op(v, lambda e, A_=A_, C_=C_: e.tensor_scalar(C_, A_, -4.0, 4.0, ALU.max, ALU.min))
        sc.op(v, lambda e, D_=D_, wv=wv: e.tensor_scalar(D_, wv, 10.0, 4.0, ALU.is_ge, ALU.mult))
        sc.op(v, lambda e, A_=A_, wv=wv: e.tensor_scalar(A_, wv, -10.0, 4.0, ALU.is_le, ALU.mult))
        sc.op(v, lambda e, A_=A_, D_=D_: e.tensor_tensor(D_, D_, A_, ALU.subtract))
        sc.op(v, lambda e, B_=B_, C_=C_: e.tensor_tensor(B_, B_, C_, ALU.add))
        sc.op(v, lambda e, B_=B_, D_=D_: e.tensor_tensor(B_, B_, D_, ALU.add))
        dst3 = dst_ap.rearrange("p (nb b) -> p nb b", b=16)
        B3 = B_.rearrange("p (nb b) -> p nb b", b=16)
        if pe_guard:
            sc.wait(v, s_pe, pe_guard)
        sc.op(v, lambda e, dst3=dst3, B3=B3, qsbc=qsbc: e.tensor_tensor(
            dst3, B3, qsbc, ALU.mult), inc=s_v)


    XL = [Sem(nc, "s_xl0"), Sem(nc, "s_xl1")]
    XH = Sem(nc, "s_xh")
    SH = Sem(nc, "s_sh")

    with contextlib.ExitStack() as _esA:
        def sba(name, shape, dt):
            return _esA.enter_context(nc.sbuf_tensor(name, shape, dt)).ap()
        xtile = sba("xtile", [128, 2 * D], F32)
        xhalo = sba("xhalo", [2, D], F32)
        stats = [sba("ln_stats0", [128, 12], F32), sba("ln_stats1", [128, 12], F32)]
        aggr = [sba("ln_aggr0", [128, 2], F32), sba("ln_aggr1", [128, 2], F32)]
        lntmp = [sba("ln_tmp0", [128, 4], F32), sba("ln_tmp1", [128, 4], F32)]
        xnf = sba("xnf", [128, D], F32)
        xn16 = sba("xn16", [128, D], BF16)
        xq16 = sba("xq16", [128, D], BF16)
        xqTst = sba("xqTst", [128, D], BF16)
        hstat = sba("hstat", [2, 12], F32)
        haggr = sba("haggr", [2, 2], F32)
        htmp = sba("htmp", [2, 4], F32)
        xnh = sba("xnh", [2, D], F32)
        xnh16 = sba("xnh16", [2, D], BF16)
        xnT = sba("xnT", [128, KT * QW], BF16)
        qkT = [sba("qkT0", [128, 16 * QW], BF16), sba("qkT1", [128, 16 * QW], BF16)]
        vsb = [sba("vsb0", [128, 4 * D], BF16), sba("vsb1", [128, 4 * D], BF16)]
        vhalo = sba("vhalo", [2, D], BF16)
        vh16 = sba("vh16", [128, 16], BF16)
        e16 = sba("e16", [128, H * 130], BF16)
        rs32 = sba("rs32", [128, H], F32)
        rsi32 = sba("rsi32", [128, H], F32)
        rinv16 = sba("rinv16b", [128, H], BF16)
        qodd = sba("qodd", [64, 8 * 128], BF16)
        kodd = sba("kodd", [64, 8 * 132], BF16)
        PTm = sba("PTm", [128, H * 128], BF16)
        PTt = sba("PTt", [2, H * 128], BF16)
        outT16 = sba("outT16", [128, D], BF16)
        sOut = sba("sOut", [128, D], BF16)
        wqkv16 = sba("wqkv16", [128, KT * 3 * D], BF16)
        wproj16 = sba("wproj16", [128, KT * D], BF16)

        gld_kt = []
        for kt in range(KT):
            sc.dma("gpsimd", wqkv16[:, kt * 3 * D:(kt + 1) * 3 * D],
                   w_qkv[kt * 128:(kt + 1) * 128, :], inc=s_gld)
            gld_kt.append(s_gld.n)
        ngld = [0]

        wpce = [sba("wpce0", [128, 1024], F32), sba("wpce1", [128, 1024], F32)]
        wq16s = [sba("w_q16%d" % i, [128, 1024], BF16) for i in range(4)]
        wstage = [sba("w_stage0", [128, 1024], BF16), sba("w_stage1", [128, 1024], BF16)]
        wtr = P4[:, 1024:2048]
        WL = [Sem(nc, "s_wl0"), Sem(nc, "s_wl1")]
        WLg = [Sem(nc, "s_wlg0"), Sem(nc, "s_wlg1")]
        wpieces = []
        for (srcw, dstw, nch, k) in ((wg_d, d_wgT, IKB, D), (wu_d, d_wuT, IKB, D),
                                     (wd_d, d_wdT, KT, I_FF)):
            for c in range(nch):
                for p0 in range(0, k, 1024):
                    wpieces.append((srcw, dstw, c, p0, min(p0 + 1024, k)))
        NWP = len(wpieces)
        wp_ld = {}
        wq_end = {}
        wt_end = {}
        wst_tot = {}
        wcur = [0, 0, 0]

        def w_load(pi):
            srcw, dstw, c, p0, p1 = wpieces[pi]
            if pi < 6:
                q, sem = "sync", WL[pi % 2]
            else:
                q, sem = "gpsimd", WLg[pi % 2]
            if pi - 2 in wq_end:
                sc.wait(q, s_v, wq_end[pi - 2])
            sc.dma(q, wpce[pi % 2][:, :p1 - p0],
                   srcw[c * 128:(c + 1) * 128, p0:p1], inc=sem)
            wp_ld[pi] = (sem, sem.n)

        def w_quant(pi):
            while wcur[0] < min(pi + 2, NWP):
                w_load(wcur[0])
                wcur[0] += 1
            srcw, dstw, c, p0, p1 = wpieces[pi]
            lsem, lcnt = wp_ld[pi]
            sc.wait("vector", lsem, lcnt)
            assert pi < 4 or (pi - 4) in wt_end
            quant(wpce[pi % 2][:, :p1 - p0], wq16s[pi % 4][:, :p1 - p0], p1 - p0,
                  pe_guard=wt_end.get(pi - 4, 0))
            wq_end[pi] = s_v.n

        def w_store(pi):
            srcw, dstw, c, p0, p1 = wpieces[pi]
            nb = (p1 - p0) // 128
            sc.wait("tensor", s_v, wq_end[pi])
            sc.wait("tensor", s_s)
            for b in range(nb):
                sc.op("tensor", lambda e, b=b, pi=pi: e.transpose(
                    wtr[:, b * 128:(b + 1) * 128],
                    wq16s[pi % 4][:, b * 128:(b + 1) * 128], ident[:]),
                    inc=(s_pe if b == nb - 1 else None))
            wt_end[pi] = s_pe.n
            sc.wait("scalar", s_pe)
            if pi - 2 in wst_tot:
                sc.wait("scalar", s_st, wst_tot[pi - 2])
            sc.op("scalar", lambda e, nb=nb, pi=pi: e.copy(
                wstage[pi % 2][:, :nb * 128], wtr[:, :nb * 128]), inc=s_s)
            sc.wait("scalar", s_s)
            out_ap = dstw[p0 // 128:p0 // 128 + nb, :, c * 128:(c + 1) * 128].transpose([1, 0, 2])
            in_ap = wstage[pi % 2][:, :nb * 128].rearrange("p (kb q) -> p kb q", q=128)
            sc.dma("scalar", out_ap, in_ap, inc=s_st)
            wst_tot[pi] = s_st.n

        wtick = [0]

        def emit_w_stores():
            while wcur[2] < wcur[1] - 2:
                w_store(wcur[2])
                wcur[2] += 1

        def emit_w_quants():
            wtick[0] += 1
            qtarget = min(wtick[0] * NWP // 48 + 1, NWP)
            while wcur[1] < qtarget:
                pi = wcur[1]
                while wcur[2] <= pi - 4:
                    w_store(wcur[2])
                    wcur[2] += 1
                w_quant(pi)
                wcur[1] += 1

        def flush_w():
            wtick[0] = 32
            emit_w_quants()
            while wcur[2] < NWP:
                w_store(wcur[2])
                wcur[2] += 1

        xl_idx = {}
        xl_cnt = [0]
        xn_guard = {}
        trx_guard = [0]
        sc_ev = [0, 0]
        qk_ev = [0, 0]
        xqst_guard = [0]
        sst_guard = [0]

        def xload(t, guard):
            if guard:
                sc.wait("gpsimd", s_v, guard)
            sc.dma("gpsimd", xtile[:, (t % 2) * D:(t % 2) * D + D],
                   xs[t * 128:(t + 1) * 128, :], inc=XL[t % 2])
            xl_idx[t] = XL[t % 2].n

        def ln_chain(xap, n_p, statsb, aggrb, tmpb):
            v, s = "vector", "scalar"
            sc.op(v, lambda e, xap=xap, statsb=statsb, n_p=n_p: e.bn_stats(
                statsb[:n_p, 0:6], xap[:, 0:512]))
            sc.op(v, lambda e, xap=xap, statsb=statsb, n_p=n_p: e.bn_stats(
                statsb[:n_p, 6:12], xap[:, 512:1024]), inc=s_v)
            sc.wait(v, s_v)
            sc.op(v, lambda e, statsb=statsb, aggrb=aggrb, n_p=n_p: e.bn_aggr(
                aggrb[:n_p, :], statsb[:n_p, :]), inc=s_v)
            sc.wait(v, s_v)
            sc.op(v, lambda e, aggrb=aggrb, tmpb=tmpb, n_p=n_p: e.tensor_scalar(
                tmpb[:n_p, 0:1], aggrb[:n_p, 0:1], -1.0, None, ALU.mult), inc=s_v)
            sc.wait(s, s_v)
            sc.op(s, lambda e, aggrb=aggrb, tmpb=tmpb, n_p=n_p: e.activation(
                tmpb[:n_p, 1:2], aggrb[:n_p, 1:2], AF.Ln, bias=cvec[:n_p, 0:1]), inc=s_s)
            sc.wait(s, s_s)
            sc.op(s, lambda e, tmpb=tmpb, n_p=n_p: e.activation(
                tmpb[:n_p, 2:3], tmpb[:n_p, 1:2], AF.Exp, scale=cvec[:n_p, 1:2]),
                inc=s_s)

        rstd_ev = {}

        def ln_stats(t):
            par = t % 2
            sc.wait("vector", XL[par], xl_idx[t])
            ln_chain(xtile[:, par * D:par * D + D], 128, stats[par], aggr[par], lntmp[par])
            rstd_ev[t] = s_s.n

        def ln_apply(t, xout, x16out, pe_guard=0):
            v, s = "vector", "scalar"
            par = t % 2
            xap = xtile[:, par * D:par * D + D]
            tmpb = lntmp[par]
            sc.wait(v, s_s, rstd_ev[t])
            sc.op(v, lambda e, xap=xap, xout=xout, tmpb=tmpb: e.tensor_scalar(
                xout, xap, tmpb[:, 0:1], tmpb[:, 2:3], ALU.add, ALU.mult),
                inc=s_v)
            sc.wait(s, s_v)
            if pe_guard:
                sc.wait(s, s_pe, pe_guard)
            sc.op(s, lambda e, xout=xout, x16out=x16out: e.copy(x16out, xout), inc=s_s)

        def ln_tile(xap, n_p, statsb, aggrb, tmpb, xout, x16out, pe_guard=0):
            v, s = "vector", "scalar"
            ln_chain(xap, n_p, statsb, aggrb, tmpb)
            sc.wait(v, s_s)
            sc.op(v, lambda e, xap=xap, xout=xout, tmpb=tmpb, n_p=n_p: e.tensor_scalar(
                xout, xap, tmpb[:n_p, 0:1], tmpb[:n_p, 2:3], ALU.add, ALU.mult),
                inc=s_v)
            sc.wait(s, s_v)
            if pe_guard:
                sc.wait(s, s_pe, pe_guard)
            sc.op(s, lambda e, xout=xout, x16out=x16out: e.copy(x16out, xout), inc=s_s)

        def stage1(st):
            par = st % 2
            if st == 1:
                for kt in range(KT):
                    sc.dma("gpsimd", wproj16[:, kt * D:(kt + 1) * D],
                           w_proj[kt * 128:(kt + 1) * 128, :], inc=s_gld)
                ngld[0] = s_gld.n
            if True:
                trA = P4[:, 0:1024]
                qkps = [P0, P1]
                vps = P2
                hps = P3[:, 0:16]
                sc.wait("tensor", s_s)
                sc.wait("tensor", s_v)

                for sub in range(2):
                    t = 4 * st + sub
                    xload(t, xn_guard.get(t - 2, 0))
                if st == NST - 1:
                    sc.dma("gpsimd", xhalo[:], xs[TOK:TOK + 2, :], inc=XH)

                for sub in range(4):
                    t = 4 * st + sub
                    emit_w_stores()
                    xap = xtile[:, (t % 2) * D:(t % 2) * D + D]
                    if t == 0:
                        ln_stats(0)
                    ln_apply(t, xnf[:], xn16[:], pe_guard=trx_guard[0])
                    if t + 1 < NT:
                        ln_stats(t + 1)
                    if dbg & 16:
                        sc.wait("sync", s_v)
                        sc.dma("sync", d_xn[t, :, :], xnf[:], inc=s_dbg)
                        sc.dma("sync", d_xt[t, :, :], xap, inc=s_dbg)
                        sc.wait("vector", s_dbg)
                    xn_guard[t] = s_v.n
                    if sub + 2 < 4 or st + 1 < NST:
                        xload(t + 2, xn_guard[t])
                    quant(xnf[:], xq16[:], D, pe_guard=trx_guard[0])
                    sc.wait("tensor", s_s)
                    sc.wait("tensor", s_v)
                    for kb in range(KT):
                        sc.op("tensor", lambda e, kb=kb, trA=trA: e.transpose(
                            trA[:, kb * 128:(kb + 1) * 128],
                            xn16[:, kb * 128:(kb + 1) * 128], ident[:]),
                            inc=(s_pe if kb == KT - 1 else None))
                    sc.wait("scalar", s_pe)
                    sc.op("scalar", lambda e, sub=sub, trA=trA: e.copy(
                        xnT[:].rearrange("p (kt q) -> p kt q", q=QW)[:, :, sub * 128:sub * 128 + 128],
                        trA[:].rearrange("p (kt q) -> p kt q", q=128)), inc=s_s)
                    sc.wait("tensor", s_s)
                    for kb in range(KT):
                        sc.op("tensor", lambda e, kb=kb, trA=trA: e.transpose(
                            trA[:, kb * 128:(kb + 1) * 128],
                            xq16[:, kb * 128:(kb + 1) * 128], ident[:]),
                            inc=(s_pe if kb == KT - 1 else None))
                    trx_guard[0] = s_pe.n
                    sc.wait("scalar", s_pe)
                    sc.wait("scalar", s_xqst, xqst_guard[0])
                    sc.op("scalar", lambda e, trA=trA: e.copy(xqTst[:], trA[:]), inc=s_s)
                    sc.wait("sync", s_s)
                    sc.dma("sync", d_xqT[t, :, :], xqTst[:], inc=s_xqst)
                    xqst_guard[0] = s_xqst.n
                    sc.wait("tensor", s_s)
                    emit_w_quants()

                if st == NST - 1:
                    sc.wait("vector", XH, 16)
                    ln_tile(xhalo[:], 2, hstat, haggr, htmp, xnh[:], xnh16[:],
                            pe_guard=trx_guard[0])
                    sc.wait("tensor", s_s)
                    for kb in range(KT):
                        sc.op("tensor", lambda e, kb=kb, trA=trA: e.transpose(
                            trA[:, kb * 2:(kb + 1) * 2],
                            xnh16[:, kb * 128:(kb + 1) * 128], ident[0:2, 0:2]),
                            inc=(s_pe if kb == KT - 1 else None))
                    sc.wait("scalar", s_pe)
                    sc.op("scalar", lambda e, trA=trA: e.copy(
                        xnT[:].rearrange("p (kt q) -> p kt q", q=QW)[:, :, 512:514],
                        trA[:, :2 * KT].rearrange("p (kt q) -> p kt q", q=2)), inc=s_s)
                    sc.wait("tensor", s_s)

                if alevel < 2:
                    return
                xnT3 = xnT[:].rearrange("p (kt q) -> p kt q", q=QW)
                for m in range(16):
                    q = m % 2
                    if qk_ev[q] > 0:
                        sc.wait("tensor", s_s, qk_ev[q])
                    for kt in range(KT):
                        if st == 0 and m == 0:
                            sc.wait("tensor", s_gld, gld_kt[kt])
                        sc.op("tensor", lambda e, kt=kt, m=m, q=q, qkps=qkps, xnT3=xnT3: e.matmul(
                            qkps[q][:, :],
                            wqkv16[:, kt * 3 * D + m * 128:kt * 3 * D + m * 128 + 128],
                            xnT3[:, kt, 0:512], start=(kt == 0), stop=(kt == KT - 1)),
                            inc=(s_pe if kt == KT - 1 else None))
                    sc.wait("scalar", s_pe)
                    sc.op("scalar", lambda e, m=m, q=q, par=par, qkps=qkps: e.copy(
                        qkT[par][:, m * QW:m * QW + 512], qkps[q][:, :]), inc=s_s)
                    qk_ev[q] = s_s.n
                    if m % 8 == 7:
                        emit_w_quants()
                if st == NST - 1:
                    for m in range(8, 16):
                        for kt in range(KT):
                            sc.op("tensor", lambda e, kt=kt, m=m, hps=hps, xnT3=xnT3: e.matmul(
                                hps[:, (m - 8) * 2:(m - 8) * 2 + 2],
                                wqkv16[:, kt * 3 * D + m * 128:kt * 3 * D + m * 128 + 128],
                                xnT3[:, kt, 512:514], start=(kt == 0), stop=(kt == KT - 1)),
                                inc=(s_pe if kt == KT - 1 and m == 15 else None))
                    sc.wait("scalar", s_pe)
                    sc.op("scalar", lambda e, par=par, hps=hps: e.copy(
                        qkT[par][:].rearrange("p (m q) -> p m q", q=QW)[:, 8:16, 512:514],
                        hps[:].rearrange("p (m q) -> p m q", q=2)), inc=s_s)

                v_ev = 0
                for sub in range(4):
                    if v_ev > 0:
                        sc.wait("tensor", s_s, v_ev)
                    for kt in range(KT):
                        for nch in range(2):
                            sc.op("tensor", lambda e, kt=kt, sub=sub, nch=nch, vps=vps, xnT3=xnT3: e.matmul(
                                vps[:, nch * 512:nch * 512 + 512],
                                xnT3[:, kt, sub * 128:sub * 128 + 128],
                                wqkv16[:, kt * 3 * D + 2 * D + nch * 512:kt * 3 * D + 2 * D + nch * 512 + 512],
                                start=(kt == 0), stop=(kt == KT - 1)),
                                inc=(s_pe if kt == KT - 1 and nch == 1 else None))
                    sc.wait("scalar", s_pe)
                    sc.op("scalar", lambda e, sub=sub, par=par, vps=vps: e.copy(
                        vsb[par][:, sub * D:sub * D + D], vps[:, :]), inc=s_s)
                    v_ev = s_s.n
                    if sub % 2 == 1:
                        emit_w_quants()
                if st == NST - 1:
                    sc.wait("tensor", s_s, v_ev)
                    for m in range(8):
                        for kt in range(KT):
                            sc.op("tensor", lambda e, kt=kt, m=m, hps=hps, xnT3=xnT3: e.matmul(
                                hps[:, m * 2:m * 2 + 2],
                                wqkv16[:, kt * 3 * D + 2 * D + m * 128:kt * 3 * D + 2 * D + m * 128 + 128],
                                xnT3[:, kt, 512:514], start=(kt == 0), stop=(kt == KT - 1)),
                                inc=(s_pe if kt == KT - 1 and m == 7 else None))
                    sc.wait("scalar", s_pe)
                    sc.op("scalar", lambda e, hps=hps: e.copy(vh16[:], hps[:]), inc=s_s)
                    sc.wait("tensor", s_s)
                    for m in range(8):
                        sc.op("tensor", lambda e, m=m, trA=trA: e.transpose(
                            trA[0:2, m * 128:(m + 1) * 128],
                            vh16[:, m * 2:m * 2 + 2], ident[:]),
                            inc=(s_pe if m == 7 else None))
                    sc.wait("scalar", s_pe)
                    sc.op("scalar", lambda e, trA=trA: e.copy(
                        vhalo[:], trA[0:2, 0:D]), inc=s_s)

        def attn(st):
            if alevel < 3:
                return
            par = st % 2
            qk3 = qkT[par][:].rearrange("p (m q) -> p m q", q=QW)
            qk3n = qkT[1 - par][:].rearrange("p (m q) -> p m q", q=QW)
            scps = [P0[:, 0:390], P1[:, 0:390]]
            sc.wait("tensor", s_s)
            sc.wait("tensor", s_v)
            qk3o = qkT[par][:].rearrange("p (m q) -> p m q", q=QW)
            qk3no = qkT[1 - par][:].rearrange("p (m q) -> p m q", q=QW)
            ep_guard = [0]
            pt_guard = [0]
            ptm_ev = {}
            sout_free = [0]
            sc.wait("sync", s_s)
            odd_guard = [0]

            def emit_tail(sub):
                t = 4 * st + sub
                otp = P2
                sc.wait("tensor", s_s, ptm_ev[sub])
                sc.wait("tensor", s_s, sout_free[0])
                for hh in range(H):
                    r0 = 64 * (hh % 2)
                    mco = (hh // 2) * 128
                    vmain = vsb[par][:, sub * D + hh * 64:sub * D + hh * 64 + 64]
                    if sub < 3:
                        vnext = vsb[par][0:2, (sub + 1) * D + hh * 64:(sub + 1) * D + hh * 64 + 64]
                    elif st < NST - 1:
                        vnext = vsb[1 - par][0:2, hh * 64:hh * 64 + 64]
                    else:
                        vnext = vhalo[0:2, hh * 64:hh * 64 + 64]
                    sc.op("tensor", lambda e, vmain=vmain, hh=hh, r0=r0, mco=mco, otp=otp: e.matmul(
                        otp[r0:r0 + 64, mco:mco + 128], vmain,
                        PTm[:, hh * 128:hh * 128 + 128], start=True, stop=False))
                    sc.op("tensor", lambda e, vnext=vnext, hh=hh, r0=r0, mco=mco, otp=otp: e.matmul(
                        otp[r0:r0 + 64, mco:mco + 128], vnext,
                        PTt[0:2, hh * 128:hh * 128 + 128], start=False, stop=True),
                        inc=(s_pe if hh == H - 1 else None))
                sc.wait("scalar", s_pe)
                sc.op("scalar", lambda e, otp=otp: e.copy(outT16[:], otp[:]), inc=s_s)
                sps = P2
                sc.wait("tensor", s_s)
                sc.wait("tensor", s_gld, ngld[0])
                for kt in range(KT):
                    for nch in range(2):
                        sc.op("tensor", lambda e, kt=kt, nch=nch, sps=sps: e.matmul(
                            sps[:, nch * 512:nch * 512 + 512],
                            outT16[:, kt * 128:kt * 128 + 128],
                            wproj16[:, kt * D + nch * 512:kt * D + nch * 512 + 512],
                            start=(kt == 0), stop=False))
                for nch in range(2):
                    sc.op("tensor", lambda e, nch=nch, sps=sps: e.matmul(
                        sps[:, nch * 512:nch * 512 + 512], onesr[:],
                        bproj16[:, nch * 512:nch * 512 + 512],
                        start=False, stop=True),
                        inc=(s_pe if nch == 1 else None))
                sc.wait("scalar", s_pe)
                sc.wait("scalar", s_sst, sst_guard[0])
                sc.op("scalar", lambda e, sps=sps: e.copy(sOut[:], sps[:]), inc=s_s)
                sout_free[0] = s_s.n
                sc.wait("sync", s_s)
                sc.dma("sync", d_s[t, :, :], sOut[:], inc=s_sst)
                sst_guard[0] = s_sst.n

            for sub in range(4):
                t = 4 * st + sub
                if odd_guard[0] > 0:
                    sc.wait("sync", s_pe, odd_guard[0])
                sc.dma("sync", qodd[:].rearrange("p (m q) -> p m q", q=128),
                       qk3o[64:128, 0:8, sub * 128:sub * 128 + 128], inc=SH)
                kodd3 = kodd[:].rearrange("p (m q) -> p m q", q=132)
                if sub < 3 or st == NST - 1:
                    sc.dma("sync", kodd3[:, :, 0:130],
                           qk3o[64:128, 8:16, sub * 128:sub * 128 + 130], inc=SH)
                else:
                    sc.dma("sync", kodd3[:, :, 0:128],
                           qk3o[64:128, 8:16, sub * 128:sub * 128 + 128], inc=SH)
                    sc.dma("sync", kodd3[:, :, 128:130],
                           qk3no[64:128, 8:16, 0:2], inc=SH)
                sh_n = SH.n
                groups = [(0, 3), (3, 6), (6, 9), (9, 12), (12, 14), (14, 16)]
                sc.wait("tensor", SH, sh_n)
                for gi, (h0, h1) in enumerate(groups):
                    g = gi % 2
                    if sc_ev[g] > 0:
                        sc.wait("tensor", s_s, sc_ev[g])
                    for hh in range(h0, h1):
                        odd = hh % 2
                        mq = hh // 2
                        co = (hh - h0) * 130
                        if odd:
                            lhs = qodd[:].rearrange("p (m q) -> p m q", q=128)[:, mq, :]
                            rhs = kodd[:].rearrange("p (m q) -> p m q", q=132)[:, mq, 0:130]
                            sc.op("tensor", lambda e, lhs=lhs, rhs=rhs, g=g, co=co, scps=scps: e.matmul(
                                scps[g][:, co:co + 130], lhs, rhs, start=True, stop=True),
                                inc=(s_pe if hh == h1 - 1 else None))
                        else:
                            mk = 8 + hh // 2
                            lhs = qk3[0:64, mq, sub * 128:sub * 128 + 128]
                            if sub < 3 or st == NST - 1:
                                rhs = qk3[0:64, mk, sub * 128:sub * 128 + 130]
                                sc.op("tensor", lambda e, lhs=lhs, rhs=rhs, g=g, co=co, scps=scps: e.matmul(
                                    scps[g][:, co:co + 130], lhs, rhs, start=True, stop=True),
                                    inc=(s_pe if hh == h1 - 1 else None))
                            else:
                                rhs = qk3[0:64, mk, sub * 128:sub * 128 + 128]
                                rhs2 = qk3n[0:64, mk, 0:2]
                                sc.op("tensor", lambda e, lhs=lhs, rhs=rhs, g=g, co=co, scps=scps: e.matmul(
                                    scps[g][:, co:co + 128], lhs, rhs, start=True, stop=True))
                                sc.op("tensor", lambda e, lhs=lhs, rhs2=rhs2, g=g, co=co, scps=scps: e.matmul(
                                    scps[g][:, co + 128:co + 130], lhs, rhs2, start=True, stop=True),
                                    inc=(s_pe if hh == h1 - 1 else None))
                    sc.wait("scalar", s_pe)
                    if gi == 0:
                        sc.wait("scalar", s_v, ep_guard[0])
                        sc.wait("scalar", s_pe, pt_guard[0])
                    sc.op("scalar", lambda e, h0=h0, h1=h1, g=g, scps=scps: e.activation(
                        e16[:, h0 * 130:h1 * 130], scps[g][:, :(h1 - h0) * 130],
                        AF.Exp, scale=cvec[:, 2:3]), inc=s_s)
                    sc_ev[g] = s_s.n
                exp_ev = s_s.n
                odd_guard[0] = s_pe.n
                emit_w_stores()
                if alevel < 4:
                    continue
                if sub > 0 and alevel >= 6:
                    emit_tail(sub - 1)
                mask = maskl if t == NT - 1 else maskm
                e3 = e16[:].rearrange("p (h w) -> p h w", w=130)
                eb3 = e3
                p3 = e3
                mb = mask[:].unsqueeze(1).broadcast_to([128, H, 130])
                sc.wait("vector", s_s, exp_ev)
                sc.wait("vector", s_pe, pt_guard[0])
                sc.op("vector", lambda e, e3=e3, eb3=eb3, mb=mb: e.tensor_tensor(
                    eb3, e3, mb, ALU.mult))
                sc.op("vector", lambda e, eb3=eb3: e.tensor_reduce(
                    rs32[:], eb3, AX.X, ALU.add), inc=s_v)
                sc.wait("vector", s_v)
                sc.op("vector", lambda e: e.reciprocal(rsi32[:], rs32[:]), inc=s_v)
                sc.wait("vector", s_v)
                sc.op("vector", lambda e: e.tensor_copy(rinv16[:], rsi32[:]), inc=s_v)
                sc.wait("vector", s_v)
                rb = rinv16[:].unsqueeze(2).broadcast_to([128, H, 130])
                sc.op("vector", lambda e, p3=p3, eb3=eb3, rb=rb: e.tensor_tensor(
                    p3, eb3, rb, ALU.mult), inc=s_v)
                ep_guard[0] = s_v.n
                if dbg & 16:
                    sc.wait("sync", s_v)
                    sc.dma("sync", d_e[t, :, :], e16[:], inc=s_dbg)
                    sc.dma("sync", d_p[t, :, :], e16[:], inc=s_dbg)
                    sc.dma("sync", d_rs[t, :, :], rs32[:], inc=s_dbg)
                    sc.dma("sync", d_eb[t, :, :], e16[:], inc=s_dbg)
                    sc.dma("sync", d_ri[t, :, :], rinv16[:], inc=s_dbg)
                    sc.wait("scalar", s_dbg)
                if alevel < 5:
                    continue
                if True:
                    ptm = P4; ptt = P3b
                    sc.wait("tensor", s_v)
                    sc.wait("tensor", s_s)
                    for hh in range(H):
                        sc.op("tensor", lambda e, hh=hh, ptm=ptm: e.transpose(
                            ptm[:, hh * 128:hh * 128 + 128],
                            e16[:, hh * 130:hh * 130 + 128], ident[:]))
                    for hh in range(H):
                        sc.op("tensor", lambda e, hh=hh, ptt=ptt: e.transpose(
                            ptt[0:2, hh * 128:hh * 128 + 128],
                            e16[:, hh * 130 + 128:hh * 130 + 130], ident[:]),
                            inc=(s_pe if hh == H - 1 else None))
                    pt_guard[0] = s_pe.n
                    sc.wait("scalar", s_pe)
                    sc.op("scalar", lambda e, ptm=ptm: e.copy(PTm[:], ptm[:]))
                    sc.op("scalar", lambda e, ptt=ptt: e.copy(PTt[:], ptt[0:2, :]), inc=s_s)
                    ptm_ev[sub] = s_s.n
                emit_w_quants()
            if alevel >= 6:
                emit_tail(3)

        for st in range(NST):
            stage1(st)
            if st > 0:
                attn(st - 1)
        attn(NST - 1)
        flush_w()
        n_w_store = s_st.n

    nA_st = s_st.n
    nA_xq = s_xqst.n
    nA_ss = s_sst.n
    nA_v = s_v.n
    nA_s = s_s.n
    nA_pe = s_pe.n
    if phases == "WA":
        sc.wait("sync", s_st, nA_st)
        sc.wait("sync", s_xqst, nA_xq)
        sc.wait("sync", s_sst, nA_ss)
        sc.wait("sync", s_v, nA_v)
        sc.wait("sync", s_s, nA_s)
        sc.wait("sync", s_pe, nA_pe)
        sc.run()
        return nc

    with contextlib.ExitStack() as _esB:
        def sbb(name, shape, dt):
            return _esB.enter_context(nc.sbuf_tensor(name, shape, dt)).ap()
        wgT = sbb("wgT", [128, KT * I_FF], BF16)
        wuT = sbb("wuT", [128, KT * I_FF], BF16)
        wdT = sbb("wdT", [128, IKB * D], BF16)
        xqTt = sbb("xqTt", [128, 2 * D], BF16)
        sT = sbb("sT", [128, 2 * D], BF16)
        xT = sbb("xT", [128, 2 * D], F32)
        sgb = sbb("sgb", [128, 1024], F32)
        hb = sbb("hb", [128, 1024], F32)
        hq16 = sbb("hq16", [128, I_FF], BF16)
        hqT = sbb("hqT", [128, IKB * 128], BF16)
        gb = sbb("gb", [128, D], F32)
        tb = sbb("tbb", [128, D], F32)
        ob = sbb("ob", [128, 2 * D], F32)

        for qq in ("sync", "scalar", "gpsimd"):
            sc.wait(qq, s_st, nA_st)
            sc.wait(qq, s_xqst, nA_xq)
            sc.wait(qq, s_sst, nA_ss)
            sc.wait(qq, s_v, nA_v)
            sc.wait(qq, s_s, nA_s)
            sc.wait(qq, s_pe, nA_pe)

        BL = [Sem(nc, "s_bl0"), Sem(nc, "s_bl1")]
        bl_idx = {}
        pe_tile_end = {}
        v_tile_end = {}
        hq_guard = {}

        def bload(t):
            par = t % 2
            if t - 2 in pe_tile_end:
                sc.wait("gpsimd", s_pe, pe_tile_end[t - 2])
                sc.wait("gpsimd", s_v, v_tile_end[t - 2])
            sc.dma("gpsimd", xqTt[:, par * D:par * D + D], d_xqT[t, :, :], inc=BL[par])
            sc.dma("gpsimd", sT[:, par * D:par * D + D], d_s[t, :, :], inc=BL[par])
            sc.dma("gpsimd", xT[:, par * D:par * D + D],
                   xs[t * 128:(t + 1) * 128, :], inc=BL[par])
            bl_idx[t] = BL[par].n

        s_wg = [Sem(nc, "s_wg%d" % i) for i in range(6)]
        s_wu = [Sem(nc, "s_wu%d" % i) for i in range(6)]
        s_wd = [Sem(nc, "s_wd%d" % i) for i in range(3)]
        for ci, (c0, c1) in enumerate(IC_CH):
            for kt in range(KT):
                sc.dma("sync", wgT[:, kt * I_FF + c0:kt * I_FF + c1],
                       d_wgT[kt, :, c0:c1], inc=s_wg[ci])
                sc.dma("scalar", wuT[:, kt * I_FF + c0:kt * I_FF + c1],
                       d_wuT[kt, :, c0:c1], inc=s_wu[ci])
        bload(0)
        for g in range(3):
            for kt in range(8 * g, min(IKB, 8 * g + 8)):
                sc.dma("gpsimd", wdT[:, kt * D:(kt + 1) * D], d_wdT[kt, :, :], inc=s_wd[g])
            if g == 0:
                bload(1)

        if True:
            gpsA = [P0, P3[:, 0:512]]
            upsA = [P1, P3[:, 512:1024]]
            dps = P2
            trB = P4[:, 0:1024]

            h_ev = {}
            sg_ev = {}
            q_ev = {}
            hqT_ev = {}
            hqT_ev_roll = {}
            gb_ev = {}
            pending = {}

            def tr_pair(t, pr, p0, pw):
                sc.wait("tensor", s_v, q_ev[(t, pr)])
                nblk = pw // 128
                gpr = t * 3 + pr
                if gpr >= 1:
                    sc.wait("tensor", s_s, hqT_ev_roll[gpr - 1])
                for b in range(nblk):
                    kb = p0 // 128 + b
                    sc.op("tensor", lambda e, b=b, kb=kb: e.transpose(
                        trB[:, b * 128:(b + 1) * 128],
                        hq16[:, kb * 128:(kb + 1) * 128], ident[:]),
                        inc=(s_pe if b == nblk - 1 else None))
                hq_guard[pr] = s_pe.n
                sc.wait("scalar", s_pe, hq_guard[pr])
                sc.op("scalar", lambda e, p0=p0, nblk=nblk: e.copy(
                    hqT[:, p0:p0 + nblk * 128], trB[:, :nblk * 128]), inc=s_s)
                hqT_ev[(t, pr)] = s_s.n
                hqT_ev_roll[gpr] = s_s.n

            def emit_down_pair(t, pr):
                c0 = IC_CH[pr * 2][0]
                c1 = IC_CH[pr * 2 + 1][1]
                if t <= 1:
                    g = min(pr, 2)
                    sc.wait("tensor", s_wd[g], s_wd[g].n)
                if pr == 0 and (t - 1) in gb_ev:
                    sc.wait("tensor", s_s, gb_ev[t - 1])
                sc.wait("tensor", s_s, hqT_ev[(t, pr)])
                for kt in range(c0 // 128, c1 // 128):
                    for nch in range(2):
                        sc.op("tensor", lambda e, kt=kt, nch=nch: e.matmul(
                            dps[:, nch * 512:nch * 512 + 512],
                            hqT[:, kt * 128:kt * 128 + 128],
                            wdT[:, kt * D + nch * 512:kt * D + nch * 512 + 512],
                            start=(kt == 0), stop=(kt == IKB - 1)),
                            inc=(s_pe if kt == c1 // 128 - 1 and nch == 1 else None))

            def tile_epilogue(t):
                par = t % 2
                s_t = sT[:, par * D:par * D + D]
                x_t = xT[:, par * D:par * D + D]
                pe_tile_end[t] = s_pe.n
                sc.wait("scalar", s_pe)
                sc.wait("scalar", s_v, v_tile_end.get(t - 1, 0))
                sc.op("scalar", lambda e: e.activation(gb[:], dps[:], AF.Silu), inc=s_s)
                gb_ev[t] = s_s.n
                sc.wait("vector", s_s, gb_ev[t])
                if t >= 2:
                    sc.wait("vector", OUT[par], OUT[par].n)
                sc.op("vector", lambda e, s_t=s_t: e.tensor_tensor(
                    tb[:], s_t, gb[:], ALU.mult))
                sc.op("vector", lambda e, x_t=x_t, par=par: e.tensor_tensor(
                    ob[:, par * D:par * D + D], x_t, tb[:], ALU.add), inc=s_v)
                v_tile_end[t] = s_v.n
                sc.wait("sync", s_v)
                sc.dma("sync", out_d[t * 128:(t + 1) * 128, :],
                       ob[:, par * D:par * D + D], inc=OUT[par])
                if t + 2 < NT:
                    bload(t + 2)

            NG = NT * 6
            for gci in range(NG):
                t, ci = divmod(gci, 6)
                c0, c1 = IC_CH[ci]
                cw = c1 - c0
                par = t % 2
                xq_t = xqTt[:, par * D:par * D + D]
                gps = gpsA[ci % 2]
                ups = upsA[ci % 2]
                if ci == 0:
                    sc.wait("tensor", BL[par], bl_idx[t])
                if gci >= 2:
                    sc.wait("tensor", s_s, sg_ev[gci - 2])
                    sc.wait("tensor", s_v, h_ev[gci - 2])
                if t <= 1:
                    sc.wait("tensor", s_wg[ci], s_wg[ci].n)
                    sc.wait("tensor", s_wu[ci], s_wu[ci].n)
                for kt in range(KT):
                    sc.op("tensor", lambda e, kt=kt, c0=c0, cw=cw, xq_t=xq_t, gps=gps: e.matmul(
                        gps[:, :cw], xq_t[:, kt * 128:kt * 128 + 128],
                        wgT[:, kt * I_FF + c0:kt * I_FF + c0 + cw],
                        start=(kt == 0), stop=(kt == KT - 1)))
                for kt in range(KT):
                    sc.op("tensor", lambda e, kt=kt, c0=c0, cw=cw, xq_t=xq_t, ups=ups: e.matmul(
                        ups[:, :cw], xq_t[:, kt * 128:kt * 128 + 128],
                        wuT[:, kt * I_FF + c0:kt * I_FF + c0 + cw],
                        start=(kt == 0), stop=(kt == KT - 1)),
                        inc=(s_pe if kt == KT - 1 else None))
                gu_end = s_pe.n
                for fn in pending.pop(gci, []):
                    fn()
                sc.wait("scalar", s_pe, gu_end)
                sc.wait("scalar", s_v, h_ev.get(gci - 2, 0))
                sc.op("scalar", lambda e, cw=cw, gps=gps, ci=ci: e.activation(
                    sgb[:, (ci % 2) * 512:(ci % 2) * 512 + cw], gps[:, :cw], AF.Silu), inc=s_s)
                sg_ev[gci] = s_s.n
                sc.wait("vector", s_s, sg_ev[gci])
                sc.op("vector", lambda e, cw=cw, ups=ups, ci=ci: e.tensor_tensor(
                    hb[:, (ci % 2) * 512:(ci % 2) * 512 + cw],
                    sgb[:, (ci % 2) * 512:(ci % 2) * 512 + cw],
                    ups[:, :cw], ALU.mult), inc=s_v)
                h_ev[gci] = s_v.n
                if ci % 2 == 1:
                    pr = ci // 2
                    p0 = IC_CH[ci - 1][0]
                    pw = c1 - p0
                    quant(hb[:, :pw], hq16[:, p0:c1], pw, pe_guard=hq_guard.get(pr, 0))
                    q_ev[(t, pr)] = s_v.n

                    pending.setdefault(gci + 2, []).append(
                        lambda t=t, pr=pr, p0=p0, pw=pw: tr_pair(t, pr, p0, pw))

                    def deferred(t=t, pr=pr):
                        emit_down_pair(t, pr)
                        if pr == 2:
                            tile_epilogue(t)
                    pending.setdefault(gci + 3, []).append(deferred)
            for gci in range(NG, NG + 4):
                for fn in pending.pop(gci, []):
                    fn()

    sc.wait("sync", OUT[0], OUT[0].n)
    sc.wait("sync", OUT[1], OUT[1].n)
    sc.run()
    return nc


_prog_cache = {}


def kernel(L0_attn_x, L0_attn_w_qkv, L0_attn_w_proj, L0_attn_b_proj,
           L0_ffn_w_gate, L0_ffn_w_up, L0_ffn_w_down):
    x = np.ascontiguousarray(np.asarray(L0_attn_x, dtype=np.float32))
    B, T, _ = x.shape
    n_cores = 8
    per_b = n_cores // B

    if "nc" not in _prog_cache:
        _prog_cache["nc"] = build_program()
    nc = _prog_cache["nc"]

    i_idx = np.arange(128)[:, None]
    j_idx = np.arange(130)[None, :]
    band = (j_idx >= i_idx) & (j_idx <= i_idx + 2)
    maskmain = band.astype(ml_dtypes.bfloat16)
    masklast_edge = (band & (j_idx < 128)).astype(ml_dtypes.bfloat16)
    ident = np.eye(128, dtype=ml_dtypes.bfloat16)
    onesrow = np.ones((1, 128), dtype=ml_dtypes.bfloat16)
    cvec = np.tile(np.array([[1e-5, -0.5, 0.125, -1.0]], dtype=np.float32), (128, 1))

    wqkv = np.ascontiguousarray(np.asarray(L0_attn_w_qkv, dtype=np.float32))
    wproj = np.ascontiguousarray(np.asarray(L0_attn_w_proj, dtype=np.float32))
    bproj = np.ascontiguousarray(np.asarray(L0_attn_b_proj, dtype=np.float32)).reshape(1, -1)
    wg = np.ascontiguousarray(np.asarray(L0_ffn_w_gate, dtype=np.float32))
    wu = np.ascontiguousarray(np.asarray(L0_ffn_w_up, dtype=np.float32))
    wd = np.ascontiguousarray(np.asarray(L0_ffn_w_down, dtype=np.float32))

    in_maps = []
    for c in range(n_cores):
        b = c // per_b
        r0 = (c % per_b) * TOK
        xsc = np.zeros((TOK + 2, D), dtype=np.float32)
        end = min(r0 + TOK + 2, T)
        xsc[:end - r0] = x[b, r0:end]
        last = (c % per_b) == per_b - 1
        in_maps.append(dict(
            xs=xsc, w_qkv=wqkv, w_proj=wproj, b_proj=bproj,
            wg=wg, wu=wu, wd=wd, ident=ident,
            maskmain=maskmain, masklast=(masklast_edge if last else maskmain),
            onesrow=onesrow, cvec=cvec,
        ))

    res = run_bass_kernel_spmd(nc, in_maps, core_ids=list(range(n_cores)))
    out = np.empty((B, T, D), dtype=np.float32)
    for c in range(n_cores):
        b = c // per_b
        r0 = (c % per_b) * TOK
        out[b, r0:r0 + TOK] = np.asarray(res.results[c]["out"], dtype=np.float32)
    return out


# version 18
# speedup vs baseline: 1.1217x; 1.0183x over previous
import contextlib
import numpy as np
import ml_dtypes
import concourse.bass as bass
import concourse.mybir as mybir
from concourse.bass_utils import run_bass_kernel_spmd

F32 = mybir.dt.float32
BF16 = mybir.dt.bfloat16
AF = mybir.ActivationFunctionType
ALU = mybir.AluOpType
AX = mybir.AxisListType

M1 = 12582912.0
M2 = 25165824.0

D = 1024
KT = 8
H = 16
TOK = 2048
NT = 16
NST = 4
I_FF = 2816
IKB = 22
QW = 520
IC_CH = [(0, 512), (512, 1024), (1024, 1536), (1536, 2048), (2048, 2560), (2560, 2816)]

ENGS = ("sync", "vector", "scalar", "tensor", "gpsimd")


class Sem:
    def __init__(self, nc, name):
        self.h = nc.alloc_semaphore(name)
        self.n = 0


class Sched:

    def __init__(self, nc):
        self.nc = nc
        self.q = {e: [] for e in ENGS}

    def wait(self, eng, sem, val=None):
        v = sem.n if val is None else val
        if v > 0:
            self.q[eng].append(("w", sem.h, v))

    def op(self, eng, fn, inc=None, k=1):
        incs = []
        if inc is not None:
            for s in (inc if isinstance(inc, (list, tuple)) else [inc]):
                s.n += k
                incs.append((s.h, k))
        self.q[eng].append(("o", fn, incs))

    def dma(self, eng, out_ap, in_ap, inc=None):
        self.op(eng, lambda e, o=out_ap, i=in_ap: e.dma_start(o, i), inc=inc, k=16)

    def run(self):
        with self.nc.Block() as blk:
            def emit(e, items):
                for it in items:
                    if it[0] == "w":
                        e.wait_ge(it[1], it[2])
                    else:
                        instr = it[1](e)
                        for (h, k) in it[2]:
                            instr = instr.then_inc(h, k)
            blk.sync(lambda e: emit(e, self.q["sync"]))
            blk.vector(lambda e: emit(e, self.q["vector"]))
            blk.scalar(lambda e: emit(e, self.q["scalar"]))
            blk.tensor(lambda e: emit(e, self.q["tensor"]))
            blk.gpsimd(lambda e: emit(e, self.q["gpsimd"]))


def build_program(phases="WAB", alevel=9, dbg=0):
    assert phases in ("WA", "WAB")
    nc = bass.Bass("TRN2", target_bir_lowering=False)
    sc = Sched(nc)

    def din(name, shape, dt):
        return nc.dram_tensor(name, shape, dt, kind="ExternalInput").ap()

    xs = din("xs", [TOK + 2, D], F32)
    w_qkv = din("w_qkv", [D, 3 * D], F32)
    w_proj = din("w_proj", [D, D], F32)
    b_proj = din("b_proj", [1, D], F32)
    wg_d = din("wg", [I_FF, D], F32)
    wu_d = din("wu", [I_FF, D], F32)
    wd_d = din("wd", [D, I_FF], F32)
    ident_in = din("ident", [128, 128], BF16)
    maskmain_in = din("maskmain", [128, 130], BF16)
    masklast_in = din("masklast", [128, 130], BF16)
    onesrow_in = din("onesrow", [1, 128], BF16)
    cvec_in = din("cvec", [128, 4], F32)

    out_d = nc.dram_tensor("out", [TOK, D], F32, kind="ExternalOutput").ap()

    _stage_kind = "ExternalOutput" if (dbg & 8) else "Internal"
    d_s = nc.dram_tensor("d_s", [NT, 128, D], BF16, kind=_stage_kind).ap()
    d_xqT = nc.dram_tensor("d_xqT", [NT, 128, D], BF16, kind=_stage_kind).ap()
    if dbg & 16:
        d_e = nc.dram_tensor("d_e", [NT, 128, H * 130], BF16, kind="ExternalOutput").ap()
        d_p = nc.dram_tensor("d_p", [NT, 128, H * 130], BF16, kind="ExternalOutput").ap()
        d_rs = nc.dram_tensor("d_rs", [NT, 128, H], F32, kind="ExternalOutput").ap()
        d_eb = nc.dram_tensor("d_eb", [NT, 128, H * 130], BF16, kind="ExternalOutput").ap()
        d_ri = nc.dram_tensor("d_ri", [NT, 128, H], BF16, kind="ExternalOutput").ap()
        d_xn = nc.dram_tensor("d_xn", [NT, 128, D], F32, kind="ExternalOutput").ap()
        d_xt = nc.dram_tensor("d_xt", [NT, 128, D], F32, kind="ExternalOutput").ap()
    d_wgT = nc.dram_tensor("d_wgT", [KT, 128, IKB * 128], BF16, kind="Internal").ap()
    d_wuT = nc.dram_tensor("d_wuT", [KT, 128, IKB * 128], BF16, kind="Internal").ap()
    d_wdT = nc.dram_tensor("d_wdT", [IKB, 128, KT * 128], BF16, kind="Internal").ap()

    def sb(name, shape, dt):
        return nc.alloc_sbuf_tensor(name, shape, dt).ap()

    ident = sb("ident_s", [128, 128], BF16)
    maskm = sb("maskm_s", [128, 130], BF16)
    maskl = sb("maskl_s", [128, 130], BF16)
    onesr = sb("onesr_s", [1, 128], BF16)
    cvec = sb("cvec_s", [128, 4], F32)
    bproj16 = sb("bproj16", [1, D], BF16)

    QWID = 1024
    wbuf = sb("q_w", [128, QWID], F32)
    t1 = sb("q_t1", [128, QWID], BF16)
    t2 = sb("q_t2", [128, QWID], BF16)
    t3 = sb("q_t3", [128, QWID], BF16)
    t4 = sb("q_t4", [128, QWID], BF16)
    NBMAX = QWID // 16
    bmb = sb("q_bm", [128, NBMAX], F32)
    scb = sb("q_sc", [128, NBMAX], F32)
    lnb = sb("q_ln", [128, NBMAX], F32)
    rinvb = sb("q_rinv", [128, NBMAX], F32)
    rinv2b = sb("q_rinv2", [128, NBMAX], F32)
    qscb = sb("q_qsc", [128, NBMAX], BF16)

    P0 = nc.alloc_psum_tensor("P0", [128, 512], F32).ap()
    P1 = nc.alloc_psum_tensor("P1", [128, 512], F32).ap()
    P2 = nc.alloc_psum_tensor("P2", [128, 1024], F32).ap()
    P3 = nc.alloc_psum_tensor("P3", [128, 1024], F32).ap()
    P4 = nc.alloc_psum_tensor("P4", [128, 2048], BF16).ap()
    P3b = P3.bitcast(BF16)

    s_ld = Sem(nc, "s_ld")
    s_gld = Sem(nc, "s_gld")
    s_st = Sem(nc, "s_st")
    s_xqst = Sem(nc, "s_xqst")
    s_sst = Sem(nc, "s_sst")
    s_dbg = Sem(nc, "s_dbg")
    OUT = [Sem(nc, "s_out0"), Sem(nc, "s_out1")]
    s_pe = Sem(nc, "s_pe")
    s_v = Sem(nc, "s_v")
    s_s = Sem(nc, "s_s")

    sc.dma("sync", ident[:], ident_in[:], inc=s_ld)
    sc.dma("sync", maskm[:], maskmain_in[:], inc=s_ld)
    sc.dma("sync", maskl[:], masklast_in[:], inc=s_ld)
    sc.dma("sync", onesr[:], onesrow_in[:], inc=s_ld)
    sc.dma("sync", cvec[:], cvec_in[:], inc=s_ld)
    N_PROLOG = 5
    for _e in ("vector", "scalar", "tensor", "sync"):
        sc.wait(_e, s_ld, N_PROLOG * 16)
    sc.dma("gpsimd", bproj16[:], b_proj[:], inc=s_gld)

    def quant(src_ap, dst_ap, W, pe_guard=0):
        nb = W // 16
        src3 = src_ap.rearrange("p (nb b) -> p nb b", b=16)
        v, s = "vector", "scalar"
        sc.op(v, lambda e, src3=src3, nb=nb: e.tensor_reduce(
            bmb[:, :nb], src3, AX.X, ALU.max, apply_absolute_value=True), inc=s_v)
        sc.wait(v, s_v)
        sc.op(v, lambda e, nb=nb: e.tensor_scalar(
            scb[:, :nb], bmb[:, :nb], 1.0 / 6.0, 1e-12, ALU.mult, ALU.max), inc=s_v)
        sc.wait(v, s_v)
        sc.op(v, lambda e, nb=nb: e.reciprocal(rinvb[:, :nb], scb[:, :nb]), inc=s_v)
        sc.wait(v, s_v)
        sc.op(v, lambda e, nb=nb: e.tensor_scalar(
            rinv2b[:, :nb], rinvb[:, :nb], 2.0, None, ALU.mult))
        sc.op(v, lambda e, nb=nb: e.tensor_scalar(
            qscb[:, :nb], scb[:, :nb], 0.5, None, ALU.mult))
        r2bc = rinv2b[:, :nb].unsqueeze(2).broadcast_to([128, nb, 16])
        qsbc = qscb[:, :nb].unsqueeze(2).broadcast_to([128, nb, 16])
        wv = wbuf[:, :W]
        wv3 = wv.rearrange("p (nb b) -> p nb b", b=16)
        A_, B_, C_, D_ = (t1[:, :W], t2[:, :W], t3[:, :W], t4[:, :W])
        sc.op(v, lambda e, wv3=wv3, src3=src3, r2bc=r2bc: e.tensor_tensor(
            wv3, src3, r2bc, ALU.mult))
        sc.op(v, lambda e, A_=A_, wv=wv: e.tensor_scalar(A_, wv, M1, -M1, ALU.add, ALU.add))
        sc.op(v, lambda e, A_=A_, B_=B_: e.tensor_scalar(B_, A_, -4.0, 4.0, ALU.max, ALU.min))
        sc.op(v, lambda e, C_=C_, wv=wv: e.tensor_scalar(C_, wv, M2, -M2, ALU.add, ALU.add))
        sc.op(v, lambda e, C_=C_, D_=D_: e.tensor_scalar(D_, C_, -4.0, 4.0, ALU.max, ALU.min))
        sc.op(v, lambda e, A_=A_, C_=C_, D_=D_: e.tensor_tensor(A_, C_, D_, ALU.subtract))
        sc.op(v, lambda e, A_=A_, C_=C_: e.tensor_scalar(C_, A_, -4.0, 4.0, ALU.max, ALU.min))
        sc.op(v, lambda e, D_=D_, wv=wv: e.tensor_scalar(D_, wv, 10.0, 4.0, ALU.is_ge, ALU.mult))
        sc.op(v, lambda e, A_=A_, wv=wv: e.tensor_scalar(A_, wv, -10.0, 4.0, ALU.is_le, ALU.mult))
        sc.op(v, lambda e, A_=A_, D_=D_: e.tensor_tensor(D_, D_, A_, ALU.subtract))
        sc.op(v, lambda e, B_=B_, C_=C_: e.tensor_tensor(B_, B_, C_, ALU.add))
        sc.op(v, lambda e, B_=B_, D_=D_: e.tensor_tensor(B_, B_, D_, ALU.add))
        dst3 = dst_ap.rearrange("p (nb b) -> p nb b", b=16)
        B3 = B_.rearrange("p (nb b) -> p nb b", b=16)
        if pe_guard:
            sc.wait(v, s_pe, pe_guard)
        sc.op(v, lambda e, dst3=dst3, B3=B3, qsbc=qsbc: e.tensor_tensor(
            dst3, B3, qsbc, ALU.mult), inc=s_v)


    XL = [Sem(nc, "s_xl0"), Sem(nc, "s_xl1")]
    XH = Sem(nc, "s_xh")
    SH = Sem(nc, "s_sh")

    with contextlib.ExitStack() as _esA:
        def sba(name, shape, dt):
            return _esA.enter_context(nc.sbuf_tensor(name, shape, dt)).ap()
        xtile = sba("xtile", [128, 2 * D], F32)
        xnf = sba("xnf", [128, D], F32)
        xn16 = sba("xn16", [128, D], BF16)
        xq16 = sba("xq16", [128, D], BF16)
        xqTst = sba("xqTst", [128, D], BF16)
        xnT = sba("xnT", [128, KT * QW], BF16)
        qkT0_ = sba("qkT0", [128, 16 * QW], BF16)
        vsb0_ = sba("vsb0", [128, 4 * D], BF16)
        wqkv16 = sba("wqkv16", [128, KT * 3 * D], BF16)
        xhalo = sba("xhalo", [2, D], F32)
        stats = [sba("ln_stats0", [128, 12], F32), sba("ln_stats1", [128, 12], F32)]
        aggr = [sba("ln_aggr0", [128, 2], F32), sba("ln_aggr1", [128, 2], F32)]
        lntmp = [sba("ln_tmp0", [128, 4], F32), sba("ln_tmp1", [128, 4], F32)]
        hstat = sba("hstat", [2, 12], F32)
        haggr = sba("haggr", [2, 2], F32)
        htmp = sba("htmp", [2, 4], F32)
        xnh = sba("xnh", [2, D], F32)
        xnh16 = sba("xnh16", [2, D], BF16)
        qkT = [qkT0_, sba("qkT1", [128, 16 * QW], BF16)]
        vsb = [vsb0_, sba("vsb1", [128, 4 * D], BF16)]
        vhalo = sba("vhalo", [2, D], BF16)
        vh16 = sba("vh16", [128, 16], BF16)
        e16 = sba("e16", [128, H * 130], BF16)
        rs32 = sba("rs32", [128, H], F32)
        rsi32 = sba("rsi32", [128, H], F32)
        rinv16 = sba("rinv16b", [128, H], BF16)
        qodd = sba("qodd", [64, 8 * 128], BF16)
        kodd = sba("kodd", [64, 8 * 132], BF16)
        PTm = sba("PTm", [128, H * 128], BF16)
        PTt = sba("PTt", [2, H * 128], BF16)
        outT16 = sba("outT16", [128, D], BF16)
        sOut = sba("sOut", [128, D], BF16)
        wproj16 = sba("wproj16", [128, KT * D], BF16)

        gld_kt = []
        for kt in range(KT):
            sc.dma("gpsimd", wqkv16[:, kt * 3 * D:(kt + 1) * 3 * D],
                   w_qkv[kt * 128:(kt + 1) * 128, :], inc=s_gld)
            gld_kt.append(s_gld.n)
        ngld = [0]

        wpce = [sba("wpce0", [128, 1024], F32), sba("wpce1", [128, 1024], F32)]
        wq16s = [sba("w_q16%d" % i, [128, 1024], BF16) for i in range(4)]
        wstage = [sba("w_stage0", [128, 1024], BF16), sba("w_stage1", [128, 1024], BF16)]
        wtr = P4[:, 1024:2048]
        WL = [Sem(nc, "s_wl0"), Sem(nc, "s_wl1")]
        WLg = [Sem(nc, "s_wlg0"), Sem(nc, "s_wlg1")]
        wpieces = []
        for (srcw, dstw, nch, k) in ((wg_d, d_wgT, IKB, D), (wu_d, d_wuT, IKB, D),
                                     (wd_d, d_wdT, KT, I_FF)):
            for c in range(nch):
                for p0 in range(0, k, 1024):
                    wpieces.append((srcw, dstw, c, p0, min(p0 + 1024, k)))
        NWP = len(wpieces)
        wp_ld = {}
        wq_end = {}
        wt_end = {}
        wst_tot = {}
        wcur = [0, 0, 0]

        def w_load(pi):
            srcw, dstw, c, p0, p1 = wpieces[pi]
            if pi < 6:
                q, sem = "sync", WL[pi % 2]
            else:
                q, sem = "gpsimd", WLg[pi % 2]
            if pi - 2 in wq_end:
                sc.wait(q, s_v, wq_end[pi - 2])
            sc.dma(q, wpce[pi % 2][:, :p1 - p0],
                   srcw[c * 128:(c + 1) * 128, p0:p1], inc=sem)
            wp_ld[pi] = (sem, sem.n)

        def w_quant(pi):
            while wcur[0] < min(pi + 2, NWP):
                w_load(wcur[0])
                wcur[0] += 1
            srcw, dstw, c, p0, p1 = wpieces[pi]
            lsem, lcnt = wp_ld[pi]
            sc.wait("vector", lsem, lcnt)
            assert pi < 4 or (pi - 4) in wt_end
            quant(wpce[pi % 2][:, :p1 - p0], wq16s[pi % 4][:, :p1 - p0], p1 - p0,
                  pe_guard=wt_end.get(pi - 4, 0))
            wq_end[pi] = s_v.n

        def w_store(pi):
            srcw, dstw, c, p0, p1 = wpieces[pi]
            nb = (p1 - p0) // 128
            sc.wait("tensor", s_v, wq_end[pi])
            sc.wait("tensor", s_s)
            for b in range(nb):
                sc.op("tensor", lambda e, b=b, pi=pi: e.transpose(
                    wtr[:, b * 128:(b + 1) * 128],
                    wq16s[pi % 4][:, b * 128:(b + 1) * 128], ident[:]),
                    inc=(s_pe if b == nb - 1 else None))
            wt_end[pi] = s_pe.n
            sc.wait("scalar", s_pe)
            if pi - 2 in wst_tot:
                sc.wait("scalar", s_st, wst_tot[pi - 2])
            sc.op("scalar", lambda e, nb=nb, pi=pi: e.copy(
                wstage[pi % 2][:, :nb * 128], wtr[:, :nb * 128]), inc=s_s)
            sc.wait("scalar", s_s)
            out_ap = dstw[p0 // 128:p0 // 128 + nb, :, c * 128:(c + 1) * 128].transpose([1, 0, 2])
            in_ap = wstage[pi % 2][:, :nb * 128].rearrange("p (kb q) -> p kb q", q=128)
            sc.dma("scalar", out_ap, in_ap, inc=s_st)
            wst_tot[pi] = s_st.n

        wtick = [0]

        def emit_w_stores():
            while wcur[2] < wcur[1] - 2:
                w_store(wcur[2])
                wcur[2] += 1

        def emit_w_quants():
            wtick[0] += 1
            qtarget = min(wtick[0] * NWP // 48 + 1, NWP)
            while wcur[1] < qtarget:
                pi = wcur[1]
                while wcur[2] <= pi - 4:
                    w_store(wcur[2])
                    wcur[2] += 1
                w_quant(pi)
                wcur[1] += 1

        def flush_w():
            wtick[0] = 32
            emit_w_quants()
            while wcur[2] < NWP:
                w_store(wcur[2])
                wcur[2] += 1

        xl_idx = {}
        xl_cnt = [0]
        xn_guard = {}
        trx_guard = [0]
        sc_ev = [0, 0]
        qk_ev = [0, 0]
        xqst_guard = [0]
        sst_guard = [0]

        def xload(t, guard):
            if guard:
                sc.wait("gpsimd", s_v, guard)
            sc.dma("gpsimd", xtile[:, (t % 2) * D:(t % 2) * D + D],
                   xs[t * 128:(t + 1) * 128, :], inc=XL[t % 2])
            xl_idx[t] = XL[t % 2].n

        def ln_chain(xap, n_p, statsb, aggrb, tmpb):
            v, s = "vector", "scalar"
            sc.op(v, lambda e, xap=xap, statsb=statsb, n_p=n_p: e.bn_stats(
                statsb[:n_p, 0:6], xap[:, 0:512]))
            sc.op(v, lambda e, xap=xap, statsb=statsb, n_p=n_p: e.bn_stats(
                statsb[:n_p, 6:12], xap[:, 512:1024]), inc=s_v)
            sc.wait(v, s_v)
            sc.op(v, lambda e, statsb=statsb, aggrb=aggrb, n_p=n_p: e.bn_aggr(
                aggrb[:n_p, :], statsb[:n_p, :]), inc=s_v)
            sc.wait(v, s_v)
            sc.op(v, lambda e, aggrb=aggrb, tmpb=tmpb, n_p=n_p: e.tensor_scalar(
                tmpb[:n_p, 0:1], aggrb[:n_p, 0:1], -1.0, None, ALU.mult), inc=s_v)
            sc.wait(s, s_v)
            sc.op(s, lambda e, aggrb=aggrb, tmpb=tmpb, n_p=n_p: e.activation(
                tmpb[:n_p, 1:2], aggrb[:n_p, 1:2], AF.Ln, bias=cvec[:n_p, 0:1]), inc=s_s)
            sc.wait(s, s_s)
            sc.op(s, lambda e, tmpb=tmpb, n_p=n_p: e.activation(
                tmpb[:n_p, 2:3], tmpb[:n_p, 1:2], AF.Exp, scale=cvec[:n_p, 1:2]),
                inc=s_s)

        rstd_ev = {}

        def ln_stats(t):
            par = t % 2
            sc.wait("vector", XL[par], xl_idx[t])
            ln_chain(xtile[:, par * D:par * D + D], 128, stats[par], aggr[par], lntmp[par])
            rstd_ev[t] = s_s.n

        def ln_apply(t, xout, x16out, pe_guard=0):
            v, s = "vector", "scalar"
            par = t % 2
            xap = xtile[:, par * D:par * D + D]
            tmpb = lntmp[par]
            sc.wait(v, s_s, rstd_ev[t])
            sc.op(v, lambda e, xap=xap, xout=xout, tmpb=tmpb: e.tensor_scalar(
                xout, xap, tmpb[:, 0:1], tmpb[:, 2:3], ALU.add, ALU.mult),
                inc=s_v)
            sc.wait(s, s_v)
            if pe_guard:
                sc.wait(s, s_pe, pe_guard)
            sc.op(s, lambda e, xout=xout, x16out=x16out: e.copy(x16out, xout), inc=s_s)

        def ln_tile(xap, n_p, statsb, aggrb, tmpb, xout, x16out, pe_guard=0):
            v, s = "vector", "scalar"
            ln_chain(xap, n_p, statsb, aggrb, tmpb)
            sc.wait(v, s_s)
            sc.op(v, lambda e, xap=xap, xout=xout, tmpb=tmpb, n_p=n_p: e.tensor_scalar(
                xout, xap, tmpb[:n_p, 0:1], tmpb[:n_p, 2:3], ALU.add, ALU.mult),
                inc=s_v)
            sc.wait(s, s_v)
            if pe_guard:
                sc.wait(s, s_pe, pe_guard)
            sc.op(s, lambda e, xout=xout, x16out=x16out: e.copy(x16out, xout), inc=s_s)

        def stage1(st):
            par = st % 2
            if st == 1:
                for kt in range(KT):
                    sc.dma("gpsimd", wproj16[:, kt * D:(kt + 1) * D],
                           w_proj[kt * 128:(kt + 1) * 128, :], inc=s_gld)
                ngld[0] = s_gld.n
            if True:
                trA = P4[:, 0:1024]
                qkps = [P0, P1]
                vps = P2
                hps = P3[:, 0:16]
                sc.wait("tensor", s_s)
                sc.wait("tensor", s_v)

                for sub in range(2):
                    t = 4 * st + sub
                    xload(t, xn_guard.get(t - 2, 0))
                if st == NST - 1:
                    sc.dma("gpsimd", xhalo[:], xs[TOK:TOK + 2, :], inc=XH)

                for sub in range(4):
                    t = 4 * st + sub
                    emit_w_stores()
                    xap = xtile[:, (t % 2) * D:(t % 2) * D + D]
                    if t == 0:
                        ln_stats(0)
                    ln_apply(t, xnf[:], xn16[:], pe_guard=trx_guard[0])
                    if t + 1 < NT:
                        ln_stats(t + 1)
                    if dbg & 16:
                        sc.wait("sync", s_v)
                        sc.dma("sync", d_xn[t, :, :], xnf[:], inc=s_dbg)
                        sc.dma("sync", d_xt[t, :, :], xap, inc=s_dbg)
                        sc.wait("vector", s_dbg)
                    xn_guard[t] = s_v.n
                    if sub + 2 < 4 or st + 1 < NST:
                        xload(t + 2, xn_guard[t])
                    quant(xnf[:], xq16[:], D, pe_guard=trx_guard[0])
                    sc.wait("tensor", s_s)
                    sc.wait("tensor", s_v)
                    for kb in range(KT):
                        sc.op("tensor", lambda e, kb=kb, trA=trA: e.transpose(
                            trA[:, kb * 128:(kb + 1) * 128],
                            xn16[:, kb * 128:(kb + 1) * 128], ident[:]),
                            inc=(s_pe if kb == KT - 1 else None))
                    sc.wait("scalar", s_pe)
                    sc.op("scalar", lambda e, sub=sub, trA=trA: e.copy(
                        xnT[:].rearrange("p (kt q) -> p kt q", q=QW)[:, :, sub * 128:sub * 128 + 128],
                        trA[:].rearrange("p (kt q) -> p kt q", q=128)), inc=s_s)
                    sc.wait("tensor", s_s)
                    for kb in range(KT):
                        sc.op("tensor", lambda e, kb=kb, trA=trA: e.transpose(
                            trA[:, kb * 128:(kb + 1) * 128],
                            xq16[:, kb * 128:(kb + 1) * 128], ident[:]),
                            inc=(s_pe if kb == KT - 1 else None))
                    trx_guard[0] = s_pe.n
                    sc.wait("scalar", s_pe)
                    sc.wait("scalar", s_xqst, xqst_guard[0])
                    sc.op("scalar", lambda e, trA=trA: e.copy(xqTst[:], trA[:]), inc=s_s)
                    sc.wait("sync", s_s)
                    sc.dma("sync", d_xqT[t, :, :], xqTst[:], inc=s_xqst)
                    xqst_guard[0] = s_xqst.n
                    sc.wait("tensor", s_s)
                    emit_w_quants()

                if st == NST - 1:
                    sc.wait("vector", XH, 16)
                    ln_tile(xhalo[:], 2, hstat, haggr, htmp, xnh[:], xnh16[:],
                            pe_guard=trx_guard[0])
                    sc.wait("tensor", s_s)
                    for kb in range(KT):
                        sc.op("tensor", lambda e, kb=kb, trA=trA: e.transpose(
                            trA[:, kb * 2:(kb + 1) * 2],
                            xnh16[:, kb * 128:(kb + 1) * 128], ident[0:2, 0:2]),
                            inc=(s_pe if kb == KT - 1 else None))
                    sc.wait("scalar", s_pe)
                    sc.op("scalar", lambda e, trA=trA: e.copy(
                        xnT[:].rearrange("p (kt q) -> p kt q", q=QW)[:, :, 512:514],
                        trA[:, :2 * KT].rearrange("p (kt q) -> p kt q", q=2)), inc=s_s)
                    sc.wait("tensor", s_s)

                if alevel < 2:
                    return
                xnT3 = xnT[:].rearrange("p (kt q) -> p kt q", q=QW)
                for m in range(16):
                    q = m % 2
                    if qk_ev[q] > 0:
                        sc.wait("tensor", s_s, qk_ev[q])
                    for kt in range(KT):
                        if st == 0 and m == 0:
                            sc.wait("tensor", s_gld, gld_kt[kt])
                        sc.op("tensor", lambda e, kt=kt, m=m, q=q, qkps=qkps, xnT3=xnT3: e.matmul(
                            qkps[q][:, :],
                            wqkv16[:, kt * 3 * D + m * 128:kt * 3 * D + m * 128 + 128],
                            xnT3[:, kt, 0:512], start=(kt == 0), stop=(kt == KT - 1)),
                            inc=(s_pe if kt == KT - 1 else None))
                    sc.wait("scalar", s_pe)
                    sc.op("scalar", lambda e, m=m, q=q, par=par, qkps=qkps: e.copy(
                        qkT[par][:, m * QW:m * QW + 512], qkps[q][:, :]), inc=s_s)
                    qk_ev[q] = s_s.n
                    if m % 8 == 7:
                        emit_w_quants()
                if st == NST - 1:
                    for m in range(8, 16):
                        for kt in range(KT):
                            sc.op("tensor", lambda e, kt=kt, m=m, hps=hps, xnT3=xnT3: e.matmul(
                                hps[:, (m - 8) * 2:(m - 8) * 2 + 2],
                                wqkv16[:, kt * 3 * D + m * 128:kt * 3 * D + m * 128 + 128],
                                xnT3[:, kt, 512:514], start=(kt == 0), stop=(kt == KT - 1)),
                                inc=(s_pe if kt == KT - 1 and m == 15 else None))
                    sc.wait("scalar", s_pe)
                    sc.op("scalar", lambda e, par=par, hps=hps: e.copy(
                        qkT[par][:].rearrange("p (m q) -> p m q", q=QW)[:, 8:16, 512:514],
                        hps[:].rearrange("p (m q) -> p m q", q=2)), inc=s_s)

                v_ev = 0
                for sub in range(4):
                    if v_ev > 0:
                        sc.wait("tensor", s_s, v_ev)
                    for kt in range(KT):
                        for nch in range(2):
                            sc.op("tensor", lambda e, kt=kt, sub=sub, nch=nch, vps=vps, xnT3=xnT3: e.matmul(
                                vps[:, nch * 512:nch * 512 + 512],
                                xnT3[:, kt, sub * 128:sub * 128 + 128],
                                wqkv16[:, kt * 3 * D + 2 * D + nch * 512:kt * 3 * D + 2 * D + nch * 512 + 512],
                                start=(kt == 0), stop=(kt == KT - 1)),
                                inc=(s_pe if kt == KT - 1 and nch == 1 else None))
                    sc.wait("scalar", s_pe)
                    sc.op("scalar", lambda e, sub=sub, par=par, vps=vps: e.copy(
                        vsb[par][:, sub * D:sub * D + D], vps[:, :]), inc=s_s)
                    v_ev = s_s.n
                    if sub % 2 == 1:
                        emit_w_quants()
                if st == NST - 1:
                    sc.wait("tensor", s_s, v_ev)
                    for m in range(8):
                        for kt in range(KT):
                            sc.op("tensor", lambda e, kt=kt, m=m, hps=hps, xnT3=xnT3: e.matmul(
                                hps[:, m * 2:m * 2 + 2],
                                wqkv16[:, kt * 3 * D + 2 * D + m * 128:kt * 3 * D + 2 * D + m * 128 + 128],
                                xnT3[:, kt, 512:514], start=(kt == 0), stop=(kt == KT - 1)),
                                inc=(s_pe if kt == KT - 1 and m == 7 else None))
                    sc.wait("scalar", s_pe)
                    sc.op("scalar", lambda e, hps=hps: e.copy(vh16[:], hps[:]), inc=s_s)
                    sc.wait("tensor", s_s)
                    for m in range(8):
                        sc.op("tensor", lambda e, m=m, trA=trA: e.transpose(
                            trA[0:2, m * 128:(m + 1) * 128],
                            vh16[:, m * 2:m * 2 + 2], ident[:]),
                            inc=(s_pe if m == 7 else None))
                    sc.wait("scalar", s_pe)
                    sc.op("scalar", lambda e, trA=trA: e.copy(
                        vhalo[:], trA[0:2, 0:D]), inc=s_s)

        def attn(st):
            if alevel < 3:
                return
            par = st % 2
            qk3 = qkT[par][:].rearrange("p (m q) -> p m q", q=QW)
            qk3n = qkT[1 - par][:].rearrange("p (m q) -> p m q", q=QW)
            scps = [P0[:, 0:390], P1[:, 0:390]]
            sc.wait("tensor", s_s)
            sc.wait("tensor", s_v)
            qk3o = qkT[par][:].rearrange("p (m q) -> p m q", q=QW)
            qk3no = qkT[1 - par][:].rearrange("p (m q) -> p m q", q=QW)
            ep_guard = [0]
            pt_guard = [0]
            ptm_ev = {}
            sout_free = [0]
            sc.wait("sync", s_s)
            odd_guard = [0]

            def emit_tail(sub):
                t = 4 * st + sub
                otp = P2
                sc.wait("tensor", s_s, ptm_ev[sub])
                sc.wait("tensor", s_s, sout_free[0])
                for hh in range(H):
                    r0 = 64 * (hh % 2)
                    mco = (hh // 2) * 128
                    vmain = vsb[par][:, sub * D + hh * 64:sub * D + hh * 64 + 64]
                    if sub < 3:
                        vnext = vsb[par][0:2, (sub + 1) * D + hh * 64:(sub + 1) * D + hh * 64 + 64]
                    elif st < NST - 1:
                        vnext = vsb[1 - par][0:2, hh * 64:hh * 64 + 64]
                    else:
                        vnext = vhalo[0:2, hh * 64:hh * 64 + 64]
                    sc.op("tensor", lambda e, vmain=vmain, hh=hh, r0=r0, mco=mco, otp=otp: e.matmul(
                        otp[r0:r0 + 64, mco:mco + 128], vmain,
                        PTm[:, hh * 128:hh * 128 + 128], start=True, stop=False))
                    sc.op("tensor", lambda e, vnext=vnext, hh=hh, r0=r0, mco=mco, otp=otp: e.matmul(
                        otp[r0:r0 + 64, mco:mco + 128], vnext,
                        PTt[0:2, hh * 128:hh * 128 + 128], start=False, stop=True),
                        inc=(s_pe if hh == H - 1 else None))
                sc.wait("scalar", s_pe)
                sc.op("scalar", lambda e, otp=otp: e.copy(outT16[:], otp[:]), inc=s_s)
                sps = P2
                sc.wait("tensor", s_s)
                sc.wait("tensor", s_gld, ngld[0])
                for kt in range(KT):
                    for nch in range(2):
                        sc.op("tensor", lambda e, kt=kt, nch=nch, sps=sps: e.matmul(
                            sps[:, nch * 512:nch * 512 + 512],
                            outT16[:, kt * 128:kt * 128 + 128],
                            wproj16[:, kt * D + nch * 512:kt * D + nch * 512 + 512],
                            start=(kt == 0), stop=False))
                for nch in range(2):
                    sc.op("tensor", lambda e, nch=nch, sps=sps: e.matmul(
                        sps[:, nch * 512:nch * 512 + 512], onesr[:],
                        bproj16[:, nch * 512:nch * 512 + 512],
                        start=False, stop=True),
                        inc=(s_pe if nch == 1 else None))
                sc.wait("scalar", s_pe)
                sc.wait("scalar", s_sst, sst_guard[0])
                sc.op("scalar", lambda e, sps=sps: e.copy(sOut[:], sps[:]), inc=s_s)
                sout_free[0] = s_s.n
                sc.wait("sync", s_s)
                sc.dma("sync", d_s[t, :, :], sOut[:], inc=s_sst)
                sst_guard[0] = s_sst.n

            for sub in range(4):
                t = 4 * st + sub
                if odd_guard[0] > 0:
                    sc.wait("sync", s_pe, odd_guard[0])
                sc.dma("sync", qodd[:].rearrange("p (m q) -> p m q", q=128),
                       qk3o[64:128, 0:8, sub * 128:sub * 128 + 128], inc=SH)
                kodd3 = kodd[:].rearrange("p (m q) -> p m q", q=132)
                if sub < 3 or st == NST - 1:
                    sc.dma("sync", kodd3[:, :, 0:130],
                           qk3o[64:128, 8:16, sub * 128:sub * 128 + 130], inc=SH)
                else:
                    sc.dma("sync", kodd3[:, :, 0:128],
                           qk3o[64:128, 8:16, sub * 128:sub * 128 + 128], inc=SH)
                    sc.dma("sync", kodd3[:, :, 128:130],
                           qk3no[64:128, 8:16, 0:2], inc=SH)
                sh_n = SH.n
                groups = [(0, 3), (3, 6), (6, 9), (9, 12), (12, 14), (14, 16)]
                sc.wait("tensor", SH, sh_n)
                for gi, (h0, h1) in enumerate(groups):
                    g = gi % 2
                    if sc_ev[g] > 0:
                        sc.wait("tensor", s_s, sc_ev[g])
                    for hh in range(h0, h1):
                        odd = hh % 2
                        mq = hh // 2
                        co = (hh - h0) * 130
                        if odd:
                            lhs = qodd[:].rearrange("p (m q) -> p m q", q=128)[:, mq, :]
                            rhs = kodd[:].rearrange("p (m q) -> p m q", q=132)[:, mq, 0:130]
                            sc.op("tensor", lambda e, lhs=lhs, rhs=rhs, g=g, co=co, scps=scps: e.matmul(
                                scps[g][:, co:co + 130], lhs, rhs, start=True, stop=True),
                                inc=(s_pe if hh == h1 - 1 else None))
                        else:
                            mk = 8 + hh // 2
                            lhs = qk3[0:64, mq, sub * 128:sub * 128 + 128]
                            if sub < 3 or st == NST - 1:
                                rhs = qk3[0:64, mk, sub * 128:sub * 128 + 130]
                                sc.op("tensor", lambda e, lhs=lhs, rhs=rhs, g=g, co=co, scps=scps: e.matmul(
                                    scps[g][:, co:co + 130], lhs, rhs, start=True, stop=True),
                                    inc=(s_pe if hh == h1 - 1 else None))
                            else:
                                rhs = qk3[0:64, mk, sub * 128:sub * 128 + 128]
                                rhs2 = qk3n[0:64, mk, 0:2]
                                sc.op("tensor", lambda e, lhs=lhs, rhs=rhs, g=g, co=co, scps=scps: e.matmul(
                                    scps[g][:, co:co + 128], lhs, rhs, start=True, stop=True))
                                sc.op("tensor", lambda e, lhs=lhs, rhs2=rhs2, g=g, co=co, scps=scps: e.matmul(
                                    scps[g][:, co + 128:co + 130], lhs, rhs2, start=True, stop=True),
                                    inc=(s_pe if hh == h1 - 1 else None))
                    sc.wait("scalar", s_pe)
                    if gi == 0:
                        sc.wait("scalar", s_v, ep_guard[0])
                        sc.wait("scalar", s_pe, pt_guard[0])
                    sc.op("scalar", lambda e, h0=h0, h1=h1, g=g, scps=scps: e.activation(
                        e16[:, h0 * 130:h1 * 130], scps[g][:, :(h1 - h0) * 130],
                        AF.Exp, scale=cvec[:, 2:3]), inc=s_s)
                    sc_ev[g] = s_s.n
                exp_ev = s_s.n
                odd_guard[0] = s_pe.n
                emit_w_stores()
                if alevel < 4:
                    continue
                if sub > 0 and alevel >= 6:
                    emit_tail(sub - 1)
                mask = maskl if t == NT - 1 else maskm
                e3 = e16[:].rearrange("p (h w) -> p h w", w=130)
                eb3 = e3
                p3 = e3
                mb = mask[:].unsqueeze(1).broadcast_to([128, H, 130])
                sc.wait("vector", s_s, exp_ev)
                sc.wait("vector", s_pe, pt_guard[0])
                sc.op("vector", lambda e, e3=e3, eb3=eb3, mb=mb: e.tensor_tensor(
                    eb3, e3, mb, ALU.mult))
                sc.op("vector", lambda e, eb3=eb3: e.tensor_reduce(
                    rs32[:], eb3, AX.X, ALU.add), inc=s_v)
                sc.wait("vector", s_v)
                sc.op("vector", lambda e: e.reciprocal(rsi32[:], rs32[:]), inc=s_v)
                sc.wait("vector", s_v)
                sc.op("vector", lambda e: e.tensor_copy(rinv16[:], rsi32[:]), inc=s_v)
                sc.wait("vector", s_v)
                rb = rinv16[:].unsqueeze(2).broadcast_to([128, H, 130])
                sc.op("vector", lambda e, p3=p3, eb3=eb3, rb=rb: e.tensor_tensor(
                    p3, eb3, rb, ALU.mult), inc=s_v)
                ep_guard[0] = s_v.n
                if dbg & 16:
                    sc.wait("sync", s_v)
                    sc.dma("sync", d_e[t, :, :], e16[:], inc=s_dbg)
                    sc.dma("sync", d_p[t, :, :], e16[:], inc=s_dbg)
                    sc.dma("sync", d_rs[t, :, :], rs32[:], inc=s_dbg)
                    sc.dma("sync", d_eb[t, :, :], e16[:], inc=s_dbg)
                    sc.dma("sync", d_ri[t, :, :], rinv16[:], inc=s_dbg)
                    sc.wait("scalar", s_dbg)
                if alevel < 5:
                    continue
                if True:
                    ptm = P4; ptt = P3b
                    sc.wait("tensor", s_v)
                    sc.wait("tensor", s_s)
                    for hh in range(H):
                        sc.op("tensor", lambda e, hh=hh, ptm=ptm: e.transpose(
                            ptm[:, hh * 128:hh * 128 + 128],
                            e16[:, hh * 130:hh * 130 + 128], ident[:]))
                    for hh in range(H):
                        sc.op("tensor", lambda e, hh=hh, ptt=ptt: e.transpose(
                            ptt[0:2, hh * 128:hh * 128 + 128],
                            e16[:, hh * 130 + 128:hh * 130 + 130], ident[:]),
                            inc=(s_pe if hh == H - 1 else None))
                    pt_guard[0] = s_pe.n
                    sc.wait("scalar", s_pe)
                    sc.op("scalar", lambda e, ptm=ptm: e.copy(PTm[:], ptm[:]))
                    sc.op("scalar", lambda e, ptt=ptt: e.copy(PTt[:], ptt[0:2, :]), inc=s_s)
                    ptm_ev[sub] = s_s.n
                emit_w_quants()
            if alevel >= 6:
                emit_tail(3)

        relaxB = {}
        for st in range(NST):
            stage1(st)
            if st > 0:
                attn(st - 1)
            if st == NST - 1:
                while wcur[2] < 44:
                    w_store(wcur[2])
                    wcur[2] += 1
                relaxB.update(pe=s_pe.n, v=s_v.n, xq=s_xqst.n,
                              st=wst_tot[43])
        attn(NST - 1)
        flush_w()
        n_w_store = s_st.n

    nA_st = s_st.n
    nA_xq = s_xqst.n
    nA_ss = s_sst.n
    nA_v = s_v.n
    nA_s = s_s.n
    nA_pe = s_pe.n
    if phases == "WA":
        sc.wait("sync", s_st, nA_st)
        sc.wait("sync", s_xqst, nA_xq)
        sc.wait("sync", s_sst, nA_ss)
        sc.wait("sync", s_v, nA_v)
        sc.wait("sync", s_s, nA_s)
        sc.wait("sync", s_pe, nA_pe)
        sc.run()
        return nc

    with contextlib.ExitStack() as _esB:
        def sbb(name, shape, dt):
            return _esB.enter_context(nc.sbuf_tensor(name, shape, dt)).ap()
        wgT = sbb("wgT", [128, KT * I_FF], BF16)
        wuT = sbb("wuT", [128, KT * I_FF], BF16)
        wdT = sbb("wdT", [128, IKB * D], BF16)
        xqTt = sbb("xqTt", [128, 2 * D], BF16)
        sT = sbb("sT", [128, 2 * D], BF16)
        xT = sbb("xT", [128, 2 * D], F32)
        sgb = sbb("sgb", [128, 1024], F32)
        hb = sbb("hb", [128, 1024], F32)
        hq16 = sbb("hq16", [128, I_FF], BF16)
        hqT = sbb("hqT", [128, IKB * 128], BF16)
        gb = sbb("gb", [128, D], F32)
        tb = sbb("tbb", [128, D], F32)
        ob = sbb("ob", [128, 2 * D], F32)

        s_wgu = Sem(nc, "s_wgu")
        sc.wait("gpsimd", s_pe, relaxB["pe"])
        sc.wait("gpsimd", s_v, relaxB["v"])
        sc.wait("gpsimd", s_xqst, relaxB["xq"])
        sc.wait("gpsimd", s_st, relaxB["st"])
        for kt in range(KT):
            sc.dma("gpsimd", wgT[:, kt * I_FF:(kt + 1) * I_FF], d_wgT[kt, :, :], inc=s_wgu)
            sc.dma("gpsimd", wuT[:, kt * I_FF:(kt + 1) * I_FF], d_wuT[kt, :, :], inc=s_wgu)
        n_wgu = s_wgu.n

        for qq in ("sync", "scalar", "gpsimd"):
            sc.wait(qq, s_st, nA_st)
            sc.wait(qq, s_xqst, nA_xq)
            sc.wait(qq, s_sst, nA_ss)
            sc.wait(qq, s_v, nA_v)
            sc.wait(qq, s_s, nA_s)
            sc.wait(qq, s_pe, nA_pe)

        BL = [Sem(nc, "s_bl0"), Sem(nc, "s_bl1")]
        bl_idx = {}
        pe_tile_end = {}
        v_tile_end = {}
        hq_guard = {}

        def bload(t):
            par = t % 2
            if t - 2 in pe_tile_end:
                sc.wait("gpsimd", s_pe, pe_tile_end[t - 2])
                sc.wait("gpsimd", s_v, v_tile_end[t - 2])
            sc.dma("gpsimd", xqTt[:, par * D:par * D + D], d_xqT[t, :, :], inc=BL[par])
            sc.dma("gpsimd", sT[:, par * D:par * D + D], d_s[t, :, :], inc=BL[par])
            sc.dma("gpsimd", xT[:, par * D:par * D + D],
                   xs[t * 128:(t + 1) * 128, :], inc=BL[par])
            bl_idx[t] = BL[par].n

        s_wd = [Sem(nc, "s_wd%d" % i) for i in range(3)]
        bload(0)
        bload(1)
        ld_wd = []
        for g in range(3):
            for kt in range(8 * g, min(IKB, 8 * g + 8)):
                sc.dma("gpsimd", wdT[:, kt * D:(kt + 1) * D], d_wdT[kt, :, :], inc=s_wd[g])
            ld_wd.append(s_wd[g].n)

        if True:
            gpsA = [P0, P3[:, 0:512]]
            upsA = [P1, P3[:, 512:1024]]
            dps = P2
            trB = P4[:, 0:1024]

            h_ev = {}
            sg_ev = {}
            q_ev = {}
            hqT_ev = {}
            hqT_ev_roll = {}
            gb_ev = {}
            pending = {}

            def tr_pair(t, pr, p0, pw):
                sc.wait("tensor", s_v, q_ev[(t, pr)])
                nblk = pw // 128
                gpr = t * 3 + pr
                if gpr >= 1:
                    sc.wait("tensor", s_s, hqT_ev_roll[gpr - 1])
                for b in range(nblk):
                    kb = p0 // 128 + b
                    sc.op("tensor", lambda e, b=b, kb=kb: e.transpose(
                        trB[:, b * 128:(b + 1) * 128],
                        hq16[:, kb * 128:(kb + 1) * 128], ident[:]),
                        inc=(s_pe if b == nblk - 1 else None))
                hq_guard[pr] = s_pe.n
                sc.wait("scalar", s_pe, hq_guard[pr])
                sc.op("scalar", lambda e, p0=p0, nblk=nblk: e.copy(
                    hqT[:, p0:p0 + nblk * 128], trB[:, :nblk * 128]), inc=s_s)
                hqT_ev[(t, pr)] = s_s.n
                hqT_ev_roll[gpr] = s_s.n

            def emit_down_pair(t, pr):
                c0 = IC_CH[pr * 2][0]
                c1 = IC_CH[pr * 2 + 1][1]
                if t <= 1:
                    g = min(pr, 2)
                    sc.wait("tensor", s_wd[g], s_wd[g].n)
                if pr == 0 and (t - 1) in gb_ev:
                    sc.wait("tensor", s_s, gb_ev[t - 1])
                sc.wait("tensor", s_s, hqT_ev[(t, pr)])
                for kt in range(c0 // 128, c1 // 128):
                    for nch in range(2):
                        sc.op("tensor", lambda e, kt=kt, nch=nch: e.matmul(
                            dps[:, nch * 512:nch * 512 + 512],
                            hqT[:, kt * 128:kt * 128 + 128],
                            wdT[:, kt * D + nch * 512:kt * D + nch * 512 + 512],
                            start=(kt == 0), stop=(kt == IKB - 1)),
                            inc=(s_pe if kt == c1 // 128 - 1 and nch == 1 else None))

            def tile_epilogue(t):
                par = t % 2
                s_t = sT[:, par * D:par * D + D]
                x_t = xT[:, par * D:par * D + D]
                pe_tile_end[t] = s_pe.n
                sc.wait("scalar", s_pe)
                sc.wait("scalar", s_v, v_tile_end.get(t - 1, 0))
                sc.op("scalar", lambda e: e.activation(gb[:], dps[:], AF.Silu), inc=s_s)
                gb_ev[t] = s_s.n
                sc.wait("vector", s_s, gb_ev[t])
                if t >= 2:
                    sc.wait("vector", OUT[par], OUT[par].n)
                sc.op("vector", lambda e, s_t=s_t: e.tensor_tensor(
                    tb[:], s_t, gb[:], ALU.mult))
                sc.op("vector", lambda e, x_t=x_t, par=par: e.tensor_tensor(
                    ob[:, par * D:par * D + D], x_t, tb[:], ALU.add), inc=s_v)
                v_tile_end[t] = s_v.n
                sc.wait("sync", s_v)
                sc.dma("sync", out_d[t * 128:(t + 1) * 128, :],
                       ob[:, par * D:par * D + D], inc=OUT[par])
                if t + 2 < NT:
                    bload(t + 2)

            NG = NT * 6
            for gci in range(NG):
                t, ci = divmod(gci, 6)
                c0, c1 = IC_CH[ci]
                cw = c1 - c0
                par = t % 2
                xq_t = xqTt[:, par * D:par * D + D]
                gps = gpsA[ci % 2]
                ups = upsA[ci % 2]
                if ci == 0:
                    sc.wait("tensor", BL[par], bl_idx[t])
                if gci >= 2:
                    sc.wait("tensor", s_s, sg_ev[gci - 2])
                    sc.wait("tensor", s_v, h_ev[gci - 2])
                if t <= 1 and ci == 0:
                    sc.wait("tensor", s_wgu, n_wgu)
                for kt in range(KT):
                    sc.op("tensor", lambda e, kt=kt, c0=c0, cw=cw, xq_t=xq_t, gps=gps: e.matmul(
                        gps[:, :cw], xq_t[:, kt * 128:kt * 128 + 128],
                        wgT[:, kt * I_FF + c0:kt * I_FF + c0 + cw],
                        start=(kt == 0), stop=(kt == KT - 1)))
                for kt in range(KT):
                    sc.op("tensor", lambda e, kt=kt, c0=c0, cw=cw, xq_t=xq_t, ups=ups: e.matmul(
                        ups[:, :cw], xq_t[:, kt * 128:kt * 128 + 128],
                        wuT[:, kt * I_FF + c0:kt * I_FF + c0 + cw],
                        start=(kt == 0), stop=(kt == KT - 1)),
                        inc=(s_pe if kt == KT - 1 else None))
                gu_end = s_pe.n
                for fn in pending.pop(gci, []):
                    fn()
                sc.wait("scalar", s_pe, gu_end)
                sc.wait("scalar", s_v, h_ev.get(gci - 2, 0))
                sc.op("scalar", lambda e, cw=cw, gps=gps, ci=ci: e.activation(
                    sgb[:, (ci % 2) * 512:(ci % 2) * 512 + cw], gps[:, :cw], AF.Silu), inc=s_s)
                sg_ev[gci] = s_s.n
                sc.wait("vector", s_s, sg_ev[gci])
                sc.op("vector", lambda e, cw=cw, ups=ups, ci=ci: e.tensor_tensor(
                    hb[:, (ci % 2) * 512:(ci % 2) * 512 + cw],
                    sgb[:, (ci % 2) * 512:(ci % 2) * 512 + cw],
                    ups[:, :cw], ALU.mult), inc=s_v)
                h_ev[gci] = s_v.n
                if ci % 2 == 1:
                    pr = ci // 2
                    p0 = IC_CH[ci - 1][0]
                    pw = c1 - p0
                    quant(hb[:, :pw], hq16[:, p0:c1], pw, pe_guard=hq_guard.get(pr, 0))
                    q_ev[(t, pr)] = s_v.n

                    pending.setdefault(gci + 2, []).append(
                        lambda t=t, pr=pr, p0=p0, pw=pw: tr_pair(t, pr, p0, pw))

                    def deferred(t=t, pr=pr):
                        emit_down_pair(t, pr)
                        if pr == 2:
                            tile_epilogue(t)
                    pending.setdefault(gci + 3, []).append(deferred)
            for gci in range(NG, NG + 4):
                for fn in pending.pop(gci, []):
                    fn()

    sc.wait("sync", OUT[0], OUT[0].n)
    sc.wait("sync", OUT[1], OUT[1].n)
    sc.run()
    return nc


_prog_cache = {}


def kernel(L0_attn_x, L0_attn_w_qkv, L0_attn_w_proj, L0_attn_b_proj,
           L0_ffn_w_gate, L0_ffn_w_up, L0_ffn_w_down):
    x = np.ascontiguousarray(np.asarray(L0_attn_x, dtype=np.float32))
    B, T, _ = x.shape
    n_cores = 8
    per_b = n_cores // B

    if "nc" not in _prog_cache:
        _prog_cache["nc"] = build_program()
    nc = _prog_cache["nc"]

    i_idx = np.arange(128)[:, None]
    j_idx = np.arange(130)[None, :]
    band = (j_idx >= i_idx) & (j_idx <= i_idx + 2)
    maskmain = band.astype(ml_dtypes.bfloat16)
    masklast_edge = (band & (j_idx < 128)).astype(ml_dtypes.bfloat16)
    ident = np.eye(128, dtype=ml_dtypes.bfloat16)
    onesrow = np.ones((1, 128), dtype=ml_dtypes.bfloat16)
    cvec = np.tile(np.array([[1e-5, -0.5, 0.125, -1.0]], dtype=np.float32), (128, 1))

    wqkv = np.ascontiguousarray(np.asarray(L0_attn_w_qkv, dtype=np.float32))
    wproj = np.ascontiguousarray(np.asarray(L0_attn_w_proj, dtype=np.float32))
    bproj = np.ascontiguousarray(np.asarray(L0_attn_b_proj, dtype=np.float32)).reshape(1, -1)
    wg = np.ascontiguousarray(np.asarray(L0_ffn_w_gate, dtype=np.float32))
    wu = np.ascontiguousarray(np.asarray(L0_ffn_w_up, dtype=np.float32))
    wd = np.ascontiguousarray(np.asarray(L0_ffn_w_down, dtype=np.float32))

    in_maps = []
    for c in range(n_cores):
        b = c // per_b
        r0 = (c % per_b) * TOK
        xsc = np.zeros((TOK + 2, D), dtype=np.float32)
        end = min(r0 + TOK + 2, T)
        xsc[:end - r0] = x[b, r0:end]
        last = (c % per_b) == per_b - 1
        in_maps.append(dict(
            xs=xsc, w_qkv=wqkv, w_proj=wproj, b_proj=bproj,
            wg=wg, wu=wu, wd=wd, ident=ident,
            maskmain=maskmain, masklast=(masklast_edge if last else maskmain),
            onesrow=onesrow, cvec=cvec,
        ))

    res = run_bass_kernel_spmd(nc, in_maps, core_ids=list(range(n_cores)))
    out = np.empty((B, T, D), dtype=np.float32)
    for c in range(n_cores):
        b = c // per_b
        r0 = (c % per_b) * TOK
        out[b, r0:r0 + TOK] = np.asarray(res.results[c]["out"], dtype=np.float32)
    return out
